# version 03
# baseline (speedup 1.0000x reference)
import contextlib
import numpy as np
import concourse.bass as bass
import concourse.mybir as mybir
from concourse.bass_utils import run_bass_kernel_spmd

F32 = mybir.dt.float32
BF16 = mybir.dt.bfloat16
I32 = mybir.dt.int32
AF = mybir.ActivationFunctionType
ALU = mybir.AluOpType
AX = mybir.AxisListType

NCORE = 8
D = 1024
H = 16
DH = 64
FF = 2048
NUNIT = 9
TOK = 4096
G = 1024
NGRP = TOK // G
NT = G // 128
NTT = G // 512
C = 128
NCH = TOK // C
EPS = 1e-5

ENGINES = ("tensor", "vector", "scalar", "gpsimd", "sync")


class Sched:

    def __init__(self):
        self.items = {e: [] for e in ENGINES}
        self.count = {}

    def add_sem(self, name):
        self.count[name] = 0

    def emit(self, eng, fn, waits=(), incs=(), self_inc=False):
        merged = {}
        for s, v in waits:
            assert v <= self.count[s], (s, v, self.count[s])
            merged[s] = max(merged.get(s, 0), v)
        wl = tuple((s, v) for s, v in merged.items() if v > 0)
        resolved = []
        for s, d in incs:
            self.count[s] += d
            resolved.append((s, d))
        self.items[eng].append((fn, wl, () if self_inc else tuple(resolved)))

    def cur(self, name):
        return self.count[name]

    def run(self, eng_name, eng, sems):
        for fn, waits, incs in self.items[eng_name]:
            for s, v in waits:
                eng.wait_ge(sems[s], v)
            inst = fn(eng)
            for s, d in incs:
                if inst is None:
                    inst = eng.nop()
                inst.then_inc(sems[s], d)
                inst = None


def build_nc():
    nc = bass.Bass("TRN2", target_bir_lowering=False)

    x_in = nc.dram_tensor("x_in", [TOK, D], F32, kind="ExternalInput").ap()
    w1cat = nc.dram_tensor("w1cat", [NUNIT, D, FF], F32, kind="ExternalInput").ap()
    w3cat = nc.dram_tensor("w3cat", [NUNIT, D, FF], F32, kind="ExternalInput").ap()
    w2cat = nc.dram_tensor("w2cat", [NUNIT, FF, D], F32, kind="ExternalInput").ap()
    wg_e = nc.dram_tensor("wg_e", [D, 8], F32, kind="ExternalInput").ap()
    wqT = nc.dram_tensor("wqT", [D, D], F32, kind="ExternalInput").ap()
    wkT = nc.dram_tensor("wkT", [D, D], F32, kind="ExternalInput").ap()
    wvT = nc.dram_tensor("wvT", [D, D], F32, kind="ExternalInput").ap()
    wgT = nc.dram_tensor("wgT", [D, D], F32, kind="ExternalInput").ap()
    woT = nc.dram_tensor("woT", [D, D], F32, kind="ExternalInput").ap()
    ident_in = nc.dram_tensor("ident_in", [128, 128], F32, kind="ExternalInput").ap()
    maskt_in = nc.dram_tensor("maskt_in", [128, 128], F32, kind="ExternalInput").ap()
    gq_in = nc.dram_tensor("gq_in", [128, D], F32, kind="ExternalInput").ap()
    gk_in = nc.dram_tensor("gk_in", [128, D], F32, kind="ExternalInput").ap()
    gc_in = nc.dram_tensor("gc_in", [64, D], F32, kind="ExternalInput").ap()
    cos_in = nc.dram_tensor("cos_in", [128, NCH * 32], F32, kind="ExternalInput").ap()
    sin_in = nc.dram_tensor("sin_in", [128, NCH * 32], F32, kind="ExternalInput").ap()
    cvec_in = nc.dram_tensor("cvec_in", [128, 8], F32, kind="ExternalInput").ap()
    magic_in = nc.dram_tensor("magic_in", [1, 16], I32, kind="ExternalInput").ap()
    rmask_in = nc.dram_tensor("rmask_in", [64, 1], F32, kind="ExternalInput").ap()

    out_e = nc.dram_tensor("out_e", [TOK, D], F32, kind="ExternalOutput").ap()

    x_work = nc.dram_tensor("x_work", [TOK, D], F32)
    kt_sc = nc.dram_tensor("kt_sc", [NCH * 128, D], BF16)
    v_sc = nc.dram_tensor("v_sc", [NCH * 128, D], BF16)
    sg_sc = nc.dram_tensor("sg_sc", [NCH * 64, D], F32)
    xch = nc.dram_tensor("xch", [2 * 64, D], F32, addr_space="Shared")
    xchf = nc.dram_tensor("xchf", [2, 16], I32, addr_space="Shared")

    SEM_NAMES = ["d_sy", "d_sc", "d_gp", "e_pe", "e_dv", "e_ac", "e_gp", "pl", "bar"]

    es = contextlib.ExitStack()
    with es:
        E = es.enter_context

        sems = {n: E(nc.semaphore(n)) for n in SEM_NAMES}

        ident = E(nc.sbuf_tensor("ident", [128, 128], BF16))
        maskt = E(nc.sbuf_tensor("maskt", [128, 128], F32))
        cvec = E(nc.sbuf_tensor("cvec", [128, 8], F32))
        wg_sb = E(nc.sbuf_tensor("wg_sb", [128, 8 * 8], BF16))
        s_magic = E(nc.sbuf_tensor("s_magic", [1, 16], I32))
        s_flag = E(nc.sbuf_tensor("s_flag", [1, 16], I32))
        rmask = E(nc.sbuf_tensor("rmask", [64, 1], F32))

        sch = Sched()
        for n in SEM_NAMES:
            sch.add_sem(n)

        CV_EPS, CV_MHALF = 0, 1

        slot_free_pe = [0, 0, 0]
        loaded_dgp = {}
        barrier_level = [0]

        def phase_barrier():
            lv = barrier_level[0]
            need_sy = sch.cur("d_sy")
            need_sc = sch.cur("d_sc")
            need_gp = sch.cur("d_gp")
            sch.emit("sync", lambda e: e.nop(),
                     waits=[("d_sy", need_sy)], incs=[("bar", 1)])
            sch.emit("scalar", lambda e: e.nop(),
                     waits=[("d_sc", need_sc)], incs=[("bar", 1)])
            sch.emit("gpsimd", lambda e: e.nop(),
                     waits=[("d_gp", need_gp)], incs=[("bar", 1)])
            sch.emit("tensor", lambda e: e.nop(), incs=[("bar", 1)])
            sch.emit("vector", lambda e: e.nop(), incs=[("bar", 1)])
            barrier_level[0] += 5
            tgt = barrier_level[0]
            for en in ENGINES:
                sch.emit(en, lambda e: None, waits=[("bar", tgt)])

        def init_loads():
            def f(e):
                e.dma_start(out=ident[:], in_=ident_in).then_inc(sems["d_gp"], 16)
                e.dma_start(out=wg_sb.ap().rearrange("p (kc e2) -> p kc e2", kc=8),
                            in_=wg_e.rearrange("(kc p) e2 -> p kc e2", p=128)).then_inc(sems["d_gp"], 16)
                e.dma_start(out=s_magic[:], in_=magic_in).then_inc(sems["d_gp"], 16)
            sch.emit("gpsimd", f, incs=[("d_gp", 48)], self_inc=True)
            def f2(e):
                e.dma_start(out=maskt[:], in_=maskt_in).then_inc(sems["d_sy"], 16)
                e.dma_start(out=cvec[:], in_=cvec_in).then_inc(sems["d_sy"], 16)
                e.dma_start(out=rmask[:], in_=rmask_in).then_inc(sems["d_sy"], 16)
            sch.emit("sync", f2, incs=[("d_sy", 48)], self_inc=True)

        def ffn_phase(src, dst, phase_tag):
            fs = contextlib.ExitStack()
            with fs:
                FE = fs.enter_context
                wslot = [FE(nc.sbuf_tensor(f"wslot{i}_{phase_tag}", [128, 16384], BF16))
                         for i in range(3)]
                x_g = FE(nc.sbuf_tensor(f"x_g_{phase_tag}", [128, NT * D], F32))
                xn_t = [FE(nc.sbuf_tensor(f"xn{i}_{phase_tag}", [128, D], BF16)) for i in range(2)]
                xnT = FE(nc.sbuf_tensor(f"xnT_{phase_tag}", [128, NT * D], BF16))
                hT = FE(nc.sbuf_tensor(f"hT_{phase_tag}", [128, 16 * 512], BF16))
                tsil = [FE(nc.sbuf_tensor(f"tsil{i}_{phase_tag}", [128, 512], F32)) for i in range(2)]
                dump = FE(nc.sbuf_tensor(f"dump_{phase_tag}", [128, D], F32))
                col = {n: FE(nc.sbuf_tensor(f"c_{n}_{phase_tag}", [128, 1], F32))
                       for n in ["sum", "sumsq", "m", "ex2", "m2c", "var", "rstd", "mb",
                                 "gm1", "gm2", "den", "rden"]}
                gates = FE(nc.sbuf_tensor(f"gates_{phase_tag}", [128, 8], F32))
                gtmp = FE(nc.sbuf_tensor(f"gtmp_{phase_tag}", [128, 8], F32))
                gtmp2 = FE(nc.sbuf_tensor(f"gtmp2_{phase_tag}", [128, 8], F32))
                wsc = FE(nc.sbuf_tensor(f"wsc_{phase_tag}", [128, NT * 8], F32))
                p_tp = FE(nc.psum_tensor(f"ptp_{phase_tag}", [128, 128], BF16))
                p_gt = FE(nc.psum_tensor(f"pgt_{phase_tag}", [128, 8], F32))
                p_h1 = [FE(nc.psum_tensor(f"ph1{i}_{phase_tag}", [128, 512], F32)) for i in range(2)]
                p_h3 = [FE(nc.psum_tensor(f"ph3{i}_{phase_tag}", [128, 512], F32)) for i in range(2)]
                p_y = [FE(nc.psum_tensor(f"py{i}_{phase_tag}", [128, 512], F32)) for i in range(2)]

                load_idx = [0]

                def sched_slot_load(g, u, s):
                    l = load_idx[0]
                    load_idx[0] += 1
                    slot = l % 3
                    pe_need = slot_free_pe[slot]
                    if s < 2:
                        def f(e, u=u, s=s, slot=slot):
                            e.dma_start(
                                out=wslot[slot].ap().rearrange("p (kc f) -> p kc f", kc=16)[:, 0:8, :],
                                in_=w1cat[u].rearrange("(kc p) f -> p kc f", p=128)[:, :, s * 1024:(s + 1) * 1024]).then_inc(sems["d_gp"], 16)
                            e.dma_start(
                                out=wslot[slot].ap().rearrange("p (kc f) -> p kc f", kc=16)[:, 8:16, :],
                                in_=w3cat[u].rearrange("(kc p) f -> p kc f", p=128)[:, :, s * 1024:(s + 1) * 1024]).then_inc(sems["d_gp"], 16)
                        sch.emit("gpsimd", f, waits=[("e_pe", pe_need)], incs=[("d_gp", 32)], self_inc=True)
                    else:
                        def f(e, u=u, slot=slot):
                            return e.dma_start(
                                out=wslot[slot].ap().rearrange("p (kc d2) -> p kc d2", kc=16),
                                in_=w2cat[u].rearrange("(kc p) d2 -> p kc d2", p=128))
                        sch.emit("gpsimd", f, waits=[("e_pe", pe_need)], incs=[("d_gp", 32)])
                    loaded_dgp[(phase_tag, l)] = sch.cur("d_gp")
                    return l

                for g in range(NGRP):
                    def fload(e, g=g):
                        return e.dma_start(
                            out=x_g.ap().rearrange("p (t d2) -> p t d2", t=NT),
                            in_=src.rearrange("(t p) d2 -> p t d2", p=128)[:, g * NT:(g + 1) * NT, :])
                    sch.emit("sync", fload, waits=[("d_sy", sch.cur("d_sy")), ("e_dv", sch.cur("e_dv"))],
                             incs=[("d_sy", 16)])
                    xg_loaded = sch.cur("d_sy")

                    lmap = {}
                    for s0 in range(3):
                        lmap[(0, s0)] = sched_slot_load(g, 0, s0)

                    sq_free = [0]
                    tp_evac_cnt = [sch.cur("e_ac")]
                    for t in range(NT):
                        xt = x_g[:, t * D:(t + 1) * D]
                        def f_sum(e, xt=xt):
                            return e.reduce_sum(col["sum"][:], xt, axis=AX.X)
                        sch.emit("vector", f_sum, waits=[("d_sy", xg_loaded)], incs=[("e_dv", 1)])
                        def f_sq(e, xt=xt):
                            return e.square(dump[:], xt)
                        sch.emit("scalar", f_sq, waits=[("d_sy", xg_loaded),
                                                        ("e_dv", sq_free[0])], incs=[("e_ac", 1)])
                        sq_ac = sch.cur("e_ac")
                        def f_ssq(e):
                            return e.reduce_sum(col["sumsq"][:], dump[:], axis=AX.X)
                        sch.emit("vector", f_ssq, waits=[("e_ac", sq_ac)], incs=[("e_dv", 1)])
                        sq_done = sch.cur("e_dv")
                        sq_free[0] = sq_done
                        def f_m(e):
                            return e.tensor_scalar_mul(col["m"][:], col["sum"][:], 1.0 / D)
                        sch.emit("vector", f_m, waits=[("e_dv", sq_done)], incs=[("e_dv", 1)])
                        def f_ex2(e):
                            return e.tensor_scalar_mul(col["ex2"][:], col["sumsq"][:], 1.0 / D)
                        sch.emit("vector", f_ex2, waits=[("e_dv", sch.cur("e_dv"))], incs=[("e_dv", 1)])
                        def f_m2(e):
                            return e.tensor_scalar_mul(col["m2c"][:], col["m"][:], col["m"][:])
                        sch.emit("vector", f_m2, waits=[("e_dv", sch.cur("e_dv"))], incs=[("e_dv", 1)])
                        def f_var(e):
                            return e.tensor_sub(col["var"][:], col["ex2"][:], col["m2c"][:])
                        sch.emit("vector", f_var, waits=[("e_dv", sch.cur("e_dv"))], incs=[("e_dv", 1)])
                        var_done = sch.cur("e_dv")
                        def f_ln(e):
                            return e.activation(col["rstd"][:], col["var"][:], AF.Ln,
                                                bias=cvec[:, CV_EPS:CV_EPS + 1])
                        sch.emit("scalar", f_ln, waits=[("e_dv", var_done)], incs=[("e_ac", 1)])
                        def f_exp(e):
                            return e.activation(col["rstd"][:], col["rstd"][:], AF.Exp,
                                                scale=cvec[:, CV_MHALF:CV_MHALF + 1])
                        sch.emit("scalar", f_exp, waits=[("e_ac", sch.cur("e_ac"))], incs=[("e_ac", 1)])
                        rstd_done = sch.cur("e_ac")
                        mb_done = sch.cur("e_dv")
                        xb = xn_t[t % 2]
                        def f_xn(e, xt=xt, xb=xb):
                            return e.tensor_scalar(xb[:], xt, col["m"][:], col["rstd"][:],
                                                   ALU.subtract, ALU.mult)
                        sch.emit("vector", f_xn, waits=[("e_ac", rstd_done), ("e_dv", mb_done)],
                                 incs=[("e_dv", 1)])
                        xn_done_dv = sch.cur("e_dv")
                        def f_bridge(e):
                            return e.nop()
                        sch.emit("scalar", f_bridge, waits=[("e_dv", xn_done_dv)], incs=[("e_ac", 1)])
                        xn_done = sch.cur("e_ac")
                        for kc in range(8):
                            def f_tr(e, xb=xb, kc=kc):
                                return e.transpose(p_tp[:], xb[:, kc * 128:(kc + 1) * 128], ident[:])
                            sch.emit("tensor", f_tr,
                                     waits=[("e_ac", max(xn_done, tp_evac_cnt[0]))],
                                     incs=[("e_pe", 1)])
                            tr_done = sch.cur("e_pe")
                            def f_ev(e, t=t, kc=kc):
                                return e.copy(xnT[:, kc * D + t * 128: kc * D + (t + 1) * 128], p_tp[:])
                            sch.emit("scalar", f_ev, waits=[("e_pe", tr_done)], incs=[("e_ac", 1)])
                            tp_evac_cnt[0] = sch.cur("e_ac")

                    xnT_ready = sch.cur("e_ac")

                    wg_loaded = 48
                    for t in range(NT):
                        sig_prev = sch.cur("e_ac")
                        for kc in range(8):
                            def f_g(e, t=t, kc=kc):
                                return e.matmul(p_gt[:], xnT[:, kc * D + t * 128: kc * D + (t + 1) * 128],
                                                wg_sb[:, kc * 8:(kc + 1) * 8],
                                                start=(kc == 0), stop=(kc == 7))
                            w = [("e_ac", xnT_ready), ("d_gp", wg_loaded)] if kc == 0 else ()
                            if kc == 0:
                                w = list(w) + [("e_dv", sch.cur("e_dv"))]
                            sch.emit("tensor", f_g, waits=w,
                                     incs=([("e_pe", 1)] if kc == 7 else ()))
                        gmm_done = sch.cur("e_pe")
                        def f_sig(e):
                            return e.activation(gates[:], p_gt[:], AF.Sigmoid)
                        sch.emit("scalar", f_sig, waits=[("e_pe", gmm_done)], incs=[("e_ac", 1)])
                        sig_done = sch.cur("e_ac")
                        chain = [
                            lambda e: e.reduce_max(col["gm1"][:], gates[:], axis=AX.X),
                            lambda e: e.tensor_scalar(gtmp[:], gates[:], col["gm1"][:], None, ALU.is_ge),
                            lambda e: e.scalar_tensor_tensor(gtmp2[:], gtmp[:], -2.0, gates[:], ALU.mult, ALU.add),
                            lambda e: e.reduce_max(col["gm2"][:], gtmp2[:], axis=AX.X),
                            lambda e: e.tensor_scalar(gtmp[:], gates[:], col["gm2"][:], None, ALU.is_ge),
                            lambda e: e.tensor_add(col["den"][:], col["gm1"][:], col["gm2"][:]),
                            lambda e: e.reciprocal(col["rden"][:], col["den"][:]),
                            lambda e: e.scalar_tensor_tensor(gtmp2[:], gates[:], col["rden"][:], gtmp[:], ALU.mult, ALU.mult),
                        ]
                        for i, cf in enumerate(chain):
                            w = [("e_ac", sig_done)] if i == 0 else [("e_dv", sch.cur("e_dv"))]
                            sch.emit("vector", cf, waits=w, incs=[("e_dv", 1)])
                        def f_wsc(e, t=t):
                            return e.tensor_scalar_mul(wsc[:, t * 8:(t + 1) * 8], gtmp2[:], 1.0 / 6.0)
                        sch.emit("vector", f_wsc, waits=[("e_dv", sch.cur("e_dv"))], incs=[("e_dv", 1)])
                    wsc_ready = sch.cur("e_dv")

                    silu_done = {}
                    hp_done = {}
                    yev_done = {}
                    for u in range(NUNIT):
                        sl13 = [lmap[(u, 0)] % 3, lmap[(u, 1)] % 3]
                        sl2 = None
                        for tt in range(NTT):
                            for fc in range(16):
                                half = fc // 8
                                l = lmap[(u, half)]
                                slot = l % 3
                                fcl = fc % 8
                                b = fc % 2
                                for which, psum, wofs in ((0, p_h1[b], 0), (1, p_h3[b], 8192)):
                                    for kc in range(8):
                                        def f_mm(e, slot=slot, wofs=wofs, kc=kc, fcl=fcl, tt=tt, psum=psum):
                                            return e.matmul(
                                                psum[:],
                                                wslot[slot][:, wofs + kc * 1024 + fcl * 128: wofs + kc * 1024 + (fcl + 1) * 128],
                                                xnT[:, kc * D + tt * 512: kc * D + (tt + 1) * 512],
                                                start=(kc == 0), stop=(kc == 7))
                                        w = []
                                        if kc == 0:
                                            w.append(("d_gp", loaded_dgp[(phase_tag, l)]))
                                            if which == 0:
                                                w.append(("e_ac", silu_done.get((u, tt, fc - 2), silu_done.get((u, tt - 1, 14 + b), silu_done.get((u - 1, NTT - 1, 14 + b), 0)))))
                                            else:
                                                w.append(("e_dv", hp_done.get((u, tt, fc - 2), hp_done.get((u, tt - 1, 14 + b), hp_done.get((u - 1, NTT - 1, 14 + b), 0)))))
                                        sch.emit("tensor", f_mm, waits=w,
                                                 incs=([("e_pe", 1)] if kc == 7 else ()))
                                    if which == 0:
                                        h1_done = sch.cur("e_pe")
                                    else:
                                        h3_done = sch.cur("e_pe")
                                slot_free_pe[slot] = sch.cur("e_pe")
                                tb = tsil[fc % 2]
                                tmp_free = hp_done.get((u, tt, fc - 2), hp_done.get((u, tt - 1, 14 + (fc % 2)), hp_done.get((u - 1, NTT - 1, 14 + (fc % 2)), 0)))
                                def f_sil(e, tb=tb, psum=p_h1[b]):
                                    return e.activation(tb[:], psum[:], AF.Silu)
                                sch.emit("scalar", f_sil,
                                         waits=[("e_pe", h1_done), ("e_dv", tmp_free)],
                                         incs=[("e_ac", 1)])
                                silu_done[(u, tt, fc)] = sch.cur("e_ac")
                                def f_hp(e, tb=tb, psum=p_h3[b], fc=fc):
                                    return e.tensor_mul(hT[:, fc * 512:(fc + 1) * 512], tb[:], psum[:])
                                sch.emit("vector", f_hp,
                                         waits=[("e_ac", silu_done[(u, tt, fc)]), ("e_pe", h3_done)],
                                         incs=[("e_dv", 1)])
                                hp_done[(u, tt, fc)] = sch.cur("e_dv")
                                if tt == NTT - 1 and fc in (7, 15) and u + 1 < NUNIT:
                                    ns = 0 if fc == 7 else 1
                                    lmap[(u + 1, ns)] = sched_slot_load(g, u + 1, ns)
                            hp_all = sch.cur("e_dv")
                            l2 = lmap[(u, 2)]
                            slot2 = l2 % 3
                            for m in range(4):
                                t = tt * 4 + m
                                for dc in range(2):
                                    b = (m * 2 + dc) % 2
                                    for fc in range(16):
                                        def f_ymm(e, fc=fc, m=m, dc=dc, slot2=slot2, b=b):
                                            return e.matmul(
                                                p_y[b][:],
                                                hT[:, fc * 512 + m * 128: fc * 512 + (m + 1) * 128],
                                                wslot[slot2][:, fc * 1024 + dc * 512: fc * 1024 + (dc + 1) * 512],
                                                start=(fc == 0), stop=(fc == 15))
                                        w = []
                                        if fc == 0:
                                            w = [("e_dv", hp_all), ("d_gp", loaded_dgp[(phase_tag, l2)]),
                                                 ("e_dv", yev_done.get((u, tt, m * 2 + dc - 2),
                                                          yev_done.get((u, tt - 1, 6 + b),
                                                          yev_done.get((u - 1, NTT - 1, 6 + b), 0))))]
                                        sch.emit("tensor", f_ymm, waits=w,
                                                 incs=([("e_pe", 1)] if fc == 15 else ()))
                                    ymm_done = sch.cur("e_pe")
                                    prev_write = yev_done.get((u - 1, tt, m * 2 + dc), 0)
                                    if u == 0:
                                        def f_ye(e, t=t, dc=dc, b=b):
                                            return e.scalar_tensor_tensor(
                                                x_g[:, t * D + dc * 512: t * D + (dc + 1) * 512],
                                                p_y[b][:], 1.0 / 6.0,
                                                x_g[:, t * D + dc * 512: t * D + (dc + 1) * 512],
                                                ALU.mult, ALU.add)
                                    else:
                                        def f_ye(e, t=t, dc=dc, b=b, u=u):
                                            return e.scalar_tensor_tensor(
                                                x_g[:, t * D + dc * 512: t * D + (dc + 1) * 512],
                                                p_y[b][:], wsc[:, t * 8 + (u - 1): t * 8 + u],
                                                x_g[:, t * D + dc * 512: t * D + (dc + 1) * 512],
                                                ALU.mult, ALU.add)
                                    sch.emit("vector", f_ye,
                                             waits=[("e_pe", ymm_done), ("e_dv", max(prev_write, wsc_ready))],
                                             incs=[("e_dv", 1)])
                                    yev_done[(u, tt, m * 2 + dc)] = sch.cur("e_dv")
                        slot_free_pe[lmap[(u, 2)] % 3] = sch.cur("e_pe")
                        if u + 1 < NUNIT:
                            lmap[(u + 1, 2)] = sched_slot_load(g, u + 1, 2)

                    last_ev = sch.cur("e_dv")
                    def f_store(e, g=g):
                        return e.dma_start(
                            out=dst.rearrange("(t p) d2 -> p t d2", p=128)[:, g * NT:(g + 1) * NT, :],
                            in_=x_g.ap().rearrange("p (t d2) -> p t d2", t=NT))
                    sch.emit("sync", f_store, waits=[("e_dv", last_ev)], incs=[("d_sy", 16)])

        def ln_tile(col, cvx, x_ap, out_ap, xload_sem, xload_cnt, extra_wait=()):
            def f_sum(e):
                return e.reduce_sum(col["sum"][:], x_ap, axis=AX.X)
            sch.emit("vector", f_sum, waits=[(xload_sem, xload_cnt)] + list(extra_wait), incs=[("e_dv", 1)])
            def f_sq(e):
                return e.square(col["dump"][:], x_ap)
            sch.emit("scalar", f_sq, waits=[(xload_sem, xload_cnt), ("e_dv", col["sq_free"][0])],
                     incs=[("e_ac", 1)])
            sq_ac = sch.cur("e_ac")
            def f_ssq(e):
                return e.reduce_sum(col["sumsq"][:], col["dump"][:], axis=AX.X)
            sch.emit("vector", f_ssq, waits=[("e_ac", sq_ac)], incs=[("e_dv", 1)])
            sq = sch.cur("e_dv")
            col["sq_free"][0] = sq
            steps = [
                lambda e: e.tensor_scalar_mul(col["m"][:], col["sum"][:], 1.0 / D),
                lambda e: e.tensor_scalar_mul(col["ex2"][:], col["sumsq"][:], 1.0 / D),
                lambda e: e.tensor_scalar_mul(col["m2c"][:], col["m"][:], col["m"][:]),
                lambda e: e.tensor_sub(col["var"][:], col["ex2"][:], col["m2c"][:]),
            ]
            for i, s in enumerate(steps):
                sch.emit("vector", s, waits=[("e_dv", sq if i == 0 else sch.cur("e_dv"))],
                         incs=[("e_dv", 1)])
            var_done = sch.cur("e_dv")
            def f_ln(e):
                return e.activation(col["rstd"][:], col["var"][:], AF.Ln, bias=cvx[:, 0:1])
            sch.emit("scalar", f_ln, waits=[("e_dv", var_done)], incs=[("e_ac", 1)])
            def f_exp(e):
                return e.activation(col["rstd"][:], col["rstd"][:], AF.Exp, scale=cvx[:, 1:2])
            sch.emit("scalar", f_exp, waits=[("e_ac", sch.cur("e_ac"))], incs=[("e_ac", 1)])
            rstd_done = sch.cur("e_ac")
            def f_xn(e):
                return e.tensor_scalar(out_ap, x_ap, col["m"][:], col["rstd"][:],
                                       ALU.subtract, ALU.mult)
            sch.emit("vector", f_xn, waits=[("e_ac", rstd_done), ("e_dv", sch.cur("e_dv"))],
                     incs=[("e_dv", 1)])
            xn_done_dv = sch.cur("e_dv")
            def f_bridge(e):
                return e.nop()
            sch.emit("scalar", f_bridge, waits=[("e_dv", xn_done_dv)], incs=[("e_ac", 1)])
            return sch.cur("e_ac")

        def transposes8(src_sb, dst_sb, p_tp, ready_ac, tp_evac):
            for kc in range(8):
                def f_tr(e, kc=kc):
                    return e.transpose(p_tp[:], src_sb[:, kc * 128:(kc + 1) * 128], ident[:])
                sch.emit("tensor", f_tr, waits=[("e_ac", max(ready_ac, tp_evac[0]))],
                         incs=[("e_pe", 1)])
                tr_done = sch.cur("e_pe")
                def f_ev(e, kc=kc):
                    return e.copy(dst_sb[:, kc * 128:(kc + 1) * 128], p_tp[:])
                sch.emit("scalar", f_ev, waits=[("e_pe", tr_done)], incs=[("e_ac", 1)])
                tp_evac[0] = sch.cur("e_ac")

        def rope4(p_src, gt_sb, t0, wbuf, out_bf, ch, cos_sb, sin_sb, psrc_done):
            cosb = bass.AP(cos_sb, ch * 32, [[NCH * 32, 128], [0, 32], [1, 32]])
            sinb = bass.AP(sin_sb, ch * 32, [[NCH * 32, 128], [0, 16], [1, 32]])
            hap = lambda t, o: bass.AP(t, o, [[int(t.shape[1]), 128], [64, 16], [1, 32]])
            def f_t0(e):
                return e.tensor_mul(t0[:], p_src[:], gt_sb[:])
            sch.emit("vector", f_t0, waits=[("e_pe", psrc_done)], incs=[("e_dv", 1)])
            t0_done = sch.cur("e_dv")
            def f_u(e):
                return e.tensor_tensor(wbuf[:], t0[:], cosb, ALU.mult)
            sch.emit("vector", f_u, waits=[("e_dv", t0_done)], incs=[("e_dv", 1)])
            u_done = sch.cur("e_dv")
            def f_w0(e):
                return e.scalar_tensor_tensor(hap(out_bf, 0), hap(t0, 32), -1.0, sinb,
                                              ALU.mult, ALU.mult)
            sch.emit("vector", f_w0, waits=[("e_dv", t0_done)], incs=[("e_dv", 1)])
            def f_w1(e):
                return e.tensor_tensor(hap(out_bf, 32), hap(t0, 0), sinb, ALU.mult)
            sch.emit("vector", f_w1, waits=[("e_dv", t0_done)], incs=[("e_dv", 1)])
            w_done = sch.cur("e_dv")
            def f_c0(e):
                return e.tensor_add(hap(out_bf, 0), hap(out_bf, 0), hap(wbuf, 0))
            sch.emit("vector", f_c0, waits=[("e_dv", w_done)], incs=[("e_dv", 1)])
            def f_c1(e):
                return e.tensor_add(hap(out_bf, 32), hap(out_bf, 32), hap(wbuf, 32))
            sch.emit("vector", f_c1, waits=[("e_dv", sch.cur("e_dv"))], incs=[("e_dv", 1)])
            return sch.cur("e_dv")

        def attn_pass1():
            fs = contextlib.ExitStack()
            with fs:
                FE = fs.enter_context
                wk_sb = FE(nc.sbuf_tensor("wk_sb", [128, 8 * D], BF16))
                wv_sb = FE(nc.sbuf_tensor("wv_sb", [128, 8 * D], BF16))
                gk_sb = FE(nc.sbuf_tensor("gk_sb", [128, D], F32))
                gc_sb = FE(nc.sbuf_tensor("gc_sb", [64, D], F32))
                cos_sb = FE(nc.sbuf_tensor("cos_sb1", [128, NCH * 32], F32))
                sin_sb = FE(nc.sbuf_tensor("sin_sb1", [128, NCH * 32], F32))
                x_ch = [FE(nc.sbuf_tensor(f"x_ch1_{i}", [128, D], F32)) for i in range(2)]
                xn_ch = [FE(nc.sbuf_tensor(f"xn_ch1_{i}", [128, D], BF16)) for i in range(2)]
                xnT_ch = [FE(nc.sbuf_tensor(f"xnT_ch1_{i}", [128, D], BF16)) for i in range(2)]
                t0 = FE(nc.sbuf_tensor("t0_1", [128, D], F32))
                wbuf = FE(nc.sbuf_tensor("wbuf_1", [128, D], F32))
                khat = [FE(nc.sbuf_tensor(f"khat_{i}", [128, D], BF16)) for i in range(2)]
                khT = [FE(nc.sbuf_tensor(f"khT_{i}", [128, D], BF16)) for i in range(2)]
                vbf = [FE(nc.sbuf_tensor(f"vbf_{i}", [128, D], BF16)) for i in range(2)]
                sgb = FE(nc.sbuf_tensor("sgb", [64, D], F32))
                sall = FE(nc.sbuf_tensor("sall", [64, D], F32))
                srecv = FE(nc.sbuf_tensor("srecv", [64, D], F32))
                tst = FE(nc.sbuf_tensor("tst", [64, D], F32))
                col = {n: FE(nc.sbuf_tensor(f"a1_{n}", [128, 1], F32))
                       for n in ["sum", "sumsq", "m", "ex2", "m2c", "var", "rstd", "mb"]}
                col["dump"] = FE(nc.sbuf_tensor("a1_dump", [128, D], F32))
                col["sq_free"] = [0]
                p_tp = FE(nc.psum_tensor("a1_tp", [128, 128], BF16))
                p_k = FE(nc.psum_tensor("a1_pk", [128, D], F32))
                p_v = FE(nc.psum_tensor("a1_pv", [128, D], F32))
                p_s = FE(nc.psum_tensor("a1_ps", [64, D], F32))

                def f_w(e):
                    e.dma_start(out=wk_sb.ap().rearrange("p (kc d2) -> p kc d2", kc=8),
                                in_=wkT.rearrange("(kc p) d2 -> p kc d2", p=128)).then_inc(sems["d_gp"], 16)
                    e.dma_start(out=wv_sb.ap().rearrange("p (kc d2) -> p kc d2", kc=8),
                                in_=wvT.rearrange("(kc p) d2 -> p kc d2", p=128)).then_inc(sems["d_gp"], 16)
                sch.emit("gpsimd", f_w, incs=[("d_gp", 32)], self_inc=True)
                def f_c(e):
                    e.dma_start(out=gk_sb[:], in_=gk_in).then_inc(sems["d_sc"], 16)
                    e.dma_start(out=gc_sb[:], in_=gc_in).then_inc(sems["d_sc"], 16)
                    e.dma_start(out=cos_sb[:], in_=cos_in).then_inc(sems["d_sc"], 16)
                    e.dma_start(out=sin_sb[:], in_=sin_in).then_inc(sems["d_sc"], 16)
                sch.emit("scalar", f_c, incs=[("d_sc", 64)], self_inc=True)
                w_loaded = sch.cur("d_gp")
                c_loaded = sch.cur("d_sc")
                sch.emit("vector", lambda e: e.memset(sgb[:], 0.0), incs=[("e_dv", 1)])

                tp_evac = [sch.cur("e_ac")]
                kv_mm_done = {}
                rope_done = {}
                store_done = {}
                for ch in range(NCH):
                    r = ch % 2
                    def f_x(e, ch=ch, r=r):
                        return e.dma_start(out=x_ch[r][:], in_=x_work.rearrange(
                            "(c p) d2 -> c p d2", p=128)[ch])
                    sch.emit("sync", f_x, waits=[("e_dv", rope_done.get(ch - 2, 0))],
                             incs=[("d_sy", 16)])
                    x_loaded = sch.cur("d_sy")
                    xn_done = ln_tile(col, cvec, x_ch[r][:], xn_ch[r][:], "d_sy", x_loaded)
                    transposes8(xn_ch[r], xnT_ch[r], p_tp, xn_done, tp_evac)
                    xnT_done = sch.cur("e_ac")
                    for which, psum, wsb in ((0, p_k, wk_sb), (1, p_v, wv_sb)):
                        for dc in range(2):
                            for kc in range(8):
                                def f_mm(e, psum=psum, wsb=wsb, dc=dc, kc=kc, r=r):
                                    return e.matmul(
                                        psum[:, dc * 512:(dc + 1) * 512],
                                        xnT_ch[r][:, kc * 128:(kc + 1) * 128],
                                        wsb[:, kc * D + dc * 512: kc * D + (dc + 1) * 512],
                                        start=(kc == 0), stop=(kc == 7))
                                w = []
                                if kc == 0:
                                    w = [("e_ac", xnT_done)]
                                    if ch == 0 and which == 0 and dc == 0:
                                        w.append(("d_gp", w_loaded))
                                    if which == 0:
                                        w.append(("e_dv", rope_done.get(ch - 1, 0)))
                                    else:
                                        w.append(("e_dv", store_done.get(("vev", ch - 1), 0)))
                                sch.emit("tensor", f_mm, waits=w,
                                         incs=([("e_pe", 1)] if kc == 7 else ()))
                        kv_mm_done[(ch, which)] = sch.cur("e_pe")
                    if ch == 0:
                        sch.emit("vector", lambda e: None, waits=[("d_sc", c_loaded)])
                    kd = rope4(p_k, gk_sb, t0, wbuf, khat[r], ch, cos_sb, sin_sb,
                               kv_mm_done[(ch, 0)])
                    rope_done[ch] = kd
                    def f_vev(e, r=r):
                        return e.tensor_copy(vbf[r][:], p_v[:])
                    sch.emit("vector", f_vev,
                             waits=[("e_pe", kv_mm_done[(ch, 1)]),
                                    ("d_sc", store_done.get(("kvst", ch - 2), 0))],
                             incs=[("e_dv", 1)])
                    store_done[("vev", ch)] = sch.cur("e_dv")
                    for kc in range(8):
                        def f_tr(e, r=r, kc=kc):
                            return e.transpose(p_tp[:], khat[r][:, kc * 128:(kc + 1) * 128], ident[:])
                        w = [("e_dv", kd), ("e_ac", tp_evac[0])]
                        sch.emit("tensor", f_tr, waits=w, incs=[("e_pe", 1)])
                        tr_done = sch.cur("e_pe")
                        def f_ev(e, r=r, kc=kc):
                            return e.copy(khT[r][:, kc * 128:(kc + 1) * 128], p_tp[:])
                        w2 = [("e_pe", tr_done)]
                        if kc == 0:
                            w2.append(("d_sc", store_done.get(("kvst", ch - 2), 0)))
                        sch.emit("scalar", f_ev, waits=w2, incs=[("e_ac", 1)])
                        tp_evac[0] = sch.cur("e_ac")
                    khT_done = sch.cur("e_ac")
                    def f_st(e, ch=ch, r=r):
                        e.dma_start(out=kt_sc.rearrange("(c p) d2 -> c p d2", p=128)[ch],
                                    in_=khT[r][:]).then_inc(sems["d_sc"], 16)
                        e.dma_start(out=v_sc.rearrange("(c p) d2 -> c p d2", p=128)[ch],
                                    in_=vbf[r][:]).then_inc(sems["d_sc"], 16)
                    sch.emit("scalar", f_st, waits=[("e_ac", khT_done), ("e_dv", store_done[("vev", ch)])],
                             incs=[("d_sc", 32)], self_inc=True)
                    store_done[("kvst", ch)] = sch.cur("d_sc")
                    for h in range(H):
                        def f_smm(e, h=h, r=r):
                            return e.matmul(p_s[:, h * 64:(h + 1) * 64],
                                            khat[r][:, h * 64:(h + 1) * 64],
                                            vbf[r][:, h * 64:(h + 1) * 64],
                                            start=True, stop=True)
                        w = []
                        if h == 0:
                            w = [("e_dv", max(kd, store_done[("vev", ch)],
                                              store_done.get(("sup", ch - 1), 0)))]
                        sch.emit("tensor", f_smm, waits=w,
                                 incs=([("e_pe", 1)] if h == H - 1 else ()))
                    smm_done = sch.cur("e_pe")
                    def f_sup(e):
                        return e.tensor_add(sall[:], sgb[:], p_s[:])
                    sch.emit("vector", f_sup, waits=[("e_pe", smm_done), ("e_dv", sch.cur("e_dv"))],
                             incs=[("e_dv", 1)])
                    def f_sg(e):
                        return e.tensor_mul(sgb[:], sall[:], gc_sb[:])
                    sch.emit("vector", f_sg,
                             waits=[("e_dv", sch.cur("e_dv")),
                                    ("d_sc", store_done.get(("sgst", ch - 1), 0))],
                             incs=[("e_dv", 1)])
                    store_done[("sup", ch)] = sch.cur("e_dv")
                    if ch < NCH - 1:
                        def f_sgst(e, ch=ch):
                            return e.dma_start(out=sg_sc.rearrange("(c p) d2 -> c p d2", p=64)[ch],
                                               in_=sgb[:])
                        sch.emit("scalar", f_sgst, waits=[("e_dv", store_done[("sup", ch)])],
                                 incs=[("d_sc", 16)])
                        store_done[("sgst", ch)] = sch.cur("d_sc")
                sall_done = sch.cur("e_dv")

                def f_xch(e):
                    with (
                        e.register("r_h") as r_h,
                        e.register("r_soff") as r_soff,
                        e.register("r_poff") as r_poff,
                        e.register("r_foff") as r_foff,
                        e.register("r_pfoff") as r_pfoff,
                        e.register("r_magic") as r_magic,
                        e.register("r_flag") as r_flag,
                        e.register("r_cond") as r_cond,
                        e.register("r_tgt") as r_tgt,
                    ):
                        pid = e.compute_val(nc.partition_id(), donate=False)
                        e.reg_alu(r_h, pid, 1, ALU.bitwise_and)
                        e.reg_mul(r_soff, r_h, 64 * D)
                        e.reg_alu(r_poff, r_h, 1, ALU.bitwise_xor)
                        e.reg_mul(r_poff, r_poff, 64 * D)
                        e.reg_mul(r_foff, r_h, 16)
                        e.reg_alu(r_pfoff, r_h, 1, ALU.bitwise_xor)
                        e.reg_mul(r_pfoff, r_pfoff, 16)
                        e.reg_load(r_magic, s_magic[:1, :1])
                        e.dma_start(out=bass.AP(xch, r_soff, [[D, 64], [1, D]]),
                                    in_=sall[:]).then_inc(sems["pl"], 16)
                        e.reg_mov(r_tgt, 16)
                        e.wait_ge(sems["pl"], r_tgt)
                        e.dma_start(out=bass.AP(xchf, r_foff, [[16, 1], [1, 16]]),
                                    in_=s_magic[:]).then_inc(sems["pl"], 16)
                        e.reg_mov(r_tgt, 32)
                        e.wait_ge(sems["pl"], r_tgt)
                        e.reg_mov(r_cond, 1)
                        with e.While(r_cond):
                            e.dma_start(out=s_flag[:],
                                        in_=bass.AP(xchf, r_pfoff, [[16, 1], [1, 16]])
                                        ).then_inc(sems["pl"], 16)
                            e.reg_add(r_tgt, r_tgt, 16)
                            e.wait_ge(sems["pl"], r_tgt)
                            e.reg_load(r_flag, s_flag[:1, :1])
                            e.reg_sub(r_cond, r_flag, r_magic)
                        e.dma_start(out=srecv[:],
                                    in_=bass.AP(xch, r_poff, [[D, 64], [1, D]])
                                    ).then_inc(sems["pl"], 16)
                        e.reg_add(r_tgt, r_tgt, 16)
                        e.wait_ge(sems["pl"], r_tgt)
                    return e.nop()
                sch.emit("gpsimd", f_xch,
                         waits=[("e_dv", sall_done), ("d_gp", sch.cur("d_gp"))],
                         incs=[("e_gp", 1)])
                def f_t0(e):
                    return e.scalar_tensor_tensor(tst[:], srecv[:], rmask[:], gc_sb[:],
                                                  ALU.mult, ALU.mult)
                sch.emit("vector", f_t0, waits=[("e_gp", sch.cur("e_gp"))], incs=[("e_dv", 1)])
                def f_tst(e):
                    return e.dma_start(out=sg_sc.rearrange("(c p) d2 -> c p d2", p=64)[NCH - 1],
                                       in_=tst[:])
                sch.emit("scalar", f_tst, waits=[("e_dv", sch.cur("e_dv"))], incs=[("d_sc", 16)])

        def attn_pass2():
            fs = contextlib.ExitStack()
            with fs:
                FE = fs.enter_context
                wq_sb = FE(nc.sbuf_tensor("wq_sb", [128, 8 * D], BF16))
                wgt_sb = FE(nc.sbuf_tensor("wgt_sb", [128, 8 * D], BF16))
                wo_sb = FE(nc.sbuf_tensor("wo_sb", [128, 8 * D], BF16))
                gq_sb = FE(nc.sbuf_tensor("gq_sb", [128, D], F32))
                gc_sb = FE(nc.sbuf_tensor("gc_sb2", [64, D], F32))
                cos_sb = FE(nc.sbuf_tensor("cos_sb2", [128, NCH * 32], F32))
                sin_sb = FE(nc.sbuf_tensor("sin_sb2", [128, NCH * 32], F32))
                x_ch = [FE(nc.sbuf_tensor(f"x_ch2_{i}", [128, D], F32)) for i in range(2)]
                xn_ch = [FE(nc.sbuf_tensor(f"xn_ch2_{i}", [128, D], BF16)) for i in range(2)]
                xnT_ch = [FE(nc.sbuf_tensor(f"xnT_ch2_{i}", [128, D], BF16)) for i in range(2)]
                t0 = FE(nc.sbuf_tensor("t0_2", [128, D], F32))
                wbuf = FE(nc.sbuf_tensor("wbuf_2", [128, D], F32))
                qhat = FE(nc.sbuf_tensor("qhat", [128, D], BF16))
                qhT = FE(nc.sbuf_tensor("qhT", [128, D], BF16))
                g_sb = FE(nc.sbuf_tensor("g_sb", [128, D], BF16))
                og = FE(nc.sbuf_tensor("og", [128, D], BF16))
                ogT = FE(nc.sbuf_tensor("ogT", [128, D], BF16))
                kh_sb = [FE(nc.sbuf_tensor(f"kh2_{i}", [128, D], BF16)) for i in range(2)]
                v_sb = [FE(nc.sbuf_tensor(f"v2_{i}", [128, D], BF16)) for i in range(2)]
                adt = [FE(nc.sbuf_tensor(f"adt_{i}", [128, 128], BF16)) for i in range(2)]
                sg_r = [FE(nc.sbuf_tensor(f"sg_r{i}", [64, D], F32)) for i in range(2)]
                tst = FE(nc.sbuf_tensor("tst2", [64, D], F32))
                rhs_s = FE(nc.sbuf_tensor("rhs_s", [128, D], BF16))
                col = {n: FE(nc.sbuf_tensor(f"a2_{n}", [128, 1], F32))
                       for n in ["sum", "sumsq", "m", "ex2", "m2c", "var", "rstd", "mb"]}
                col["dump"] = FE(nc.sbuf_tensor("a2_dump", [128, D], F32))
                col["sq_free"] = [0]
                p_tp = FE(nc.psum_tensor("a2_tp", [128, 128], BF16))
                p_qy = FE(nc.psum_tensor("a2_qy", [128, D], F32))
                p_g = FE(nc.psum_tensor("a2_g", [128, D], F32))
                p_at = [FE(nc.psum_tensor(f"a2_at{i}", [128, 128], F32)) for i in range(2)]
                p_o = FE(nc.psum_tensor("a2_o", [128, 512], F32))

                def f_w(e):
                    e.dma_start(out=wq_sb.ap().rearrange("p (kc d2) -> p kc d2", kc=8),
                                in_=wqT.rearrange("(kc p) d2 -> p kc d2", p=128)).then_inc(sems["d_gp"], 16)
                    e.dma_start(out=wgt_sb.ap().rearrange("p (kc d2) -> p kc d2", kc=8),
                                in_=wgT.rearrange("(kc p) d2 -> p kc d2", p=128)).then_inc(sems["d_gp"], 16)
                    e.dma_start(out=wo_sb.ap().rearrange("p (kc d2) -> p kc d2", kc=8),
                                in_=woT.rearrange("(kc p) d2 -> p kc d2", p=128)).then_inc(sems["d_gp"], 16)
                sch.emit("gpsimd", f_w, incs=[("d_gp", 48)], self_inc=True)
                def f_c(e):
                    e.dma_start(out=gq_sb[:], in_=gq_in).then_inc(sems["d_sc"], 16)
                    e.dma_start(out=gc_sb[:], in_=gc_in).then_inc(sems["d_sc"], 16)
                    e.dma_start(out=cos_sb[:], in_=cos_in).then_inc(sems["d_sc"], 16)
                    e.dma_start(out=sin_sb[:], in_=sin_in).then_inc(sems["d_sc"], 16)
                    e.dma_start(out=tst[:],
                                in_=sg_sc.rearrange("(c p) d2 -> c p d2", p=64)[NCH - 1]).then_inc(sems["d_sc"], 16)
                sch.emit("scalar", f_c, incs=[("d_sc", 80)], self_inc=True)
                w_loaded = sch.cur("d_gp")
                c_loaded = sch.cur("d_sc")

                tp_evac = [sch.cur("e_ac")]
                done = {}
                for ch in range(NCH):
                    r = ch % 2
                    def f_x(e, ch=ch, r=r):
                        return e.dma_start(out=x_ch[r][:], in_=x_work.rearrange(
                            "(c p) d2 -> c p d2", p=128)[ch])
                    sch.emit("sync", f_x,
                             waits=[("d_sy", sch.cur("d_sy") if ch < 2 else done[("xst", ch - 2)])],
                             incs=[("d_sy", 16)])
                    x_loaded = sch.cur("d_sy")
                    def f_kv(e, ch=ch, r=r):
                        e.dma_start(out=kh_sb[r][:],
                                    in_=kt_sc.rearrange("(c p) d2 -> c p d2", p=128)[ch]).then_inc(sems["d_sc"], 16)
                        e.dma_start(out=v_sb[r][:],
                                    in_=v_sc.rearrange("(c p) d2 -> c p d2", p=128)[ch]).then_inc(sems["d_sc"], 16)
                    sch.emit("scalar", f_kv, waits=[("e_pe", done.get(("kvuse", ch - 2), 0))],
                             incs=[("d_sc", 32)], self_inc=True)
                    kv_loaded = sch.cur("d_sc")
                    if ch >= 1:
                        def f_sg(e, ch=ch, r=r):
                            return e.dma_start(out=sg_r[r][:],
                                               in_=sg_sc.rearrange("(c p) d2 -> c p d2", p=64)[ch - 1])
                        sch.emit("scalar", f_sg, waits=[("e_dv", done.get(("rhs", ch - 2), 0))],
                                 incs=[("d_sc", 16)])
                    sg_loaded = sch.cur("d_sc")

                    xn_done = ln_tile(col, cvec, x_ch[r][:], xn_ch[r][:], "d_sy", x_loaded)
                    transposes8(xn_ch[r], xnT_ch[r], p_tp, xn_done, tp_evac)
                    xnT_done = sch.cur("e_ac")
                    for which, psum, wsb in ((0, p_qy, wq_sb), (1, p_g, wgt_sb)):
                        for dc in range(2):
                            for kc in range(8):
                                def f_mm(e, psum=psum, wsb=wsb, dc=dc, kc=kc, r=r):
                                    return e.matmul(
                                        psum[:, dc * 512:(dc + 1) * 512],
                                        xnT_ch[r][:, kc * 128:(kc + 1) * 128],
                                        wsb[:, kc * D + dc * 512: kc * D + (dc + 1) * 512],
                                        start=(kc == 0), stop=(kc == 7))
                                w = []
                                if kc == 0:
                                    w = [("e_ac", xnT_done)]
                                    if ch == 0 and which == 0 and dc == 0:
                                        w.append(("d_gp", w_loaded))
                                        w.append(("d_sc", c_loaded))
                                    if which == 0:
                                        w.append(("e_dv", done.get(("xadd", ch - 1), 0)))
                                    else:
                                        w.append(("e_ac", done.get(("gsil", ch - 1), 0)))
                                sch.emit("tensor", f_mm, waits=w,
                                         incs=([("e_pe", 1)] if kc == 7 else ()))
                        done[(("qmm" if which == 0 else "gmm"), ch)] = sch.cur("e_pe")
                    def f_gs(e):
                        return e.activation(g_sb[:], p_g[:], AF.Silu)
                    sch.emit("scalar", f_gs, waits=[("e_pe", done[("gmm", ch)]),
                                                    ("e_dv", done.get(("ogev", ch - 1), 0))],
                             incs=[("e_ac", 1)])
                    done[("gsil", ch)] = sch.cur("e_ac")
                    qd = rope4(p_qy, gq_sb, t0, wbuf, qhat, ch, cos_sb, sin_sb,
                               done[("qmm", ch)])
                    for kc in range(8):
                        def f_tr(e, kc=kc):
                            return e.transpose(p_tp[:], qhat[:, kc * 128:(kc + 1) * 128], ident[:])
                        sch.emit("tensor", f_tr, waits=[("e_dv", qd), ("e_ac", tp_evac[0])],
                                 incs=[("e_pe", 1)])
                        tr_done = sch.cur("e_pe")
                        def f_ev(e, kc=kc):
                            return e.copy(qhT[:, kc * 128:(kc + 1) * 128], p_tp[:])
                        sch.emit("scalar", f_ev, waits=[("e_pe", tr_done)], incs=[("e_ac", 1)])
                        tp_evac[0] = sch.cur("e_ac")
                    qhT_done = sch.cur("e_ac")
                    if ch == 0:
                        def f_rhs(e, r=r):
                            return e.tensor_scalar_mul(rhs_s[0:64, :], tst[:], 1.0)
                        sch.emit("vector", f_rhs, waits=[("d_sc", c_loaded)], incs=[("e_dv", 1)])
                    else:
                        def f_rhs(e, r=r):
                            return e.tensor_add(rhs_s[0:64, :], sg_r[r][:], tst[:])
                        sch.emit("vector", f_rhs,
                                 waits=[("d_sc", sg_loaded), ("e_dv", sch.cur("e_dv")),
                                        ("e_pe", done.get(("cross", ch - 1), 0))],
                                 incs=[("e_dv", 1)])
                    def f_rhs2(e):
                        return e.tensor_copy(rhs_s[64:128, :], rhs_s[0:64, :])
                    sch.emit("vector", f_rhs2, waits=[("e_dv", sch.cur("e_dv"))], incs=[("e_dv", 1)])
                    rhs_done = sch.cur("e_dv")
                    def f_tup(e):
                        return e.tensor_mul(tst[:], tst[:], gc_sb[:])
                    sch.emit("vector", f_tup, waits=[("e_dv", rhs_done)], incs=[("e_dv", 1)])
                    done[("rhs", ch)] = sch.cur("e_dv")

                    for hg in range(2):
                        for hh in range(8):
                            h = hg * 8 + hh
                            pb = h % 2
                            rp = (h % 2) * 64
                            kcb = h // 2
                            def f_at(e, h=h, pb=pb, rp=rp, kcb=kcb, r=r):
                                return e.matmul(p_at[pb][:],
                                                kh_sb[r][rp:rp + 64, kcb * 128:(kcb + 1) * 128],
                                                qhT[rp:rp + 64, kcb * 128:(kcb + 1) * 128],
                                                start=True, stop=True)
                            w = [("e_ac", qhT_done)] if h == 0 else []
                            if h == 0:
                                w.append(("d_sc", kv_loaded))
                            w.append(("e_dv", done.get(("adt", ch, h - 2), 0)))
                            sch.emit("tensor", f_at, waits=w, incs=[("e_pe", 1)])
                            at_done = sch.cur("e_pe")
                            def f_adt(e, pb=pb, h=h):
                                return e.tensor_mul(adt[h % 2][:], p_at[pb][:], maskt[:])
                            adt_war = done.get(("omm", ch, h - 2),
                                               done.get(("omm", ch - 1, h + 14), 0))
                            sch.emit("vector", f_adt,
                                     waits=[("e_pe", max(at_done, adt_war))],
                                     incs=[("e_dv", 1)])
                            done[("adt", ch, h)] = sch.cur("e_dv")
                            def f_oi(e, h=h, hh=hh, r=r):
                                return e.matmul(p_o[:, hh * 64:(hh + 1) * 64],
                                                adt[h % 2][:],
                                                v_sb[r][:, h * 64:(h + 1) * 64],
                                                start=True, stop=False)
                            w = [("e_dv", done[("adt", ch, h)])]
                            if hh == 0:
                                w.append(("e_dv", done.get(("ogev_hg", ch, hg - 1),
                                                           done.get(("ogev_hg", ch - 1, 1), 0))))
                            sch.emit("tensor", f_oi, waits=w)
                            def f_oc(e, h=h, hh=hh, rp=rp, kcb=kcb):
                                return e.matmul(p_o[:, hh * 64:(hh + 1) * 64],
                                                qhT[rp:rp + 64, kcb * 128:(kcb + 1) * 128],
                                                rhs_s[rp:rp + 64, h * 64:(h + 1) * 64],
                                                start=False, stop=True)
                            w = []
                            if h == 0:
                                w = [("e_dv", done[("rhs", ch)])]
                            sch.emit("tensor", f_oc, waits=w, incs=[("e_pe", 1)])
                            done[("omm", ch, h)] = sch.cur("e_pe")
                        def f_og(e, hg=hg):
                            return e.tensor_mul(og[:, hg * 512:(hg + 1) * 512],
                                                p_o[:], g_sb[:, hg * 512:(hg + 1) * 512])
                        sch.emit("vector", f_og, waits=[("e_pe", done[("omm", ch, hg * 8 + 7)]),
                                                        ("e_ac", done[("gsil", ch)])],
                                 incs=[("e_dv", 1)])
                        done[("ogev_hg", ch, hg)] = sch.cur("e_dv")
                    done[("ogev", ch)] = sch.cur("e_dv")
                    done[("kvuse", ch)] = sch.cur("e_pe")
                    done[("cross", ch)] = sch.cur("e_pe")
                    for kc in range(8):
                        def f_tr(e, kc=kc):
                            return e.transpose(p_tp[:], og[:, kc * 128:(kc + 1) * 128], ident[:])
                        sch.emit("tensor", f_tr, waits=[("e_dv", done[("ogev", ch)]),
                                                        ("e_ac", tp_evac[0])],
                                 incs=[("e_pe", 1)])
                        tr_done = sch.cur("e_pe")
                        def f_ev(e, kc=kc):
                            return e.copy(ogT[:, kc * 128:(kc + 1) * 128], p_tp[:])
                        sch.emit("scalar", f_ev, waits=[("e_pe", tr_done)], incs=[("e_ac", 1)])
                        tp_evac[0] = sch.cur("e_ac")
                    ogT_done = sch.cur("e_ac")
                    for dc in range(2):
                        for kc in range(8):
                            def f_y(e, dc=dc, kc=kc):
                                return e.matmul(p_qy[:, dc * 512:(dc + 1) * 512],
                                                ogT[:, kc * 128:(kc + 1) * 128],
                                                wo_sb[:, kc * D + dc * 512: kc * D + (dc + 1) * 512],
                                                start=(kc == 0), stop=(kc == 7))
                            w = [("e_ac", ogT_done)] if kc == 0 else []
                            sch.emit("tensor", f_y, waits=w,
                                     incs=([("e_pe", 1)] if kc == 7 else ()))
                        y_done = sch.cur("e_pe")
                        def f_xa(e, dc=dc, r=r):
                            return e.tensor_add(x_ch[r][:, dc * 512:(dc + 1) * 512],
                                                x_ch[r][:, dc * 512:(dc + 1) * 512],
                                                p_qy[:, dc * 512:(dc + 1) * 512])
                        sch.emit("vector", f_xa, waits=[("e_pe", y_done)], incs=[("e_dv", 1)])
                    done[("xadd", ch)] = sch.cur("e_dv")
                    def f_xst(e, ch=ch, r=r):
                        return e.dma_start(out=x_work.rearrange("(c p) d2 -> c p d2", p=128)[ch],
                                           in_=x_ch[r][:])
                    sch.emit("sync", f_xst, waits=[("e_dv", done[("xadd", ch)])], incs=[("d_sy", 16)])
                    done[("xst", ch)] = sch.cur("d_sy")

        init_loads()
        ffn_phase(x_in, x_work, "f1")
        phase_barrier()
        attn_pass1()
        phase_barrier()
        attn_pass2()
        phase_barrier()
        ffn_phase(x_work, out_e, "f2")
        fin = sch.cur("d_sy")
        sch.emit("sync", lambda e: None, waits=[("d_sy", fin)])

        block = E(nc.Block())

        @block.tensor
        def _(eng):
            sch.run("tensor", eng, sems)

        @block.vector
        def _(eng):
            sch.run("vector", eng, sems)

        @block.scalar
        def _(eng):
            sch.run("scalar", eng, sems)

        @block.gpsimd
        def _(eng):
            sch.run("gpsimd", eng, sems)

        @block.sync
        def _(eng):
            sch.run("sync", eng, sems)

    return nc


def _host_constants(half):
    gamma = (1.0 - np.exp2(-5.0 - np.arange(H, dtype=np.float64)))
    i = np.arange(C, dtype=np.float64)
    gq = np.empty((128, D), np.float32)
    gk = np.empty((128, D), np.float32)
    gc = np.empty((64, D), np.float32)
    for h in range(H):
        gq[:, h * 64:(h + 1) * 64] = (gamma[h] ** (i + 1 - C) / 8.0)[:, None].astype(np.float32)
        gk[:, h * 64:(h + 1) * 64] = (gamma[h] ** (C - 1 - i))[:, None].astype(np.float32)
        gc[:, h * 64:(h + 1) * 64] = np.float32(gamma[h] ** C)
    maskt = np.triu(np.ones((128, 128), np.float32))
    ident = np.eye(128, dtype=np.float32)
    inv = (1.0 / (10000.0 ** (np.arange(0, DH, 2, dtype=np.float32) / DH))).astype(np.float32)
    pos = (np.arange(TOK, dtype=np.float32) + np.float32(half * TOK))
    freqs = np.einsum('i,j->ij', pos, inv).astype(np.float32)
    cosf = np.cos(freqs).astype(np.float32)
    sinf = np.sin(freqs).astype(np.float32)
    cos_t = np.ascontiguousarray(
        cosf.reshape(NCH, 128, 32).transpose(1, 0, 2).reshape(128, NCH * 32))
    sin_t = np.ascontiguousarray(
        sinf.reshape(NCH, 128, 32).transpose(1, 0, 2).reshape(128, NCH * 32))
    cvec = np.zeros((128, 8), np.float32)
    cvec[:, 0] = EPS
    cvec[:, 1] = -0.5
    rmask = np.full((64, 1), float(half), np.float32)
    return dict(ident_in=ident, maskt_in=maskt, gq_in=gq, gk_in=gk, gc_in=gc,
                cos_in=cos_t, sin_in=sin_t, cvec_in=cvec, rmask_in=rmask)


_NC_CACHE = []


def kernel(L0_attn_x, L0_attn_wq, L0_attn_wk, L0_attn_wv, L0_attn_wg, L0_attn_wo,
           L0_ffn_Wg, L0_ffn_Ws1, L0_ffn_Ws3, L0_ffn_Ws2,
           L0_ffn_We1, L0_ffn_We3, L0_ffn_We2):
    x = np.asarray(L0_attn_x, dtype=np.float32)
    w1cat = np.ascontiguousarray(
        np.concatenate([np.asarray(L0_ffn_Ws1)[None], np.asarray(L0_ffn_We1)], axis=0), np.float32)
    w3cat = np.ascontiguousarray(
        np.concatenate([np.asarray(L0_ffn_Ws3)[None], np.asarray(L0_ffn_We3)], axis=0), np.float32)
    w2cat = np.ascontiguousarray(
        np.concatenate([np.asarray(L0_ffn_Ws2)[None], np.asarray(L0_ffn_We2)], axis=0), np.float32)
    wg_e = np.ascontiguousarray(np.asarray(L0_ffn_Wg), np.float32)
    wqT = np.ascontiguousarray(np.asarray(L0_attn_wq).T, np.float32)
    wkT = np.ascontiguousarray(np.asarray(L0_attn_wk).T, np.float32)
    wvT = np.ascontiguousarray(np.asarray(L0_attn_wv).T, np.float32)
    wgT = np.ascontiguousarray(np.asarray(L0_attn_wg).T, np.float32)
    woT = np.ascontiguousarray(np.asarray(L0_attn_wo).T, np.float32)

    magic = np.full((1, 16), np.random.randint(1, 1 << 30), np.int32)

    if not _NC_CACHE:
        _NC_CACHE.append(build_nc())
    nc = _NC_CACHE[0]

    consts = [_host_constants(h) for h in range(2)]
    in_maps = []
    for c in range(NCORE):
        b, hf = c >> 1, c & 1
        m = dict(x_in=np.ascontiguousarray(x[b, hf * TOK:(hf + 1) * TOK, :]),
                 w1cat=w1cat, w3cat=w3cat, w2cat=w2cat, wg_e=wg_e,
                 wqT=wqT, wkT=wkT, wvT=wvT, wgT=wgT, woT=woT,
                 magic_in=magic, **consts[hf])
        in_maps.append(m)

    res = run_bass_kernel_spmd(nc, in_maps, core_ids=list(range(NCORE)))

    out = np.empty((4, 8192, D), np.float32)
    for c in range(NCORE):
        b, hf = c >> 1, c & 1
        out[b, hf * TOK:(hf + 1) * TOK, :] = res.results[c]["out_e"]
    return out


# version 06
# speedup vs baseline: 1.0621x; 1.0621x over previous
import contextlib
import numpy as np
import concourse.bass as bass
import concourse.mybir as mybir
from concourse.bass_utils import run_bass_kernel_spmd

F32 = mybir.dt.float32
BF16 = mybir.dt.bfloat16
I32 = mybir.dt.int32
AF = mybir.ActivationFunctionType
ALU = mybir.AluOpType
AX = mybir.AxisListType

NCORE = 8
D = 1024
H = 16
DH = 64
FF = 2048
NUNIT = 9
TOK = 4096
G = 1024
NGRP = TOK // G
NT = G // 128
NTT = G // 512
C = 128
NCH = TOK // C
EPS = 1e-5

ENGINES = ("tensor", "vector", "scalar", "gpsimd", "sync")


class Sched:

    def __init__(self):
        self.items = {e: [] for e in ENGINES}
        self.count = {}

    def add_sem(self, name):
        self.count[name] = 0

    def emit(self, eng, fn, waits=(), incs=(), self_inc=False):
        merged = {}
        for s, v in waits:
            assert v <= self.count[s], (s, v, self.count[s])
            merged[s] = max(merged.get(s, 0), v)
        wl = tuple((s, v) for s, v in merged.items() if v > 0)
        resolved = []
        for s, d in incs:
            self.count[s] += d
            resolved.append((s, d))
        self.items[eng].append((fn, wl, () if self_inc else tuple(resolved)))

    def cur(self, name):
        return self.count[name]

    def run(self, eng_name, eng, sems):
        for fn, waits, incs in self.items[eng_name]:
            for s, v in waits:
                eng.wait_ge(sems[s], v)
            inst = fn(eng)
            for s, d in incs:
                if inst is None:
                    inst = eng.nop()
                inst.then_inc(sems[s], d)
                inst = None


def build_nc():
    nc = bass.Bass("TRN2", target_bir_lowering=False)

    x_in = nc.dram_tensor("x_in", [TOK, D], F32, kind="ExternalInput").ap()
    w1cat = nc.dram_tensor("w1cat", [NUNIT, D, FF], F32, kind="ExternalInput").ap()
    w3cat = nc.dram_tensor("w3cat", [NUNIT, D, FF], F32, kind="ExternalInput").ap()
    w2cat = nc.dram_tensor("w2cat", [NUNIT, FF, D], F32, kind="ExternalInput").ap()
    wg_e = nc.dram_tensor("wg_e", [D, 8], F32, kind="ExternalInput").ap()
    wqT = nc.dram_tensor("wqT", [D, D], F32, kind="ExternalInput").ap()
    wkT = nc.dram_tensor("wkT", [D, D], F32, kind="ExternalInput").ap()
    wvT = nc.dram_tensor("wvT", [D, D], F32, kind="ExternalInput").ap()
    wgT = nc.dram_tensor("wgT", [D, D], F32, kind="ExternalInput").ap()
    woT = nc.dram_tensor("woT", [D, D], F32, kind="ExternalInput").ap()
    ident_in = nc.dram_tensor("ident_in", [128, 128], F32, kind="ExternalInput").ap()
    maskt_in = nc.dram_tensor("maskt_in", [128, 128], F32, kind="ExternalInput").ap()
    gq_in = nc.dram_tensor("gq_in", [128, D], F32, kind="ExternalInput").ap()
    gk_in = nc.dram_tensor("gk_in", [128, D], F32, kind="ExternalInput").ap()
    gc_in = nc.dram_tensor("gc_in", [64, D], F32, kind="ExternalInput").ap()
    cos_in = nc.dram_tensor("cos_in", [128, NCH * 32], F32, kind="ExternalInput").ap()
    sin_in = nc.dram_tensor("sin_in", [128, NCH * 32], F32, kind="ExternalInput").ap()
    cvec_in = nc.dram_tensor("cvec_in", [128, 8], F32, kind="ExternalInput").ap()
    magic_in = nc.dram_tensor("magic_in", [1, 16], I32, kind="ExternalInput").ap()
    rmask_in = nc.dram_tensor("rmask_in", [64, 1], F32, kind="ExternalInput").ap()

    out_e = nc.dram_tensor("out_e", [TOK, D], F32, kind="ExternalOutput").ap()

    x_work = nc.dram_tensor("x_work", [TOK, D], F32)
    kt_sc = nc.dram_tensor("kt_sc", [NCH * 128, D], BF16)
    v_sc = nc.dram_tensor("v_sc", [NCH * 128, D], BF16)
    sg_sc = nc.dram_tensor("sg_sc", [NCH * 64, D], F32)
    xch = nc.dram_tensor("xch", [2 * 64, D], F32, addr_space="Shared")
    xchf = nc.dram_tensor("xchf", [2, 16], I32, addr_space="Shared")

    SEM_NAMES = ["d_sy", "d_sc", "d_gp", "e_pe", "e_dv", "e_ac", "e_gp", "pl", "bar"]

    es = contextlib.ExitStack()
    with es:
        E = es.enter_context

        sems = {n: E(nc.semaphore(n)) for n in SEM_NAMES}

        ident = E(nc.sbuf_tensor("ident", [128, 128], BF16))
        maskt = E(nc.sbuf_tensor("maskt", [128, 128], F32))
        cvec = E(nc.sbuf_tensor("cvec", [128, 8], F32))
        wg_sb = E(nc.sbuf_tensor("wg_sb", [128, 8 * 8], BF16))
        s_magic = E(nc.sbuf_tensor("s_magic", [1, 16], I32))
        s_flag = E(nc.sbuf_tensor("s_flag", [1, 16], I32))
        rmask = E(nc.sbuf_tensor("rmask", [64, 1], F32))

        sch = Sched()
        for n in SEM_NAMES:
            sch.add_sem(n)

        CV_EPS, CV_MHALF = 0, 1

        slot_free_pe = [0, 0, 0]
        loaded_dgp = {}
        barrier_level = [0]

        def phase_barrier():
            lv = barrier_level[0]
            need_sy = sch.cur("d_sy")
            need_sc = sch.cur("d_sc")
            need_gp = sch.cur("d_gp")
            sch.emit("sync", lambda e: e.nop(),
                     waits=[("d_sy", need_sy)], incs=[("bar", 1)])
            sch.emit("scalar", lambda e: e.nop(),
                     waits=[("d_sc", need_sc)], incs=[("bar", 1)])
            sch.emit("gpsimd", lambda e: e.nop(),
                     waits=[("d_gp", need_gp)], incs=[("bar", 1)])
            sch.emit("tensor", lambda e: e.nop(), incs=[("bar", 1)])
            sch.emit("vector", lambda e: e.nop(), incs=[("bar", 1)])
            barrier_level[0] += 5
            tgt = barrier_level[0]
            for en in ENGINES:
                sch.emit(en, lambda e: None, waits=[("bar", tgt)])

        def init_loads():
            def f(e):
                e.dma_start(out=ident[:], in_=ident_in).then_inc(sems["d_gp"], 16)
                e.dma_start(out=wg_sb.ap().rearrange("p (kc e2) -> p kc e2", kc=8),
                            in_=wg_e.rearrange("(kc p) e2 -> p kc e2", p=128)).then_inc(sems["d_gp"], 16)
                e.dma_start(out=s_magic[:], in_=magic_in).then_inc(sems["d_gp"], 16)
            sch.emit("gpsimd", f, incs=[("d_gp", 48)], self_inc=True)
            def f2(e):
                e.dma_start(out=maskt[:], in_=maskt_in).then_inc(sems["d_sy"], 16)
                e.dma_start(out=cvec[:], in_=cvec_in).then_inc(sems["d_sy"], 16)
                e.dma_start(out=rmask[:], in_=rmask_in).then_inc(sems["d_sy"], 16)
            sch.emit("sync", f2, incs=[("d_sy", 48)], self_inc=True)

        def ffn_phase(src, dst, phase_tag):
            fs = contextlib.ExitStack()
            with fs:
                FE = fs.enter_context
                wslot = [FE(nc.sbuf_tensor(f"wslot{i}_{phase_tag}", [128, 16384], BF16))
                         for i in range(3)]
                x_g = [FE(nc.sbuf_tensor(f"x_g{i}_{phase_tag}", [128, NT * D], F32)) for i in range(2)]
                xn_t = [FE(nc.sbuf_tensor(f"xn{i}_{phase_tag}", [128, D], BF16)) for i in range(2)]
                xnT = FE(nc.sbuf_tensor(f"xnT_{phase_tag}", [128, NT * D], BF16))
                hT = FE(nc.sbuf_tensor(f"hT_{phase_tag}", [128, 16 * 512], BF16))
                tsil = [FE(nc.sbuf_tensor(f"tsil{i}_{phase_tag}", [128, 512], F32)) for i in range(2)]
                dump = FE(nc.sbuf_tensor(f"dump_{phase_tag}", [128, D], F32))
                col = {n: FE(nc.sbuf_tensor(f"c_{n}_{phase_tag}", [128, 1], F32))
                       for n in ["sum", "sumsq", "m", "ex2", "m2c", "var", "rstd", "mb",
                                 "gm1", "gm2", "den", "rden"]}
                gates = FE(nc.sbuf_tensor(f"gates_{phase_tag}", [128, 8], F32))
                gtmp = FE(nc.sbuf_tensor(f"gtmp_{phase_tag}", [128, 8], F32))
                gtmp2 = FE(nc.sbuf_tensor(f"gtmp2_{phase_tag}", [128, 8], F32))
                wsc = [FE(nc.sbuf_tensor(f"wsc{i}_{phase_tag}", [128, NT * 8], F32)) for i in range(2)]
                p_tp = FE(nc.psum_tensor(f"ptp_{phase_tag}", [128, 1024], BF16))
                p_gt = FE(nc.psum_tensor(f"pgt_{phase_tag}", [128, 8], F32))
                p_h1 = [FE(nc.psum_tensor(f"ph1{i}_{phase_tag}", [128, 512], F32)) for i in range(2)]
                p_h3 = [FE(nc.psum_tensor(f"ph3{i}_{phase_tag}", [128, 512], F32)) for i in range(2)]
                p_y = [FE(nc.psum_tensor(f"py{i}_{phase_tag}", [128, 512], F32)) for i in range(2)]

                load_idx = [0]

                def sched_slot_load(g, u, s):
                    l = load_idx[0]
                    load_idx[0] += 1
                    slot = l % 3
                    pe_need = slot_free_pe[slot]
                    if s < 2:
                        def f(e, u=u, s=s, slot=slot):
                            e.dma_start(
                                out=wslot[slot].ap().rearrange("p (kc f) -> p kc f", kc=16)[:, 0:8, :],
                                in_=w1cat[u].rearrange("(kc p) f -> p kc f", p=128)[:, :, s * 1024:(s + 1) * 1024]).then_inc(sems["d_gp"], 16)
                            e.dma_start(
                                out=wslot[slot].ap().rearrange("p (kc f) -> p kc f", kc=16)[:, 8:16, :],
                                in_=w3cat[u].rearrange("(kc p) f -> p kc f", p=128)[:, :, s * 1024:(s + 1) * 1024]).then_inc(sems["d_gp"], 16)
                        sch.emit("gpsimd", f, waits=[("e_pe", pe_need)], incs=[("d_gp", 32)], self_inc=True)
                    else:
                        def f(e, u=u, slot=slot):
                            return e.dma_start(
                                out=wslot[slot].ap().rearrange("p (kc d2) -> p kc d2", kc=16),
                                in_=w2cat[u].rearrange("(kc p) d2 -> p kc d2", p=128))
                        sch.emit("gpsimd", f, waits=[("e_pe", pe_need)], incs=[("d_gp", 32)])
                    loaded_dgp[(phase_tag, l)] = sch.cur("d_gp")
                    return l

                xst_done = {}
                xnT_free = [0]
                hmm_last = [0]
                for g in range(NGRP):
                    gb = g % 2
                    xg = x_g[gb]
                    xnTg = xnT
                    wscg = wsc[gb]
                    def fload(e, g=g, xg=xg):
                        return e.dma_start(
                            out=xg.ap().rearrange("p (t d2) -> p t d2", t=NT),
                            in_=src.rearrange("(t p) d2 -> p t d2", p=128)[:, g * NT:(g + 1) * NT, :])
                    sch.emit("sync", fload, waits=[("d_sy", xst_done.get(g - 2, 0))],
                             incs=[("d_sy", 16)])
                    xg_loaded = sch.cur("d_sy")

                    lmap = {}
                    for s0 in range(3):
                        lmap[(0, s0)] = sched_slot_load(g, 0, s0)

                    sq_free = [0]
                    tp_evac_cnt = [sch.cur("e_ac")]
                    for t in range(NT):
                        xt = xg[:, t * D:(t + 1) * D]
                        def f_sum(e, xt=xt):
                            return e.reduce_sum(col["sum"][:], xt, axis=AX.X)
                        sch.emit("vector", f_sum, waits=[("d_sy", xg_loaded)], incs=[("e_dv", 1)])
                        def f_sq(e, xt=xt):
                            return e.square(dump[:], xt)
                        sch.emit("scalar", f_sq, waits=[("d_sy", xg_loaded),
                                                        ("e_dv", sq_free[0])], incs=[("e_ac", 1)])
                        sq_ac = sch.cur("e_ac")
                        def f_ssq(e):
                            return e.reduce_sum(col["sumsq"][:], dump[:], axis=AX.X)
                        sch.emit("vector", f_ssq, waits=[("e_ac", sq_ac)], incs=[("e_dv", 1)])
                        sq_done = sch.cur("e_dv")
                        sq_free[0] = sq_done
                        def f_m(e):
                            return e.tensor_scalar_mul(col["m"][:], col["sum"][:], 1.0 / D)
                        sch.emit("vector", f_m, waits=[("e_dv", sq_done)], incs=[("e_dv", 1)])
                        def f_ex2(e):
                            return e.tensor_scalar_mul(col["ex2"][:], col["sumsq"][:], 1.0 / D)
                        sch.emit("vector", f_ex2, waits=[("e_dv", sch.cur("e_dv"))], incs=[("e_dv", 1)])
                        def f_m2(e):
                            return e.tensor_scalar_mul(col["m2c"][:], col["m"][:], col["m"][:])
                        sch.emit("vector", f_m2, waits=[("e_dv", sch.cur("e_dv"))], incs=[("e_dv", 1)])
                        def f_var(e):
                            return e.tensor_sub(col["var"][:], col["ex2"][:], col["m2c"][:])
                        sch.emit("vector", f_var, waits=[("e_dv", sch.cur("e_dv"))], incs=[("e_dv", 1)])
                        var_done = sch.cur("e_dv")
                        def f_ln(e):
                            return e.activation(col["rstd"][:], col["var"][:], AF.Ln,
                                                bias=cvec[:, CV_EPS:CV_EPS + 1])
                        sch.emit("scalar", f_ln, waits=[("e_dv", var_done)], incs=[("e_ac", 1)])
                        def f_exp(e):
                            return e.activation(col["rstd"][:], col["rstd"][:], AF.Exp,
                                                scale=cvec[:, CV_MHALF:CV_MHALF + 1])
                        sch.emit("scalar", f_exp, waits=[("e_ac", sch.cur("e_ac"))], incs=[("e_ac", 1)])
                        rstd_done = sch.cur("e_ac")
                        mb_done = sch.cur("e_dv")
                        xb = xn_t[t % 2]
                        def f_xn(e, xt=xt, xb=xb):
                            return e.tensor_scalar(xb[:], xt, col["m"][:], col["rstd"][:],
                                                   ALU.subtract, ALU.mult)
                        sch.emit("vector", f_xn, waits=[("e_ac", rstd_done), ("e_dv", mb_done)],
                                 incs=[("e_dv", 1)])
                        xn_done_dv = sch.cur("e_dv")
                        def f_bridge(e):
                            return e.nop()
                        sch.emit("scalar", f_bridge, waits=[("e_dv", xn_done_dv)], incs=[("e_ac", 1)])
                        xn_done = sch.cur("e_ac")
                        for kc in range(8):
                            def f_tr(e, xb=xb, kc=kc):
                                return e.transpose(p_tp[:, kc * 128:(kc + 1) * 128],
                                                   xb[:, kc * 128:(kc + 1) * 128], ident[:])
                            w = [("e_ac", max(xn_done, tp_evac_cnt[0]))] if kc == 0 else []
                            sch.emit("tensor", f_tr, waits=w,
                                     incs=([("e_pe", 1)] if kc == 7 else ()))
                        tr_done = sch.cur("e_pe")
                        def f_ev(e, t=t, xnTg=xnTg):
                            return e.copy(bass.AP(xnTg, t * 128, [[NT * D, 128], [D, 8], [1, 128]]),
                                          p_tp[:])
                        sch.emit("scalar", f_ev,
                                 waits=[("e_pe", max(tr_done, xnT_free[0]))],
                                 incs=[("e_ac", 1)])
                        tp_evac_cnt[0] = sch.cur("e_ac")

                    xnT_ready = sch.cur("e_ac")

                    wg_loaded = 48
                    for t in range(NT):
                        sig_prev = sch.cur("e_ac")
                        for kc in range(8):
                            def f_g(e, t=t, kc=kc):
                                return e.matmul(p_gt[:], xnT[:, kc * D + t * 128: kc * D + (t + 1) * 128],
                                                wg_sb[:, kc * 8:(kc + 1) * 8],
                                                start=(kc == 0), stop=(kc == 7))
                            w = [("e_ac", xnT_ready), ("d_gp", wg_loaded)] if kc == 0 else ()
                            if kc == 0:
                                w = list(w) + [("e_dv", sch.cur("e_dv"))]
                            sch.emit("tensor", f_g, waits=w,
                                     incs=([("e_pe", 1)] if kc == 7 else ()))
                        gmm_done = sch.cur("e_pe")
                        def f_sig(e):
                            return e.activation(gates[:], p_gt[:], AF.Sigmoid)
                        sch.emit("scalar", f_sig, waits=[("e_pe", gmm_done)], incs=[("e_ac", 1)])
                        sig_done = sch.cur("e_ac")
                        chain = [
                            lambda e: e.reduce_max(col["gm1"][:], gates[:], axis=AX.X),
                            lambda e: e.tensor_scalar(gtmp[:], gates[:], col["gm1"][:], None, ALU.is_ge),
                            lambda e: e.scalar_tensor_tensor(gtmp2[:], gtmp[:], -2.0, gates[:], ALU.mult, ALU.add),
                            lambda e: e.reduce_max(col["gm2"][:], gtmp2[:], axis=AX.X),
                            lambda e: e.tensor_scalar(gtmp[:], gates[:], col["gm2"][:], None, ALU.is_ge),
                            lambda e: e.tensor_add(col["den"][:], col["gm1"][:], col["gm2"][:]),
                            lambda e: e.reciprocal(col["rden"][:], col["den"][:]),
                            lambda e: e.scalar_tensor_tensor(gtmp2[:], gates[:], col["rden"][:], gtmp[:], ALU.mult, ALU.mult),
                        ]
                        for i, cf in enumerate(chain):
                            w = [("e_ac", sig_done)] if i == 0 else [("e_dv", sch.cur("e_dv"))]
                            sch.emit("vector", cf, waits=w, incs=[("e_dv", 1)])
                        def f_wsc(e, t=t, wscg=wscg):
                            return e.tensor_scalar_mul(wscg[:, t * 8:(t + 1) * 8], gtmp2[:], 1.0 / 6.0)
                        sch.emit("vector", f_wsc, waits=[("e_dv", sch.cur("e_dv"))], incs=[("e_dv", 1)])
                    wsc_ready = sch.cur("e_dv")

                    silu_done = {}
                    hp_done = {}
                    yev_done = {}
                    for u in range(NUNIT):
                        sl13 = [lmap[(u, 0)] % 3, lmap[(u, 1)] % 3]
                        sl2 = None
                        for tt in range(NTT):
                            for fc in range(16):
                                half = fc // 8
                                l = lmap[(u, half)]
                                slot = l % 3
                                fcl = fc % 8
                                b = fc % 2
                                for which, psum, wofs in ((0, p_h1[b], 0), (1, p_h3[b], 8192)):
                                    for kc in range(8):
                                        def f_mm(e, slot=slot, wofs=wofs, kc=kc, fcl=fcl, tt=tt, psum=psum, xnTg=xnTg):
                                            return e.matmul(
                                                psum[:],
                                                wslot[slot][:, wofs + kc * 1024 + fcl * 128: wofs + kc * 1024 + (fcl + 1) * 128],
                                                xnTg[:, kc * D + tt * 512: kc * D + (tt + 1) * 512],
                                                start=(kc == 0), stop=(kc == 7))
                                        w = []
                                        if kc == 0:
                                            w.append(("d_gp", loaded_dgp[(phase_tag, l)]))
                                            if which == 0:
                                                w.append(("e_ac", silu_done.get((u, tt, fc - 2), silu_done.get((u, tt - 1, 14 + b), silu_done.get((u - 1, NTT - 1, 14 + b), 0)))))
                                            else:
                                                w.append(("e_dv", hp_done.get((u, tt, fc - 2), hp_done.get((u, tt - 1, 14 + b), hp_done.get((u - 1, NTT - 1, 14 + b), 0)))))
                                        sch.emit("tensor", f_mm, waits=w,
                                                 incs=([("e_pe", 1)] if kc == 7 else ()))
                                    if which == 0:
                                        h1_done = sch.cur("e_pe")
                                    else:
                                        h3_done = sch.cur("e_pe")
                                        hmm_last[0] = sch.cur("e_pe")
                                slot_free_pe[slot] = sch.cur("e_pe")
                                tb = tsil[fc % 2]
                                tmp_free = hp_done.get((u, tt, fc - 2), hp_done.get((u, tt - 1, 14 + (fc % 2)), hp_done.get((u - 1, NTT - 1, 14 + (fc % 2)), 0)))
                                def f_sil(e, tb=tb, psum=p_h1[b]):
                                    return e.activation(tb[:], psum[:], AF.Silu)
                                sch.emit("scalar", f_sil,
                                         waits=[("e_pe", h1_done), ("e_dv", tmp_free)],
                                         incs=[("e_ac", 1)])
                                silu_done[(u, tt, fc)] = sch.cur("e_ac")
                                def f_hp(e, tb=tb, psum=p_h3[b], fc=fc):
                                    return e.tensor_mul(hT[:, fc * 512:(fc + 1) * 512], tb[:], psum[:])
                                sch.emit("vector", f_hp,
                                         waits=[("e_ac", silu_done[(u, tt, fc)]), ("e_pe", h3_done)],
                                         incs=[("e_dv", 1)])
                                hp_done[(u, tt, fc)] = sch.cur("e_dv")
                                if tt == NTT - 1 and fc in (7, 15) and u + 1 < NUNIT:
                                    ns = 0 if fc == 7 else 1
                                    lmap[(u + 1, ns)] = sched_slot_load(g, u + 1, ns)
                            hp_all = sch.cur("e_dv")
                            l2 = lmap[(u, 2)]
                            slot2 = l2 % 3
                            for m in range(4):
                                t = tt * 4 + m
                                for dc in range(2):
                                    b = (m * 2 + dc) % 2
                                    for fc in range(16):
                                        def f_ymm(e, fc=fc, m=m, dc=dc, slot2=slot2, b=b):
                                            return e.matmul(
                                                p_y[b][:],
                                                hT[:, fc * 512 + m * 128: fc * 512 + (m + 1) * 128],
                                                wslot[slot2][:, fc * 1024 + dc * 512: fc * 1024 + (dc + 1) * 512],
                                                start=(fc == 0), stop=(fc == 15))
                                        w = []
                                        if fc == 0:
                                            w = [("e_dv", hp_all), ("d_gp", loaded_dgp[(phase_tag, l2)]),
                                                 ("e_dv", yev_done.get((u, tt, m * 2 + dc - 2),
                                                          yev_done.get((u, tt - 1, 6 + b),
                                                          yev_done.get((u - 1, NTT - 1, 6 + b), 0))))]
                                        sch.emit("tensor", f_ymm, waits=w,
                                                 incs=([("e_pe", 1)] if fc == 15 else ()))
                                    ymm_done = sch.cur("e_pe")
                                    prev_write = yev_done.get((u - 1, tt, m * 2 + dc), 0)
                                    if u == 0:
                                        def f_ye(e, t=t, dc=dc, b=b, xg=xg):
                                            return e.scalar_tensor_tensor(
                                                xg[:, t * D + dc * 512: t * D + (dc + 1) * 512],
                                                p_y[b][:], 1.0 / 6.0,
                                                xg[:, t * D + dc * 512: t * D + (dc + 1) * 512],
                                                ALU.mult, ALU.add)
                                    else:
                                        def f_ye(e, t=t, dc=dc, b=b, u=u, xg=xg, wscg=wscg):
                                            return e.scalar_tensor_tensor(
                                                xg[:, t * D + dc * 512: t * D + (dc + 1) * 512],
                                                p_y[b][:], wscg[:, t * 8 + (u - 1): t * 8 + u],
                                                xg[:, t * D + dc * 512: t * D + (dc + 1) * 512],
                                                ALU.mult, ALU.add)
                                    sch.emit("vector", f_ye,
                                             waits=[("e_pe", ymm_done), ("e_dv", max(prev_write, wsc_ready))],
                                             incs=[("e_dv", 1)])
                                    yev_done[(u, tt, m * 2 + dc)] = sch.cur("e_dv")
                        slot_free_pe[lmap[(u, 2)] % 3] = sch.cur("e_pe")
                        if u == NUNIT - 1:
                            xnT_free[0] = hmm_last[0]
                        if u + 1 < NUNIT:
                            lmap[(u + 1, 2)] = sched_slot_load(g, u + 1, 2)

                    last_ev = sch.cur("e_dv")
                    def f_store(e, g=g, xg=xg):
                        return e.dma_start(
                            out=dst.rearrange("(t p) d2 -> p t d2", p=128)[:, g * NT:(g + 1) * NT, :],
                            in_=xg.ap().rearrange("p (t d2) -> p t d2", t=NT))
                    sch.emit("sync", f_store, waits=[("e_dv", last_ev)], incs=[("d_sy", 16)])
                    xst_done[g] = sch.cur("d_sy")

        def ln_tile(col, cvx, x_ap, out_ap, xload_sem, xload_cnt, extra_wait=()):
            def f_sum(e):
                return e.reduce_sum(col["sum"][:], x_ap, axis=AX.X)
            sch.emit("vector", f_sum, waits=[(xload_sem, xload_cnt)] + list(extra_wait), incs=[("e_dv", 1)])
            def f_sq(e):
                return e.square(col["dump"][:], x_ap)
            sch.emit("scalar", f_sq, waits=[(xload_sem, xload_cnt), ("e_dv", col["sq_free"][0])],
                     incs=[("e_ac", 1)])
            sq_ac = sch.cur("e_ac")
            def f_ssq(e):
                return e.reduce_sum(col["sumsq"][:], col["dump"][:], axis=AX.X)
            sch.emit("vector", f_ssq, waits=[("e_ac", sq_ac)], incs=[("e_dv", 1)])
            sq = sch.cur("e_dv")
            col["sq_free"][0] = sq
            steps = [
                lambda e: e.tensor_scalar_mul(col["m"][:], col["sum"][:], 1.0 / D),
                lambda e: e.tensor_scalar_mul(col["ex2"][:], col["sumsq"][:], 1.0 / D),
                lambda e: e.tensor_scalar_mul(col["m2c"][:], col["m"][:], col["m"][:]),
                lambda e: e.tensor_sub(col["var"][:], col["ex2"][:], col["m2c"][:]),
            ]
            for i, s in enumerate(steps):
                sch.emit("vector", s, waits=[("e_dv", sq if i == 0 else sch.cur("e_dv"))],
                         incs=[("e_dv", 1)])
            var_done = sch.cur("e_dv")
            def f_ln(e):
                return e.activation(col["rstd"][:], col["var"][:], AF.Ln, bias=cvx[:, 0:1])
            sch.emit("scalar", f_ln, waits=[("e_dv", var_done)], incs=[("e_ac", 1)])
            def f_exp(e):
                return e.activation(col["rstd"][:], col["rstd"][:], AF.Exp, scale=cvx[:, 1:2])
            sch.emit("scalar", f_exp, waits=[("e_ac", sch.cur("e_ac"))], incs=[("e_ac", 1)])
            rstd_done = sch.cur("e_ac")
            def f_xn(e):
                return e.tensor_scalar(out_ap, x_ap, col["m"][:], col["rstd"][:],
                                       ALU.subtract, ALU.mult)
            sch.emit("vector", f_xn, waits=[("e_ac", rstd_done), ("e_dv", sch.cur("e_dv"))],
                     incs=[("e_dv", 1)])
            xn_done_dv = sch.cur("e_dv")
            def f_bridge(e):
                return e.nop()
            sch.emit("scalar", f_bridge, waits=[("e_dv", xn_done_dv)], incs=[("e_ac", 1)])
            return sch.cur("e_ac")

        def transposes_batch(src_sb, dst_sb, tp, ready_waits):
            b = tp["idx"][0] % 2
            tp["idx"][0] += 1
            pb = tp["banks"][b]
            for kc in range(8):
                def f_tr(e, kc=kc, pb=pb):
                    return e.transpose(pb[:, kc * 128:(kc + 1) * 128],
                                       src_sb[:, kc * 128:(kc + 1) * 128], ident[:])
                w = list(ready_waits) + [("e_ac", tp["free"][b])] if kc == 0 else []
                sch.emit("tensor", f_tr, waits=w,
                         incs=([("e_pe", 1)] if kc == 7 else ()))
            tr_done = sch.cur("e_pe")
            def f_ev(e, pb=pb):
                return e.copy(dst_sb[:], pb[:])
            sch.emit("scalar", f_ev, waits=[("e_pe", tr_done)], incs=[("e_ac", 1)])
            tp["free"][b] = sch.cur("e_ac")
            return sch.cur("e_ac")

        def rope4(p_src, gt_sb, t0, wbuf, out_bf, ch, cos_sb, sin_sb, psrc_done,
                  out_war=0, prev_gp=0):
            cosb = bass.AP(cos_sb, ch * 32, [[NCH * 32, 128], [0, 32], [1, 32]])
            sinb = bass.AP(sin_sb, ch * 32, [[NCH * 32, 128], [0, 16], [1, 32]])
            hap = lambda t, o: bass.AP(t, o, [[int(t.shape[1]), 128], [64, 16], [1, 32]])
            def f_t0(e):
                return e.tensor_mul(t0[:], p_src[:], gt_sb[:])
            sch.emit("vector", f_t0, waits=[("e_pe", psrc_done), ("e_gp", prev_gp)],
                     incs=[("e_dv", 1)])
            t0_done = sch.cur("e_dv")
            def f_u(e):
                return e.tensor_tensor(wbuf[:], t0[:], cosb, ALU.mult)
            sch.emit("vector", f_u, waits=[("e_dv", t0_done)], incs=[("e_gp", 1)])
            def f_w0(e):
                return e.tensor_tensor(hap(out_bf, 0), hap(t0, 32), sinb, ALU.mult)
            sch.emit("vector", f_w0, waits=[("e_dv", t0_done), ("e_pe", out_war)],
                     incs=[("e_gp", 1)])
            def f_w1(e):
                return e.tensor_tensor(hap(out_bf, 32), hap(t0, 0), sinb, ALU.mult)
            sch.emit("vector", f_w1, waits=[("e_dv", t0_done)], incs=[("e_gp", 1)])
            w_done = sch.cur("e_gp")
            def f_c0(e):
                return e.tensor_sub(hap(out_bf, 0), hap(wbuf, 0), hap(out_bf, 0))
            sch.emit("vector", f_c0, waits=[("e_gp", w_done)], incs=[("e_dv", 1)])
            def f_c1(e):
                return e.tensor_add(hap(out_bf, 32), hap(out_bf, 32), hap(wbuf, 32))
            sch.emit("vector", f_c1, waits=[("e_dv", sch.cur("e_dv"))], incs=[("e_dv", 1)])
            return sch.cur("e_dv")

        def attn_pass1():
            fs = contextlib.ExitStack()
            with fs:
                FE = fs.enter_context
                RING = 3
                wk_sb = FE(nc.sbuf_tensor("wk_sb", [128, 8 * D], BF16))
                wv_sb = FE(nc.sbuf_tensor("wv_sb", [128, 8 * D], BF16))
                gk_sb = FE(nc.sbuf_tensor("gk_sb", [128, D], F32))
                gc_sb = FE(nc.sbuf_tensor("gc_sb", [64, D], F32))
                cos_sb = FE(nc.sbuf_tensor("cos_sb1", [128, NCH * 32], F32))
                sin_sb = FE(nc.sbuf_tensor("sin_sb1", [128, NCH * 32], F32))
                x_ch = [FE(nc.sbuf_tensor(f"x_ch1_{i}", [128, D], F32)) for i in range(RING)]
                xn_ch = [FE(nc.sbuf_tensor(f"xn_ch1_{i}", [128, D], BF16)) for i in range(RING)]
                xnT_ch = [FE(nc.sbuf_tensor(f"xnT_ch1_{i}", [128, D], BF16)) for i in range(RING)]
                t0 = FE(nc.sbuf_tensor("t0_1", [128, D], F32))
                wbuf = FE(nc.sbuf_tensor("wbuf_1", [128, D], F32))
                khat = [FE(nc.sbuf_tensor(f"khat_{i}", [128, D], BF16)) for i in range(RING)]
                khT = [FE(nc.sbuf_tensor(f"khT_{i}", [128, D], BF16)) for i in range(RING)]
                vbf = [FE(nc.sbuf_tensor(f"vbf_{i}", [128, D], BF16)) for i in range(RING)]
                sgb = FE(nc.sbuf_tensor("sgb", [64, D], F32))
                sall = FE(nc.sbuf_tensor("sall", [64, D], F32))
                srecv = FE(nc.sbuf_tensor("srecv", [64, D], F32))
                tst = FE(nc.sbuf_tensor("tst", [64, D], F32))
                col = {n: FE(nc.sbuf_tensor(f"a1_{n}", [128, 1], F32))
                       for n in ["sum", "sumsq", "m", "ex2", "m2c", "var", "rstd", "mb"]}
                col["dump"] = FE(nc.sbuf_tensor("a1_dump", [128, D], F32))
                col["sq_free"] = [0]
                tp = {"banks": [FE(nc.psum_tensor(f"a1_tp{i}", [128, 1024], BF16))
                                for i in range(2)],
                      "free": [0, 0], "idx": [0]}
                p_k = FE(nc.psum_tensor("a1_pk", [128, D], F32))
                p_v = FE(nc.psum_tensor("a1_pv", [128, D], F32))
                p_s = FE(nc.psum_tensor("a1_ps", [64, D], F32))

                def f_w(e):
                    e.dma_start(out=wk_sb.ap().rearrange("p (kc d2) -> p kc d2", kc=8),
                                in_=wkT.rearrange("(kc p) d2 -> p kc d2", p=128)).then_inc(sems["d_gp"], 16)
                    e.dma_start(out=wv_sb.ap().rearrange("p (kc d2) -> p kc d2", kc=8),
                                in_=wvT.rearrange("(kc p) d2 -> p kc d2", p=128)).then_inc(sems["d_gp"], 16)
                sch.emit("gpsimd", f_w, incs=[("d_gp", 32)], self_inc=True)
                def f_c(e):
                    e.dma_start(out=gk_sb[:], in_=gk_in).then_inc(sems["d_sc"], 16)
                    e.dma_start(out=gc_sb[:], in_=gc_in).then_inc(sems["d_sc"], 16)
                    e.dma_start(out=cos_sb[:], in_=cos_in).then_inc(sems["d_sc"], 16)
                    e.dma_start(out=sin_sb[:], in_=sin_in).then_inc(sems["d_sc"], 16)
                sch.emit("scalar", f_c, incs=[("d_sc", 64)], self_inc=True)
                w_loaded = sch.cur("d_gp")
                c_loaded = sch.cur("d_sc")
                sch.emit("vector", lambda e: e.memset(sgb[:], 0.0), incs=[("e_dv", 1)])

                kv_mm_done = {}
                rope_done = {}
                rope_gp = {}
                store_done = {}
                for ch in range(NCH):
                    r = ch % RING
                    def f_x(e, ch=ch, r=r):
                        return e.dma_start(out=x_ch[r][:], in_=x_work.rearrange(
                            "(c p) d2 -> c p d2", p=128)[ch])
                    sch.emit("sync", f_x, waits=[("e_dv", rope_done.get(ch - RING, 0))],
                             incs=[("d_sy", 16)])
                    x_loaded = sch.cur("d_sy")
                    xn_done = ln_tile(col, cvec, x_ch[r][:], xn_ch[r][:], "d_sy", x_loaded)
                    xnT_done = transposes_batch(xn_ch[r], xnT_ch[r], tp, [("e_ac", xn_done)])
                    for which, psum, wsb in ((0, p_k, wk_sb), (1, p_v, wv_sb)):
                        for dc in range(2):
                            for kc in range(8):
                                def f_mm(e, psum=psum, wsb=wsb, dc=dc, kc=kc, r=r):
                                    return e.matmul(
                                        psum[:, dc * 512:(dc + 1) * 512],
                                        xnT_ch[r][:, kc * 128:(kc + 1) * 128],
                                        wsb[:, kc * D + dc * 512: kc * D + (dc + 1) * 512],
                                        start=(kc == 0), stop=(kc == 7))
                                w = []
                                if kc == 0:
                                    w = [("e_ac", xnT_done)]
                                    if ch == 0 and which == 0 and dc == 0:
                                        w.append(("d_gp", w_loaded))
                                    if which == 0:
                                        w.append(("e_dv", rope_done.get(ch - 1, 0)))
                                    else:
                                        w.append(("e_dv", store_done.get(("vev", ch - 1), 0)))
                                sch.emit("tensor", f_mm, waits=w,
                                         incs=([("e_pe", 1)] if kc == 7 else ()))
                        kv_mm_done[(ch, which)] = sch.cur("e_pe")
                    if ch == 0:
                        sch.emit("vector", lambda e: None, waits=[("d_sc", c_loaded)])
                        sch.emit("gpsimd", lambda e: None, waits=[("d_sc", c_loaded)])
                    kd = rope4(p_k, gk_sb, t0, wbuf, khat[r], ch, cos_sb, sin_sb,
                               kv_mm_done[(ch, 0)], prev_gp=rope_gp.get(ch - 1, 0))
                    rope_done[ch] = kd
                    rope_gp[ch] = sch.cur("e_gp")
                    def f_vev(e, r=r):
                        return e.tensor_copy(vbf[r][:], p_v[:])
                    sch.emit("vector", f_vev,
                             waits=[("e_pe", kv_mm_done[(ch, 1)]),
                                    ("d_sc", store_done.get(("kvst", ch - RING), 0))],
                             incs=[("e_dv", 1)])
                    store_done[("vev", ch)] = sch.cur("e_dv")
                    khT_done = transposes_batch(
                        khat[r], khT[r], tp,
                        [("e_dv", kd), ("d_sc", store_done.get(("kvst", ch - RING), 0))])
                    def f_st(e, ch=ch, r=r):
                        e.dma_start(out=kt_sc.rearrange("(c p) d2 -> c p d2", p=128)[ch],
                                    in_=khT[r][:]).then_inc(sems["d_sc"], 16)
                        e.dma_start(out=v_sc.rearrange("(c p) d2 -> c p d2", p=128)[ch],
                                    in_=vbf[r][:]).then_inc(sems["d_sc"], 16)
                    sch.emit("scalar", f_st, waits=[("e_ac", khT_done), ("e_dv", store_done[("vev", ch)])],
                             incs=[("d_sc", 32)], self_inc=True)
                    store_done[("kvst", ch)] = sch.cur("d_sc")
                    for h in range(H):
                        def f_smm(e, h=h, r=r):
                            return e.matmul(p_s[:, h * 64:(h + 1) * 64],
                                            khat[r][:, h * 64:(h + 1) * 64],
                                            vbf[r][:, h * 64:(h + 1) * 64],
                                            start=True, stop=True)
                        w = []
                        if h == 0:
                            w = [("e_dv", max(kd, store_done[("vev", ch)],
                                              store_done.get(("sup", ch - 1), 0)))]
                        sch.emit("tensor", f_smm, waits=w,
                                 incs=([("e_pe", 1)] if h == H - 1 else ()))
                    smm_done = sch.cur("e_pe")
                    def f_sup(e):
                        return e.tensor_add(sall[:], sgb[:], p_s[:])
                    sch.emit("vector", f_sup, waits=[("e_pe", smm_done), ("e_dv", sch.cur("e_dv"))],
                             incs=[("e_dv", 1)])
                    def f_sg(e):
                        return e.tensor_mul(sgb[:], sall[:], gc_sb[:])
                    sch.emit("vector", f_sg,
                             waits=[("e_dv", sch.cur("e_dv")),
                                    ("d_sc", store_done.get(("sgst", ch - 1), 0))],
                             incs=[("e_dv", 1)])
                    store_done[("sup", ch)] = sch.cur("e_dv")
                    if ch < NCH - 1:
                        def f_sgst(e, ch=ch):
                            return e.dma_start(out=sg_sc.rearrange("(c p) d2 -> c p d2", p=64)[ch],
                                               in_=sgb[:])
                        sch.emit("scalar", f_sgst, waits=[("e_dv", store_done[("sup", ch)])],
                                 incs=[("d_sc", 16)])
                        store_done[("sgst", ch)] = sch.cur("d_sc")
                sall_done = sch.cur("e_dv")

                def f_xch(e):
                    with (
                        e.register("r_h") as r_h,
                        e.register("r_soff") as r_soff,
                        e.register("r_poff") as r_poff,
                        e.register("r_foff") as r_foff,
                        e.register("r_pfoff") as r_pfoff,
                        e.register("r_magic") as r_magic,
                        e.register("r_flag") as r_flag,
                        e.register("r_cond") as r_cond,
                        e.register("r_tgt") as r_tgt,
                    ):
                        pid = e.compute_val(nc.partition_id(), donate=False)
                        e.reg_alu(r_h, pid, 1, ALU.bitwise_and)
                        e.reg_mul(r_soff, r_h, 64 * D)
                        e.reg_alu(r_poff, r_h, 1, ALU.bitwise_xor)
                        e.reg_mul(r_poff, r_poff, 64 * D)
                        e.reg_mul(r_foff, r_h, 16)
                        e.reg_alu(r_pfoff, r_h, 1, ALU.bitwise_xor)
                        e.reg_mul(r_pfoff, r_pfoff, 16)
                        e.reg_load(r_magic, s_magic[:1, :1])
                        e.dma_start(out=bass.AP(xch, r_soff, [[D, 64], [1, D]]),
                                    in_=sall[:]).then_inc(sems["pl"], 16)
                        e.reg_mov(r_tgt, 16)
                        e.wait_ge(sems["pl"], r_tgt)
                        e.dma_start(out=bass.AP(xchf, r_foff, [[16, 1], [1, 16]]),
                                    in_=s_magic[:]).then_inc(sems["pl"], 16)
                        e.reg_mov(r_tgt, 32)
                        e.wait_ge(sems["pl"], r_tgt)
                        e.reg_mov(r_cond, 1)
                        with e.While(r_cond):
                            e.dma_start(out=s_flag[:],
                                        in_=bass.AP(xchf, r_pfoff, [[16, 1], [1, 16]])
                                        ).then_inc(sems["pl"], 16)
                            e.reg_add(r_tgt, r_tgt, 16)
                            e.wait_ge(sems["pl"], r_tgt)
                            e.reg_load(r_flag, s_flag[:1, :1])
                            e.reg_sub(r_cond, r_flag, r_magic)
                        e.dma_start(out=srecv[:],
                                    in_=bass.AP(xch, r_poff, [[D, 64], [1, D]])
                                    ).then_inc(sems["pl"], 16)
                        e.reg_add(r_tgt, r_tgt, 16)
                        e.wait_ge(sems["pl"], r_tgt)
                    return e.nop()
                sch.emit("gpsimd", f_xch,
                         waits=[("e_dv", sall_done), ("d_gp", sch.cur("d_gp"))],
                         incs=[("e_gp", 1)])
                def f_t0x(e):
                    return e.scalar_tensor_tensor(tst[:], srecv[:], rmask[:], gc_sb[:],
                                                  ALU.mult, ALU.mult)
                sch.emit("vector", f_t0x, waits=[("e_gp", sch.cur("e_gp"))], incs=[("e_dv", 1)])
                def f_tst(e):
                    return e.dma_start(out=sg_sc.rearrange("(c p) d2 -> c p d2", p=64)[NCH - 1],
                                       in_=tst[:])
                sch.emit("scalar", f_tst, waits=[("e_dv", sch.cur("e_dv"))], incs=[("d_sc", 16)])

        def attn_pass2():
            fs = contextlib.ExitStack()
            with fs:
                FE = fs.enter_context
                RING = 3
                wq_sb = FE(nc.sbuf_tensor("wq_sb", [128, 8 * D], BF16))
                wgt_sb = FE(nc.sbuf_tensor("wgt_sb", [128, 8 * D], BF16))
                wo_sb = FE(nc.sbuf_tensor("wo_sb", [128, 8 * D], BF16))
                gq_sb = FE(nc.sbuf_tensor("gq_sb", [128, D], F32))
                gc_sb = FE(nc.sbuf_tensor("gc_sb2", [64, D], F32))
                cos_sb = FE(nc.sbuf_tensor("cos_sb2", [128, NCH * 32], F32))
                sin_sb = FE(nc.sbuf_tensor("sin_sb2", [128, NCH * 32], F32))
                x_ch = [FE(nc.sbuf_tensor(f"x_ch2_{i}", [128, D], F32)) for i in range(RING)]
                xn_ch = [FE(nc.sbuf_tensor(f"xn_ch2_{i}", [128, D], BF16)) for i in range(RING)]
                xnT_ch = [FE(nc.sbuf_tensor(f"xnT_ch2_{i}", [128, D], BF16)) for i in range(RING)]
                t0 = FE(nc.sbuf_tensor("t0_2", [128, D], F32))
                wbuf = FE(nc.sbuf_tensor("wbuf_2", [128, D], F32))
                qhat = FE(nc.sbuf_tensor("qhat", [128, D], BF16))
                qhT = FE(nc.sbuf_tensor("qhT", [128, D], BF16))
                g_sb = FE(nc.sbuf_tensor("g_sb", [128, D], BF16))
                og = FE(nc.sbuf_tensor("og", [128, D], BF16))
                ogT = FE(nc.sbuf_tensor("ogT", [128, D], BF16))
                kh_sb = [FE(nc.sbuf_tensor(f"kh2_{i}", [128, D], BF16)) for i in range(RING)]
                v_sb = [FE(nc.sbuf_tensor(f"v2_{i}", [128, D], BF16)) for i in range(RING)]
                adt = [FE(nc.sbuf_tensor(f"adt_{i}", [128, 128], BF16)) for i in range(2)]
                sg_r = [FE(nc.sbuf_tensor(f"sg_r{i}", [64, D], F32)) for i in range(RING)]
                tst = FE(nc.sbuf_tensor("tst2", [64, D], F32))
                rhs_s = FE(nc.sbuf_tensor("rhs_s", [128, D], BF16))
                col = {n: FE(nc.sbuf_tensor(f"a2_{n}", [128, 1], F32))
                       for n in ["sum", "sumsq", "m", "ex2", "m2c", "var", "rstd", "mb"]}
                col["dump"] = FE(nc.sbuf_tensor("a2_dump", [128, D], F32))
                col["sq_free"] = [0]
                tp = {"banks": [FE(nc.psum_tensor(f"a2_tp{i}", [128, 1024], BF16))
                                for i in range(2)],
                      "free": [0, 0], "idx": [0]}
                p_qy = FE(nc.psum_tensor("a2_qy", [128, D], F32))
                p_g = FE(nc.psum_tensor("a2_g", [128, 512], F32))
                p_at = [FE(nc.psum_tensor(f"a2_at{i}", [128, 128], F32)) for i in range(2)]
                p_o = FE(nc.psum_tensor("a2_o", [128, 512], F32))
                maskt_b = bass.AP(maskt, 0, [[128, 128], [0, 4], [1, 128]])

                def f_w(e):
                    e.dma_start(out=wq_sb.ap().rearrange("p (kc d2) -> p kc d2", kc=8),
                                in_=wqT.rearrange("(kc p) d2 -> p kc d2", p=128)).then_inc(sems["d_gp"], 16)
                    e.dma_start(out=wgt_sb.ap().rearrange("p (kc d2) -> p kc d2", kc=8),
                                in_=wgT.rearrange("(kc p) d2 -> p kc d2", p=128)).then_inc(sems["d_gp"], 16)
                    e.dma_start(out=wo_sb.ap().rearrange("p (kc d2) -> p kc d2", kc=8),
                                in_=woT.rearrange("(kc p) d2 -> p kc d2", p=128)).then_inc(sems["d_gp"], 16)
                sch.emit("gpsimd", f_w, incs=[("d_gp", 48)], self_inc=True)
                def f_c(e):
                    e.dma_start(out=gq_sb[:], in_=gq_in).then_inc(sems["d_sc"], 16)
                    e.dma_start(out=gc_sb[:], in_=gc_in).then_inc(sems["d_sc"], 16)
                    e.dma_start(out=cos_sb[:], in_=cos_in).then_inc(sems["d_sc"], 16)
                    e.dma_start(out=sin_sb[:], in_=sin_in).then_inc(sems["d_sc"], 16)
                    e.dma_start(out=tst[:],
                                in_=sg_sc.rearrange("(c p) d2 -> c p d2", p=64)[NCH - 1]).then_inc(sems["d_sc"], 16)
                sch.emit("scalar", f_c, incs=[("d_sc", 80)], self_inc=True)
                w_loaded = sch.cur("d_gp")
                c_loaded = sch.cur("d_sc")

                done = {}
                rope_gp = {}
                for ch in range(NCH):
                    r = ch % RING
                    def f_x(e, ch=ch, r=r):
                        return e.dma_start(out=x_ch[r][:], in_=x_work.rearrange(
                            "(c p) d2 -> c p d2", p=128)[ch])
                    sch.emit("sync", f_x,
                             waits=[("d_sy", sch.cur("d_sy") if ch < RING else done[("xst", ch - RING)])],
                             incs=[("d_sy", 16)])
                    x_loaded = sch.cur("d_sy")
                    def f_kv(e, ch=ch, r=r):
                        e.dma_start(out=kh_sb[r][:],
                                    in_=kt_sc.rearrange("(c p) d2 -> c p d2", p=128)[ch]).then_inc(sems["d_sc"], 16)
                        e.dma_start(out=v_sb[r][:],
                                    in_=v_sc.rearrange("(c p) d2 -> c p d2", p=128)[ch]).then_inc(sems["d_sc"], 16)
                    sch.emit("scalar", f_kv, waits=[("e_pe", done.get(("kvuse", ch - RING), 0))],
                             incs=[("d_sc", 32)], self_inc=True)
                    kv_loaded = sch.cur("d_sc")
                    if ch >= 1:
                        def f_sg(e, ch=ch, r=r):
                            return e.dma_start(out=sg_r[r][:],
                                               in_=sg_sc.rearrange("(c p) d2 -> c p d2", p=64)[ch - 1])
                        sch.emit("scalar", f_sg, waits=[("e_dv", done.get(("rhs", ch - RING), 0))],
                                 incs=[("d_sc", 16)])
                    sg_loaded = sch.cur("d_sc")

                    xn_done = ln_tile(col, cvec, x_ch[r][:], xn_ch[r][:], "d_sy", x_loaded)
                    xnT_done = transposes_batch(xn_ch[r], xnT_ch[r], tp, [("e_ac", xn_done)])
                    for dc in range(2):
                        for kc in range(8):
                            def f_mm(e, dc=dc, kc=kc, r=r):
                                return e.matmul(
                                    p_qy[:, dc * 512:(dc + 1) * 512],
                                    xnT_ch[r][:, kc * 128:(kc + 1) * 128],
                                    wq_sb[:, kc * D + dc * 512: kc * D + (dc + 1) * 512],
                                    start=(kc == 0), stop=(kc == 7))
                            w = []
                            if kc == 0:
                                w = [("e_ac", xnT_done)]
                                if ch == 0 and dc == 0:
                                    w.append(("d_gp", w_loaded))
                                    w.append(("d_sc", c_loaded))
                                w.append(("e_dv", done.get(("xadd", ch - 1), 0)))
                            sch.emit("tensor", f_mm, waits=w,
                                     incs=([("e_pe", 1)] if kc == 7 else ()))
                    done[("qmm", ch)] = sch.cur("e_pe")
                    for gh in range(2):
                        for kc in range(8):
                            def f_mm(e, gh=gh, kc=kc, r=r):
                                return e.matmul(
                                    p_g[:],
                                    xnT_ch[r][:, kc * 128:(kc + 1) * 128],
                                    wgt_sb[:, kc * D + gh * 512: kc * D + (gh + 1) * 512],
                                    start=(kc == 0), stop=(kc == 7))
                            w = []
                            if kc == 0:
                                w = [("e_ac", xnT_done),
                                     ("e_ac", done.get(("gsil", ch, gh - 1),
                                              done.get(("gsil", ch - 1, 1), 0)))]
                            sch.emit("tensor", f_mm, waits=w,
                                     incs=([("e_pe", 1)] if kc == 7 else ()))
                        gmm = sch.cur("e_pe")
                        def f_gs(e, gh=gh):
                            return e.activation(g_sb[:, gh * 512:(gh + 1) * 512], p_g[:], AF.Silu)
                        sch.emit("scalar", f_gs,
                                 waits=[("e_pe", gmm), ("e_dv", done.get(("ogev", ch - 1), 0))],
                                 incs=[("e_ac", 1)])
                        done[("gsil", ch, gh)] = sch.cur("e_ac")
                    done[("gsil", ch)] = sch.cur("e_ac")
                    qd = rope4(p_qy, gq_sb, t0, wbuf, qhat, ch, cos_sb, sin_sb,
                               done[("qmm", ch)], prev_gp=rope_gp.get(ch - 1, 0))
                    rope_gp[ch] = sch.cur("e_gp")
                    qhT_done = transposes_batch(qhat, qhT, tp, [("e_dv", qd)])
                    if ch == 0:
                        def f_rhs(e, r=r):
                            return e.tensor_scalar_mul(rhs_s[0:64, :], tst[:], 1.0)
                        sch.emit("vector", f_rhs, waits=[("d_sc", c_loaded)], incs=[("e_dv", 1)])
                    else:
                        def f_rhs(e, r=r):
                            return e.tensor_add(rhs_s[0:64, :], sg_r[r][:], tst[:])
                        sch.emit("vector", f_rhs,
                                 waits=[("d_sc", sg_loaded), ("e_dv", sch.cur("e_dv")),
                                        ("e_pe", done.get(("cross", ch - 1), 0))],
                                 incs=[("e_dv", 1)])
                    def f_rhs2(e):
                        return e.tensor_copy(rhs_s[64:128, :], rhs_s[0:64, :])
                    sch.emit("vector", f_rhs2, waits=[("e_dv", sch.cur("e_dv"))], incs=[("e_dv", 1)])
                    rhs_done = sch.cur("e_dv")
                    def f_tup(e):
                        return e.tensor_mul(tst[:], tst[:], gc_sb[:])
                    sch.emit("vector", f_tup, waits=[("e_dv", rhs_done)], incs=[("e_dv", 1)])
                    done[("rhs", ch)] = sch.cur("e_dv")

                    for hg in range(2):
                        for hh in range(8):
                            h = hg * 8 + hh
                            pb = h % 2
                            rp = (h % 2) * 64
                            kcb = h // 2
                            def f_at(e, h=h, pb=pb, rp=rp, kcb=kcb, r=r):
                                return e.matmul(p_at[pb][:],
                                                kh_sb[r][rp:rp + 64, kcb * 128:(kcb + 1) * 128],
                                                qhT[rp:rp + 64, kcb * 128:(kcb + 1) * 128],
                                                start=True, stop=True)
                            w = [("e_ac", qhT_done)] if h == 0 else []
                            if h == 0:
                                w.append(("d_sc", kv_loaded))
                            w.append(("e_dv", done.get(("adt", ch, h - 2),
                                              done.get(("adt", ch - 1, h + 14), 0))))
                            sch.emit("tensor", f_at, waits=w, incs=[("e_pe", 1)])
                            at_done = sch.cur("e_pe")
                            def f_adt(e, pb=pb, h=h):
                                return e.tensor_mul(adt[h % 2][:], p_at[pb][:], maskt[:])
                            adt_war = done.get(("omm", ch, h - 2),
                                               done.get(("omm", ch - 1, h + 14), 0))
                            sch.emit("vector", f_adt,
                                     waits=[("e_pe", max(at_done, adt_war))],
                                     incs=[("e_dv", 1)])
                            done[("adt", ch, h)] = sch.cur("e_dv")
                            def f_oi(e, h=h, hh=hh, r=r):
                                return e.matmul(p_o[:, hh * 64:(hh + 1) * 64],
                                                adt[h % 2][:],
                                                v_sb[r][:, h * 64:(h + 1) * 64],
                                                start=True, stop=False)
                            w = [("e_dv", done[("adt", ch, h)])]
                            if hh == 0:
                                w.append(("e_dv", done.get(("ogev_hg", ch, hg - 1),
                                                           done.get(("ogev_hg", ch - 1, 1), 0))))
                            if h == 0:
                                w.append(("e_dv", done[("rhs", ch)]))
                            sch.emit("tensor", f_oi, waits=w)
                            def f_oc(e, hh=hh, h=h, rp=rp, kcb=kcb):
                                return e.matmul(p_o[:, hh * 64:(hh + 1) * 64],
                                                qhT[rp:rp + 64, kcb * 128:(kcb + 1) * 128],
                                                rhs_s[rp:rp + 64, h * 64:(h + 1) * 64],
                                                start=False, stop=True)
                            sch.emit("tensor", f_oc, incs=[("e_pe", 1)])
                            done[("omm", ch, h)] = sch.cur("e_pe")
                        def f_og(e, hg=hg):
                            return e.tensor_mul(og[:, hg * 512:(hg + 1) * 512],
                                                p_o[:], g_sb[:, hg * 512:(hg + 1) * 512])
                        sch.emit("vector", f_og, waits=[("e_pe", done[("omm", ch, hg * 8 + 7)]),
                                                        ("e_ac", done[("gsil", ch)])],
                                 incs=[("e_dv", 1)])
                        done[("ogev_hg", ch, hg)] = sch.cur("e_dv")
                    done[("ogev", ch)] = sch.cur("e_dv")
                    done[("kvuse", ch)] = sch.cur("e_pe")
                    done[("cross", ch)] = sch.cur("e_pe")
                    ogT_done = transposes_batch(og, ogT, tp, [("e_dv", done[("ogev", ch)])])
                    for dc in range(2):
                        for kc in range(8):
                            def f_y(e, dc=dc, kc=kc):
                                return e.matmul(p_qy[:, dc * 512:(dc + 1) * 512],
                                                ogT[:, kc * 128:(kc + 1) * 128],
                                                wo_sb[:, kc * D + dc * 512: kc * D + (dc + 1) * 512],
                                                start=(kc == 0), stop=(kc == 7))
                            w = [("e_ac", ogT_done)] if kc == 0 else []
                            sch.emit("tensor", f_y, waits=w,
                                     incs=([("e_pe", 1)] if kc == 7 else ()))
                        y_done = sch.cur("e_pe")
                        def f_xa(e, dc=dc, r=r):
                            return e.tensor_add(x_ch[r][:, dc * 512:(dc + 1) * 512],
                                                x_ch[r][:, dc * 512:(dc + 1) * 512],
                                                p_qy[:, dc * 512:(dc + 1) * 512])
                        sch.emit("vector", f_xa, waits=[("e_pe", y_done)], incs=[("e_dv", 1)])
                    done[("xadd", ch)] = sch.cur("e_dv")
                    def f_xst(e, ch=ch, r=r):
                        return e.dma_start(out=x_work.rearrange("(c p) d2 -> c p d2", p=128)[ch],
                                           in_=x_ch[r][:])
                    sch.emit("sync", f_xst, waits=[("e_dv", done[("xadd", ch)])], incs=[("d_sy", 16)])
                    done[("xst", ch)] = sch.cur("d_sy")

        init_loads()
        ffn_phase(x_in, x_work, "f1")
        phase_barrier()
        attn_pass1()
        phase_barrier()
        attn_pass2()
        phase_barrier()
        ffn_phase(x_work, out_e, "f2")
        fin = sch.cur("d_sy")
        sch.emit("sync", lambda e: None, waits=[("d_sy", fin)])

        block = E(nc.Block())

        @block.tensor
        def _(eng):
            sch.run("tensor", eng, sems)

        @block.vector
        def _(eng):
            sch.run("vector", eng, sems)

        @block.scalar
        def _(eng):
            sch.run("scalar", eng, sems)

        @block.gpsimd
        def _(eng):
            sch.run("gpsimd", eng, sems)

        @block.sync
        def _(eng):
            sch.run("sync", eng, sems)

    return nc


def _host_constants(half):
    gamma = (1.0 - np.exp2(-5.0 - np.arange(H, dtype=np.float64)))
    i = np.arange(C, dtype=np.float64)
    gq = np.empty((128, D), np.float32)
    gk = np.empty((128, D), np.float32)
    gc = np.empty((64, D), np.float32)
    for h in range(H):
        gq[:, h * 64:(h + 1) * 64] = (gamma[h] ** (i + 1 - C) / 8.0)[:, None].astype(np.float32)
        gk[:, h * 64:(h + 1) * 64] = (gamma[h] ** (C - 1 - i))[:, None].astype(np.float32)
        gc[:, h * 64:(h + 1) * 64] = np.float32(gamma[h] ** C)
    maskt = np.triu(np.ones((128, 128), np.float32))
    ident = np.eye(128, dtype=np.float32)
    inv = (1.0 / (10000.0 ** (np.arange(0, DH, 2, dtype=np.float32) / DH))).astype(np.float32)
    pos = (np.arange(TOK, dtype=np.float32) + np.float32(half * TOK))
    freqs = np.einsum('i,j->ij', pos, inv).astype(np.float32)
    cosf = np.cos(freqs).astype(np.float32)
    sinf = np.sin(freqs).astype(np.float32)
    cos_t = np.ascontiguousarray(
        cosf.reshape(NCH, 128, 32).transpose(1, 0, 2).reshape(128, NCH * 32))
    sin_t = np.ascontiguousarray(
        sinf.reshape(NCH, 128, 32).transpose(1, 0, 2).reshape(128, NCH * 32))
    cvec = np.zeros((128, 8), np.float32)
    cvec[:, 0] = EPS
    cvec[:, 1] = -0.5
    rmask = np.full((64, 1), float(half), np.float32)
    return dict(ident_in=ident, maskt_in=maskt, gq_in=gq, gk_in=gk, gc_in=gc,
                cos_in=cos_t, sin_in=sin_t, cvec_in=cvec, rmask_in=rmask)


_NC_CACHE = []


def kernel(L0_attn_x, L0_attn_wq, L0_attn_wk, L0_attn_wv, L0_attn_wg, L0_attn_wo,
           L0_ffn_Wg, L0_ffn_Ws1, L0_ffn_Ws3, L0_ffn_Ws2,
           L0_ffn_We1, L0_ffn_We3, L0_ffn_We2):
    x = np.asarray(L0_attn_x, dtype=np.float32)
    w1cat = np.ascontiguousarray(
        np.concatenate([np.asarray(L0_ffn_Ws1)[None], np.asarray(L0_ffn_We1)], axis=0), np.float32)
    w3cat = np.ascontiguousarray(
        np.concatenate([np.asarray(L0_ffn_Ws3)[None], np.asarray(L0_ffn_We3)], axis=0), np.float32)
    w2cat = np.ascontiguousarray(
        np.concatenate([np.asarray(L0_ffn_Ws2)[None], np.asarray(L0_ffn_We2)], axis=0), np.float32)
    wg_e = np.ascontiguousarray(np.asarray(L0_ffn_Wg), np.float32)
    wqT = np.ascontiguousarray(np.asarray(L0_attn_wq).T, np.float32)
    wkT = np.ascontiguousarray(np.asarray(L0_attn_wk).T, np.float32)
    wvT = np.ascontiguousarray(np.asarray(L0_attn_wv).T, np.float32)
    wgT = np.ascontiguousarray(np.asarray(L0_attn_wg).T, np.float32)
    woT = np.ascontiguousarray(np.asarray(L0_attn_wo).T, np.float32)

    magic = np.full((1, 16), np.random.randint(1, 1 << 30), np.int32)

    if not _NC_CACHE:
        _NC_CACHE.append(build_nc())
    nc = _NC_CACHE[0]

    consts = [_host_constants(h) for h in range(2)]
    in_maps = []
    for c in range(NCORE):
        b, hf = c >> 1, c & 1
        m = dict(x_in=np.ascontiguousarray(x[b, hf * TOK:(hf + 1) * TOK, :]),
                 w1cat=w1cat, w3cat=w3cat, w2cat=w2cat, wg_e=wg_e,
                 wqT=wqT, wkT=wkT, wvT=wvT, wgT=wgT, woT=woT,
                 magic_in=magic, **consts[hf])
        in_maps.append(m)

    res = run_bass_kernel_spmd(nc, in_maps, core_ids=list(range(NCORE)))

    out = np.empty((4, 8192, D), np.float32)
    for c in range(NCORE):
        b, hf = c >> 1, c & 1
        out[b, hf * TOK:(hf + 1) * TOK, :] = res.results[c]["out_e"]
    return out


# version 08
# speedup vs baseline: 1.3659x; 1.2860x over previous
import contextlib
import numpy as np
import concourse.bass as bass
import concourse.mybir as mybir
from concourse.bass_utils import run_bass_kernel_spmd

F32 = mybir.dt.float32
BF16 = mybir.dt.bfloat16
I32 = mybir.dt.int32
AF = mybir.ActivationFunctionType
ALU = mybir.AluOpType
AX = mybir.AxisListType

NCORE = 8
STAGE = 3
NOSCAT = False
NOGATH = False
FAKELOOP = False
NOLOOP = False
CSTUB = 0
CKMAX = 10 ** 9
DBG_DUMP = False
SIM_FFN_ONLY = False
D = 1024
H = 16
DH = 64
FF = 2048
NUNIT = 9
TOK = 4096
G = 1024
NGRP = TOK // G
NT = G // 128
NTT = G // 512
C = 128
CAP = 1536
BIG = 1 << 20
NCH = TOK // C
EPS = 1e-5

ENGINES = ("tensor", "vector", "scalar", "gpsimd", "sync")


class Sched:

    def __init__(self):
        self.items = {e: [] for e in ENGINES}
        self.count = {}
        self.fw = []

    def add_sem(self, name):
        self.count[name] = 0

    def finalize(self):
        for s, v in self.fw:
            assert v <= self.count[s], ("fwait", s, v, self.count[s])

    def emit(self, eng, fn, waits=(), incs=(), self_inc=False, fwaits=()):
        merged = {}
        for s, v in waits:
            assert v <= self.count[s], (s, v, self.count[s])
            merged[s] = max(merged.get(s, 0), v)
        for s, v in fwaits:
            self.fw.append((s, v))
            merged[s] = max(merged.get(s, 0), v)
        wl = tuple((s, v) for s, v in merged.items() if v > 0)
        resolved = []
        for s, d in incs:
            self.count[s] += d
            resolved.append((s, d))
        self.items[eng].append((fn, wl, () if self_inc else tuple(resolved)))

    def cur(self, name):
        return self.count[name]

    def run(self, eng_name, eng, sems):
        for fn, waits, incs in self.items[eng_name]:
            for s, v in waits:
                eng.wait_ge(sems[s], v)
            inst = fn(eng)
            for s, d in incs:
                if inst is None:
                    inst = eng.nop()
                inst.then_inc(sems[s], d)
                inst = None


def build_nc():
    nc = bass.Bass("TRN2", target_bir_lowering=False)

    x_in = nc.dram_tensor("x_in", [TOK, D], F32, kind="ExternalInput").ap()
    w1cat = nc.dram_tensor("w1cat", [NUNIT, D, FF], F32, kind="ExternalInput").ap()
    w3cat = nc.dram_tensor("w3cat", [NUNIT, D, FF], F32, kind="ExternalInput").ap()
    w2cat = nc.dram_tensor("w2cat", [NUNIT, FF, D], F32, kind="ExternalInput").ap()
    wg_e = nc.dram_tensor("wg_e", [D, 8], F32, kind="ExternalInput").ap()
    wqT = nc.dram_tensor("wqT", [D, D], F32, kind="ExternalInput").ap()
    wkT = nc.dram_tensor("wkT", [D, D], F32, kind="ExternalInput").ap()
    wvT = nc.dram_tensor("wvT", [D, D], F32, kind="ExternalInput").ap()
    wgT = nc.dram_tensor("wgT", [D, D], F32, kind="ExternalInput").ap()
    woT = nc.dram_tensor("woT", [D, D], F32, kind="ExternalInput").ap()
    ident_in = nc.dram_tensor("ident_in", [128, 128], F32, kind="ExternalInput").ap()
    maskt_in = nc.dram_tensor("maskt_in", [128, 128], F32, kind="ExternalInput").ap()
    gq_in = nc.dram_tensor("gq_in", [128, D], F32, kind="ExternalInput").ap()
    gk_in = nc.dram_tensor("gk_in", [128, D], F32, kind="ExternalInput").ap()
    gc_in = nc.dram_tensor("gc_in", [64, D], F32, kind="ExternalInput").ap()
    cos_in = nc.dram_tensor("cos_in", [128, NCH * 32], F32, kind="ExternalInput").ap()
    sin_in = nc.dram_tensor("sin_in", [128, NCH * 32], F32, kind="ExternalInput").ap()
    cvec_in = nc.dram_tensor("cvec_in", [128, 8], F32, kind="ExternalInput").ap()
    magic_in = nc.dram_tensor("magic_in", [1, 16], I32, kind="ExternalInput").ap()
    rmask_in = nc.dram_tensor("rmask_in", [64, 1], F32, kind="ExternalInput").ap()

    out_e = nc.dram_tensor("out_e", [TOK, D], F32, kind="ExternalOutput").ap()
    if DBG_DUMP:
        dbg_idx = nc.dram_tensor("dbg_idx", [8 * CAP, 8], I32, kind="ExternalOutput").ap()
        dbg_yab = nc.dram_tensor("dbg_yab", [2 * TOK, D], BF16, kind="ExternalOutput").ap()

    iota_in = nc.dram_tensor("iota_in", [128, 1], F32, kind="ExternalInput").ap()
    ecap_in = nc.dram_tensor("ecap_in", [128, 8], F32, kind="ExternalInput").ap()
    ident32_in = nc.dram_tensor("ident32_in", [8, 8], F32, kind="ExternalInput").ap()

    x_work = nc.dram_tensor("x_work", [TOK, D], F32)
    xn_sc = nc.dram_tensor("xn_sc", [TOK, D], BF16)
    idx_sc = nc.dram_tensor("idx_sc", [8 * CAP, 8], I32)
    yab_sc = nc.dram_tensor("yab_sc", [2 * TOK, D], BF16)
    dbg_sc = nc.dram_tensor("dbg_sc", [128, 16], I32)
    w1b_sc = nc.dram_tensor("w1b_sc", [8, D, FF], BF16)
    w3b_sc = nc.dram_tensor("w3b_sc", [8, D, FF], BF16)
    w2b_sc = nc.dram_tensor("w2b_sc", [8, FF, D], BF16)
    kt_sc = nc.dram_tensor("kt_sc", [NCH * 128, D], BF16)
    v_sc = nc.dram_tensor("v_sc", [NCH * 128, D], BF16)
    sg_sc = nc.dram_tensor("sg_sc", [NCH * 64, D], F32)
    xch = nc.dram_tensor("xch", [2 * 64, D], F32, addr_space="Shared")
    xchf = nc.dram_tensor("xchf", [2, 16], I32, addr_space="Shared")

    SEM_NAMES = ["d_sy", "d_sc", "d_gp", "e_pe", "e_dv", "e_ac", "e_gp", "pl", "bar"]

    es = contextlib.ExitStack()
    with es:
        E = es.enter_context

        sems = {n: E(nc.semaphore(n)) for n in SEM_NAMES}

        ident = E(nc.sbuf_tensor("ident", [128, 128], BF16))
        maskt = E(nc.sbuf_tensor("maskt", [128, 128], F32))
        cvec = E(nc.sbuf_tensor("cvec", [128, 8], F32))
        wg_sb = E(nc.sbuf_tensor("wg_sb", [128, 8 * 8], BF16))
        s_magic = E(nc.sbuf_tensor("s_magic", [1, 16], I32))
        s_flag = E(nc.sbuf_tensor("s_flag", [1, 16], I32))
        rmask = E(nc.sbuf_tensor("rmask", [64, 1], F32))

        sch = Sched()
        for n in SEM_NAMES:
            sch.add_sem(n)

        CV_EPS, CV_MHALF = 0, 1

        slot_free_pe = [0, 0, 0]
        loaded_dgp = {}
        barrier_level = [0]

        def phase_barrier():
            lv = barrier_level[0]
            need_sy = sch.cur("d_sy")
            need_sc = sch.cur("d_sc")
            need_gp = sch.cur("d_gp")
            sch.emit("sync", lambda e: e.nop(),
                     waits=[("d_sy", need_sy)], incs=[("bar", 1)])
            sch.emit("scalar", lambda e: e.nop(),
                     waits=[("d_sc", need_sc)], incs=[("bar", 1)])
            sch.emit("gpsimd", lambda e: e.nop(),
                     waits=[("d_gp", need_gp)], incs=[("bar", 1)])
            sch.emit("tensor", lambda e: e.nop(), incs=[("bar", 1)])
            sch.emit("vector", lambda e: e.nop(), incs=[("bar", 1)])
            barrier_level[0] += 5
            tgt = barrier_level[0]
            for en in ENGINES:
                sch.emit(en, lambda e: None, waits=[("bar", tgt)])

        def init_loads():
            def f(e):
                e.dma_start(out=ident[:], in_=ident_in).then_inc(sems["d_gp"], 16)
                e.dma_start(out=wg_sb.ap().rearrange("p (kc e2) -> p kc e2", kc=8),
                            in_=wg_e.rearrange("(kc p) e2 -> p kc e2", p=128)).then_inc(sems["d_gp"], 16)
                e.dma_start(out=s_magic[:], in_=magic_in).then_inc(sems["d_gp"], 16)
            sch.emit("gpsimd", f, incs=[("d_gp", 48)], self_inc=True)
            def f2(e):
                e.dma_start(out=maskt[:], in_=maskt_in).then_inc(sems["d_sy"], 16)
                e.dma_start(out=cvec[:], in_=cvec_in).then_inc(sems["d_sy"], 16)
                e.dma_start(out=rmask[:], in_=rmask_in).then_inc(sems["d_sy"], 16)
            sch.emit("sync", f2, incs=[("d_sy", 48)], self_inc=True)

        def ffn_phase(src, dst, phase_tag):
            fs = contextlib.ExitStack()
            with fs:
                FE = fs.enter_context
                wslot = [FE(nc.sbuf_tensor(f"wslot{i}_{phase_tag}", [128, 16384], BF16))
                         for i in range(3)]
                x_g = [FE(nc.sbuf_tensor(f"x_g{i}_{phase_tag}", [128, NT * D], F32)) for i in range(2)]
                xn_one = FE(nc.sbuf_tensor(f"xn_{phase_tag}", [128, D], BF16))
                xn_t = [xn_one, xn_one]
                xnT = FE(nc.sbuf_tensor(f"xnT_{phase_tag}", [128, NT * D], BF16))
                hT = FE(nc.sbuf_tensor(f"hT_{phase_tag}", [128, 16 * 512], BF16))
                tsil = [FE(nc.sbuf_tensor(f"tsil{i}_{phase_tag}", [128, 512], F32)) for i in range(2)]
                dump = FE(nc.sbuf_tensor(f"dump_{phase_tag}", [128, D], F32))
                col = {n: FE(nc.sbuf_tensor(f"c_{n}_{phase_tag}", [128, 1], F32))
                       for n in ["sum", "sumsq", "m", "ex2", "m2c", "var", "rstd", "mb",
                                 "gm1", "gm2", "den", "rden"]}
                gates = FE(nc.sbuf_tensor(f"gates_{phase_tag}", [128, 8], F32))
                gtmp = FE(nc.sbuf_tensor(f"gtmp_{phase_tag}", [128, 8], F32))
                gtmp2 = FE(nc.sbuf_tensor(f"gtmp2_{phase_tag}", [128, 8], F32))
                wsc = [FE(nc.sbuf_tensor(f"wsc{i}_{phase_tag}", [128, NT * 8], F32)) for i in range(2)]
                p_tp = FE(nc.psum_tensor(f"ptp_{phase_tag}", [128, 1024], BF16))
                p_gt = FE(nc.psum_tensor(f"pgt_{phase_tag}", [128, 8], F32))
                p_h1 = [FE(nc.psum_tensor(f"ph1{i}_{phase_tag}", [128, 512], F32)) for i in range(2)]
                p_h3 = [FE(nc.psum_tensor(f"ph3{i}_{phase_tag}", [128, 512], F32)) for i in range(2)]
                p_y = [FE(nc.psum_tensor(f"py{i}_{phase_tag}", [128, 512], F32)) for i in range(2)]

                load_idx = [0]

                def sched_slot_load(g, u, s):
                    l = load_idx[0]
                    load_idx[0] += 1
                    slot = l % 3
                    pe_need = slot_free_pe[slot]
                    if s < 2:
                        def f(e, u=u, s=s, slot=slot):
                            e.dma_start(
                                out=wslot[slot].ap().rearrange("p (kc f) -> p kc f", kc=16)[:, 0:8, :],
                                in_=w1cat[u].rearrange("(kc p) f -> p kc f", p=128)[:, :, s * 1024:(s + 1) * 1024]).then_inc(sems["d_gp"], 16)
                            e.dma_start(
                                out=wslot[slot].ap().rearrange("p (kc f) -> p kc f", kc=16)[:, 8:16, :],
                                in_=w3cat[u].rearrange("(kc p) f -> p kc f", p=128)[:, :, s * 1024:(s + 1) * 1024]).then_inc(sems["d_gp"], 16)
                        sch.emit("gpsimd", f, waits=[("e_pe", pe_need)], incs=[("d_gp", 32)], self_inc=True)
                    else:
                        def f(e, u=u, slot=slot):
                            return e.dma_start(
                                out=wslot[slot].ap().rearrange("p (kc d2) -> p kc d2", kc=16),
                                in_=w2cat[u].rearrange("(kc p) d2 -> p kc d2", p=128))
                        sch.emit("gpsimd", f, waits=[("e_pe", pe_need)], incs=[("d_gp", 32)])
                    loaded_dgp[(phase_tag, l)] = sch.cur("d_gp")
                    return l

                xst_done = {}
                xnT_free = [0]
                hmm_last = [0]
                for g in range(NGRP):
                    gb = g % 2
                    xg = x_g[gb]
                    xnTg = xnT
                    wscg = wsc[gb]
                    def fload(e, g=g, xg=xg):
                        return e.dma_start(
                            out=xg.ap().rearrange("p (t d2) -> p t d2", t=NT),
                            in_=src.rearrange("(t p) d2 -> p t d2", p=128)[:, g * NT:(g + 1) * NT, :])
                    sch.emit("sync", fload, waits=[("d_sy", xst_done.get(g - 2, 0))],
                             incs=[("d_sy", 16)])
                    xg_loaded = sch.cur("d_sy")

                    lmap = {}
                    for s0 in range(3):
                        lmap[(0, s0)] = sched_slot_load(g, 0, s0)

                    sq_free = [0]
                    tp_evac_cnt = [sch.cur("e_ac")]
                    for t in range(NT):
                        xt = xg[:, t * D:(t + 1) * D]
                        def f_sum(e, xt=xt):
                            return e.reduce_sum(col["sum"][:], xt, axis=AX.X)
                        sch.emit("vector", f_sum, waits=[("d_sy", xg_loaded)], incs=[("e_dv", 1)])
                        def f_sq(e, xt=xt):
                            return e.square(dump[:], xt)
                        sch.emit("scalar", f_sq, waits=[("d_sy", xg_loaded),
                                                        ("e_dv", sq_free[0])], incs=[("e_ac", 1)])
                        sq_ac = sch.cur("e_ac")
                        def f_ssq(e):
                            return e.reduce_sum(col["sumsq"][:], dump[:], axis=AX.X)
                        sch.emit("vector", f_ssq, waits=[("e_ac", sq_ac)], incs=[("e_dv", 1)])
                        sq_done = sch.cur("e_dv")
                        sq_free[0] = sq_done
                        def f_m(e):
                            return e.tensor_scalar_mul(col["m"][:], col["sum"][:], 1.0 / D)
                        sch.emit("vector", f_m, waits=[("e_dv", sq_done)], incs=[("e_dv", 1)])
                        def f_ex2(e):
                            return e.tensor_scalar_mul(col["ex2"][:], col["sumsq"][:], 1.0 / D)
                        sch.emit("vector", f_ex2, waits=[("e_dv", sch.cur("e_dv"))], incs=[("e_dv", 1)])
                        def f_m2(e):
                            return e.tensor_scalar_mul(col["m2c"][:], col["m"][:], col["m"][:])
                        sch.emit("vector", f_m2, waits=[("e_dv", sch.cur("e_dv"))], incs=[("e_dv", 1)])
                        def f_var(e):
                            return e.tensor_sub(col["var"][:], col["ex2"][:], col["m2c"][:])
                        sch.emit("vector", f_var, waits=[("e_dv", sch.cur("e_dv"))], incs=[("e_dv", 1)])
                        var_done = sch.cur("e_dv")
                        def f_ln(e):
                            return e.activation(col["rstd"][:], col["var"][:], AF.Ln,
                                                bias=cvec[:, CV_EPS:CV_EPS + 1])
                        sch.emit("scalar", f_ln, waits=[("e_dv", var_done)], incs=[("e_ac", 1)])
                        def f_exp(e):
                            return e.activation(col["rstd"][:], col["rstd"][:], AF.Exp,
                                                scale=cvec[:, CV_MHALF:CV_MHALF + 1])
                        sch.emit("scalar", f_exp, waits=[("e_ac", sch.cur("e_ac"))], incs=[("e_ac", 1)])
                        rstd_done = sch.cur("e_ac")
                        mb_done = sch.cur("e_dv")
                        xb = xn_t[t % 2]
                        def f_xn(e, xt=xt, xb=xb):
                            return e.tensor_scalar(xb[:], xt, col["m"][:], col["rstd"][:],
                                                   ALU.subtract, ALU.mult)
                        sch.emit("vector", f_xn, waits=[("e_ac", rstd_done), ("e_dv", mb_done)],
                                 incs=[("e_dv", 1)])
                        xn_done_dv = sch.cur("e_dv")
                        def f_bridge(e):
                            return e.nop()
                        sch.emit("scalar", f_bridge, waits=[("e_dv", xn_done_dv)], incs=[("e_ac", 1)])
                        xn_done = sch.cur("e_ac")
                        for kc in range(8):
                            def f_tr(e, xb=xb, kc=kc):
                                return e.transpose(p_tp[:, kc * 128:(kc + 1) * 128],
                                                   xb[:, kc * 128:(kc + 1) * 128], ident[:])
                            w = [("e_ac", max(xn_done, tp_evac_cnt[0]))] if kc == 0 else []
                            sch.emit("tensor", f_tr, waits=w,
                                     incs=([("e_pe", 1)] if kc == 7 else ()))
                        tr_done = sch.cur("e_pe")
                        def f_ev(e, t=t, xnTg=xnTg):
                            return e.copy(bass.AP(xnTg, t * 128, [[NT * D, 128], [D, 8], [1, 128]]),
                                          p_tp[:])
                        sch.emit("scalar", f_ev,
                                 waits=[("e_pe", max(tr_done, xnT_free[0]))],
                                 incs=[("e_ac", 1)])
                        tp_evac_cnt[0] = sch.cur("e_ac")

                    xnT_ready = sch.cur("e_ac")

                    wg_loaded = 48
                    for t in range(NT):
                        sig_prev = sch.cur("e_ac")
                        for kc in range(8):
                            def f_g(e, t=t, kc=kc):
                                return e.matmul(p_gt[:], xnT[:, kc * D + t * 128: kc * D + (t + 1) * 128],
                                                wg_sb[:, kc * 8:(kc + 1) * 8],
                                                start=(kc == 0), stop=(kc == 7))
                            w = [("e_ac", xnT_ready), ("d_gp", wg_loaded)] if kc == 0 else ()
                            if kc == 0:
                                w = list(w) + [("e_dv", sch.cur("e_dv"))]
                            sch.emit("tensor", f_g, waits=w,
                                     incs=([("e_pe", 1)] if kc == 7 else ()))
                        gmm_done = sch.cur("e_pe")
                        def f_sig(e):
                            return e.activation(gates[:], p_gt[:], AF.Sigmoid)
                        sch.emit("scalar", f_sig, waits=[("e_pe", gmm_done)], incs=[("e_ac", 1)])
                        sig_done = sch.cur("e_ac")
                        chain = [
                            lambda e: e.reduce_max(col["gm1"][:], gates[:], axis=AX.X),
                            lambda e: e.tensor_scalar(gtmp[:], gates[:], col["gm1"][:], None, ALU.is_ge),
                            lambda e: e.scalar_tensor_tensor(gtmp2[:], gtmp[:], -2.0, gates[:], ALU.mult, ALU.add),
                            lambda e: e.reduce_max(col["gm2"][:], gtmp2[:], axis=AX.X),
                            lambda e: e.tensor_scalar(gtmp[:], gates[:], col["gm2"][:], None, ALU.is_ge),
                            lambda e: e.tensor_add(col["den"][:], col["gm1"][:], col["gm2"][:]),
                            lambda e: e.reciprocal(col["rden"][:], col["den"][:]),
                            lambda e: e.scalar_tensor_tensor(gtmp2[:], gates[:], col["rden"][:], gtmp[:], ALU.mult, ALU.mult),
                        ]
                        for i, cf in enumerate(chain):
                            w = [("e_ac", sig_done)] if i == 0 else [("e_dv", sch.cur("e_dv"))]
                            sch.emit("vector", cf, waits=w, incs=[("e_dv", 1)])
                        def f_wsc(e, t=t, wscg=wscg):
                            return e.tensor_scalar_mul(wscg[:, t * 8:(t + 1) * 8], gtmp2[:], 1.0 / 6.0)
                        sch.emit("vector", f_wsc, waits=[("e_dv", sch.cur("e_dv"))], incs=[("e_dv", 1)])
                    wsc_ready = sch.cur("e_dv")

                    silu_done = {}
                    hp_done = {}
                    yev_done = {}
                    for u in range(NUNIT):
                        sl13 = [lmap[(u, 0)] % 3, lmap[(u, 1)] % 3]
                        sl2 = None
                        for tt in range(NTT):
                            for fc in range(16):
                                half = fc // 8
                                l = lmap[(u, half)]
                                slot = l % 3
                                fcl = fc % 8
                                b = fc % 2
                                for which, psum, wofs in ((0, p_h1[b], 0), (1, p_h3[b], 8192)):
                                    for kc in range(8):
                                        def f_mm(e, slot=slot, wofs=wofs, kc=kc, fcl=fcl, tt=tt, psum=psum, xnTg=xnTg):
                                            return e.matmul(
                                                psum[:],
                                                wslot[slot][:, wofs + kc * 1024 + fcl * 128: wofs + kc * 1024 + (fcl + 1) * 128],
                                                xnTg[:, kc * D + tt * 512: kc * D + (tt + 1) * 512],
                                                start=(kc == 0), stop=(kc == 7))
                                        w = []
                                        if kc == 0:
                                            w.append(("d_gp", loaded_dgp[(phase_tag, l)]))
                                            if which == 0:
                                                w.append(("e_ac", silu_done.get((u, tt, fc - 2), silu_done.get((u, tt - 1, 14 + b), silu_done.get((u - 1, NTT - 1, 14 + b), 0)))))
                                            else:
                                                w.append(("e_dv", hp_done.get((u, tt, fc - 2), hp_done.get((u, tt - 1, 14 + b), hp_done.get((u - 1, NTT - 1, 14 + b), 0)))))
                                        sch.emit("tensor", f_mm, waits=w,
                                                 incs=([("e_pe", 1)] if kc == 7 else ()))
                                    if which == 0:
                                        h1_done = sch.cur("e_pe")
                                    else:
                                        h3_done = sch.cur("e_pe")
                                        hmm_last[0] = sch.cur("e_pe")
                                slot_free_pe[slot] = sch.cur("e_pe")
                                tb = tsil[fc % 2]
                                tmp_free = hp_done.get((u, tt, fc - 2), hp_done.get((u, tt - 1, 14 + (fc % 2)), hp_done.get((u - 1, NTT - 1, 14 + (fc % 2)), 0)))
                                def f_sil(e, tb=tb, psum=p_h1[b]):
                                    return e.activation(tb[:], psum[:], AF.Silu)
                                sch.emit("scalar", f_sil,
                                         waits=[("e_pe", h1_done), ("e_dv", tmp_free)],
                                         incs=[("e_ac", 1)])
                                silu_done[(u, tt, fc)] = sch.cur("e_ac")
                                def f_hp(e, tb=tb, psum=p_h3[b], fc=fc):
                                    return e.tensor_mul(hT[:, fc * 512:(fc + 1) * 512], tb[:], psum[:])
                                sch.emit("vector", f_hp,
                                         waits=[("e_ac", silu_done[(u, tt, fc)]), ("e_pe", h3_done)],
                                         incs=[("e_dv", 1)])
                                hp_done[(u, tt, fc)] = sch.cur("e_dv")
                                if tt == NTT - 1 and fc in (7, 15) and u + 1 < NUNIT:
                                    ns = 0 if fc == 7 else 1
                                    lmap[(u + 1, ns)] = sched_slot_load(g, u + 1, ns)
                            hp_all = sch.cur("e_dv")
                            l2 = lmap[(u, 2)]
                            slot2 = l2 % 3
                            for m in range(4):
                                t = tt * 4 + m
                                for dc in range(2):
                                    b = (m * 2 + dc) % 2
                                    for fc in range(16):
                                        def f_ymm(e, fc=fc, m=m, dc=dc, slot2=slot2, b=b):
                                            return e.matmul(
                                                p_y[b][:],
                                                hT[:, fc * 512 + m * 128: fc * 512 + (m + 1) * 128],
                                                wslot[slot2][:, fc * 1024 + dc * 512: fc * 1024 + (dc + 1) * 512],
                                                start=(fc == 0), stop=(fc == 15))
                                        w = []
                                        if fc == 0:
                                            w = [("e_dv", hp_all), ("d_gp", loaded_dgp[(phase_tag, l2)]),
                                                 ("e_dv", yev_done.get((u, tt, m * 2 + dc - 2),
                                                          yev_done.get((u, tt - 1, 6 + b),
                                                          yev_done.get((u - 1, NTT - 1, 6 + b), 0))))]
                                        sch.emit("tensor", f_ymm, waits=w,
                                                 incs=([("e_pe", 1)] if fc == 15 else ()))
                                    ymm_done = sch.cur("e_pe")
                                    prev_write = yev_done.get((u - 1, tt, m * 2 + dc), 0)
                                    if u == 0:
                                        def f_ye(e, t=t, dc=dc, b=b, xg=xg):
                                            return e.scalar_tensor_tensor(
                                                xg[:, t * D + dc * 512: t * D + (dc + 1) * 512],
                                                p_y[b][:], 1.0 / 6.0,
                                                xg[:, t * D + dc * 512: t * D + (dc + 1) * 512],
                                                ALU.mult, ALU.add)
                                    else:
                                        def f_ye(e, t=t, dc=dc, b=b, u=u, xg=xg, wscg=wscg):
                                            return e.scalar_tensor_tensor(
                                                xg[:, t * D + dc * 512: t * D + (dc + 1) * 512],
                                                p_y[b][:], wscg[:, t * 8 + (u - 1): t * 8 + u],
                                                xg[:, t * D + dc * 512: t * D + (dc + 1) * 512],
                                                ALU.mult, ALU.add)
                                    sch.emit("vector", f_ye,
                                             waits=[("e_pe", ymm_done), ("e_dv", max(prev_write, wsc_ready))],
                                             incs=[("e_dv", 1)])
                                    yev_done[(u, tt, m * 2 + dc)] = sch.cur("e_dv")
                        slot_free_pe[lmap[(u, 2)] % 3] = sch.cur("e_pe")
                        if u == NUNIT - 1:
                            xnT_free[0] = hmm_last[0]
                        if u + 1 < NUNIT:
                            lmap[(u + 1, 2)] = sched_slot_load(g, u + 1, 2)

                    last_ev = sch.cur("e_dv")
                    def f_store(e, g=g, xg=xg):
                        return e.dma_start(
                            out=dst.rearrange("(t p) d2 -> p t d2", p=128)[:, g * NT:(g + 1) * NT, :],
                            in_=xg.ap().rearrange("p (t d2) -> p t d2", t=NT))
                    sch.emit("sync", f_store, waits=[("e_dv", last_ev)], incs=[("d_sy", 16)])
                    xst_done[g] = sch.cur("d_sy")

        def ffn_sparse(src_t, dst_t, phase_tag):
            fs = contextlib.ExitStack()
            with fs:
                FE = fs.enter_context
                wslot = [FE(nc.sbuf_tensor(f"wslot{i}_{phase_tag}", [128, 16384], BF16))
                         for i in range(3)]
                x_g = FE(nc.sbuf_tensor(f"x_g_{phase_tag}", [128, NT * D], F32))
                xn_one = FE(nc.sbuf_tensor(f"xn_{phase_tag}", [128, D], BF16))
                xn_t = [xn_one, xn_one]
                xnT = FE(nc.sbuf_tensor(f"xnT_{phase_tag}", [128, NT * D], BF16))
                hT = FE(nc.sbuf_tensor(f"hT_{phase_tag}", [128, 16 * 512], BF16))
                tsil = [FE(nc.sbuf_tensor(f"tsil{i}_{phase_tag}", [128, 512], BF16)) for i in range(2)]
                col = {n: FE(nc.sbuf_tensor(f"c_{n}_{phase_tag}", [128, 1], F32))
                       for n in ["sum", "sumsq", "m", "ex2", "m2c", "var", "rstd",
                                 "gm1", "gm2", "den", "rden", "pos1", "ovf", "offf"]}
                col["tkf"] = col["pos1"]
                col["af"] = col["ovf"]
                col["bf"] = col["offf"]
                col["t2f"] = col["gm1"]
                col["oaf"] = col["gm2"]
                coli = {n: FE(nc.sbuf_tensor(f"ci_{n}_{phase_tag}", [128, 1], I32))
                        for n in ["off1", "off2"]}
                gates = FE(nc.sbuf_tensor(f"gates_{phase_tag}", [128, 8], F32))
                gtmp = FE(nc.sbuf_tensor(f"gtmp_{phase_tag}", [128, 8], F32))
                gtmp2 = FE(nc.sbuf_tensor(f"gtmp2_{phase_tag}", [128, 8], F32))
                ptile = FE(nc.sbuf_tensor(f"ptile_{phase_tag}", [128, 8], F32))
                pecap = gates
                ecap_sb = FE(nc.sbuf_tensor(f"ecap_{phase_tag}", [128, 8], F32))
                id32_sb = FE(nc.sbuf_tensor(f"id32_{phase_tag}", [8, 8], F32))
                m1all = FE(nc.sbuf_tensor(f"m1all_{phase_tag}", [128, 32 * 8], BF16))
                t2all = FE(nc.sbuf_tensor(f"t2all_{phase_tag}", [128, 32 * 8], BF16))
                wab = FE(nc.sbuf_tensor(f"wab_{phase_tag}", [128, 32 * 2], F32))
                maskT = FE(nc.sbuf_tensor(f"maskT_{phase_tag}", [8, TOK], BF16))
                posT = FE(nc.sbuf_tensor(f"posT_{phase_tag}", [8, TOK], F32))
                onesc = FE(nc.sbuf_tensor(f"onesc_{phase_tag}", [8, 4], BF16))
                oobt = FE(nc.sbuf_tensor(f"oobt_{phase_tag}", [128, 96], I32))
                idxt = [FE(nc.sbuf_tensor(f"idxt{i}_{phase_tag}", [128, 32], I32)) for i in range(2)]
                xg_sb = [FE(nc.sbuf_tensor(f"xg{i}_{phase_tag}", [128, D], BF16)) for i in range(2)]
                y_sb = FE(nc.sbuf_tensor(f"ysb_{phase_tag}", [128, 4 * D], BF16))
                yones = FE(nc.sbuf_tensor(f"yones_{phase_tag}", [128, 8], BF16))
                idxst = FE(nc.sbuf_tensor(f"idxst_{phase_tag}", [128, 4], I32))
                oau = FE(nc.sbuf_tensor(f"oau_{phase_tag}", [128, 4], I32))
                payA = FE(nc.sbuf_tensor(f"payA_{phase_tag}", [128, 8], I32))
                payB = FE(nc.sbuf_tensor(f"payB_{phase_tag}", [128, 8], I32))
                iotac = FE(nc.sbuf_tensor(f"iotac_{phase_tag}", [128, 1], F32))
                dump = FE(nc.sbuf_tensor(f"dump_{phase_tag}", [128, D], F32))
                p_tp = FE(nc.psum_tensor(f"ptp_{phase_tag}", [128, 1024], BF16))
                p_y = [FE(nc.psum_tensor(f"py{i}_{phase_tag}", [128, 512], F32)) for i in range(2)]
                p_gt = FE(nc.psum_tensor(f"pgt_{phase_tag}", [128, 8], F32))
                p_h1 = [FE(nc.psum_tensor(f"ph1{i}_{phase_tag}", [128, 512], F32)) for i in range(2)]
                p_h3 = [FE(nc.psum_tensor(f"ph3{i}_{phase_tag}", [128, 512], F32)) for i in range(2)]

                load_idx = [0]

                def sched_slot_load(u, s):
                    l = load_idx[0]
                    load_idx[0] += 1
                    slot = l % 3
                    pe_need = slot_free_pe[slot]
                    if s < 2:
                        def f(e, u=u, s=s, slot=slot):
                            e.dma_start(
                                out=wslot[slot].ap().rearrange("p (kc f) -> p kc f", kc=16)[:, 0:8, :],
                                in_=w1cat[u].rearrange("(kc p) f -> p kc f", p=128)[:, :, s * 1024:(s + 1) * 1024]).then_inc(sems["d_gp"], 16)
                            e.dma_start(
                                out=wslot[slot].ap().rearrange("p (kc f) -> p kc f", kc=16)[:, 8:16, :],
                                in_=w3cat[u].rearrange("(kc p) f -> p kc f", p=128)[:, :, s * 1024:(s + 1) * 1024]).then_inc(sems["d_gp"], 16)
                        sch.emit("gpsimd", f, waits=[("e_pe", pe_need)], incs=[("d_gp", 32)], self_inc=True)
                    else:
                        def f(e, u=u, slot=slot):
                            return e.dma_start(
                                out=wslot[slot].ap().rearrange("p (kc d2) -> p kc d2", kc=16),
                                in_=w2cat[u].rearrange("(kc p) d2 -> p kc d2", p=128))
                        sch.emit("gpsimd", f, waits=[("e_pe", pe_need)], incs=[("d_gp", 32)])
                    loaded_dgp[(phase_tag, l)] = sch.cur("d_gp")
                    return l

                def f_cl(e):
                    e.dma_start(out=ecap_sb[:], in_=ecap_in).then_inc(sems["d_sy"], 16)
                    e.dma_start(out=id32_sb[:], in_=ident32_in).then_inc(sems["d_sy"], 16)
                    e.dma_start(out=iotac[:], in_=iota_in).then_inc(sems["d_sy"], 16)
                sch.emit("sync", f_cl, incs=[("d_sy", 48)], self_inc=True)
                sch.emit("vector", lambda e: e.memset(onesc[:], 1.0), incs=[("e_dv", 1)])
                sch.emit("vector", lambda e: e.memset(oobt[:], BIG), incs=[("e_dv", 1)])
                sch.emit("vector",
                         lambda e: e.memset(bass.AP(oobt, 1, [[96, 128], [8, 12], [1, 1]]), 0),
                         waits=[("e_dv", sch.cur("e_dv"))], incs=[("e_dv", 1)])
                consts_dsy = sch.cur("d_sy")

                pre0 = sch.cur("e_dv")
                NBLK = 8 * CAP // 128
                for a in range(8):
                    def f_pre(e, a=a):
                        return e.dma_start(
                            out=idx_sc.rearrange("(a p) w -> p a w", p=128)[:, a * (NBLK // 8):(a + 1) * (NBLK // 8), :],
                            in_=bass.AP(oobt, 0, [[96, 128], [8, NBLK // 8], [1, 8]]))
                    sch.emit("sync", f_pre, waits=[("e_dv", pre0)], incs=[("d_sy", 16)])
                prefill_dsy = sch.cur("d_sy")

                for s0 in range(3):
                    sched_slot_load(0, s0)

                cast_done = {}
                cast_fin_dsy = [0]
                if phase_tag == "f1":
                    cst = {}
                    ci = [0]
                    def cast_chunk(src_ap, dst_ap, half):
                        c = ci[0]
                        ci[0] += 1
                        bounce = y_sb[:, half * 2048:(half + 1) * 2048]
                        def f_cl2(e, src_ap=src_ap, bounce=bounce):
                            return e.dma_start(out=bounce, in_=src_ap)
                        sch.emit("gpsimd", f_cl2, waits=[("d_sy", cst.get(c - 2, 0))],
                                 incs=[("d_gp", 16)])
                        ld_done = sch.cur("d_gp")
                        def f_st2(e, dst_ap=dst_ap, bounce=bounce):
                            return e.dma_start(out=dst_ap, in_=bounce)
                        sch.emit("sync", f_st2, waits=[("d_gp", ld_done)], incs=[("d_sy", 16)])
                        cst[c] = sch.cur("d_sy")
                    for ei in range(8):
                        for c in range(8):
                            cast_chunk(
                                w1cat[ei + 1].rearrange("(kc p) f -> p kc f", p=128)[:, c, :],
                                w1b_sc[ei].rearrange("(kc p) f -> p kc f", p=128)[:, c, :],
                                ci[0] % 2)
                        for c in range(8):
                            cast_chunk(
                                w3cat[ei + 1].rearrange("(kc p) f -> p kc f", p=128)[:, c, :],
                                w3b_sc[ei].rearrange("(kc p) f -> p kc f", p=128)[:, c, :],
                                ci[0] % 2)
                        cast_done[(ei, 0)] = cast_done[(ei, 1)] = sch.cur("d_sy")
                        for c in range(8):
                            cast_chunk(
                                w2cat[ei + 1].rearrange("(kc p) d2 -> p kc d2", p=128)[:, 2 * c:2 * c + 2, :],
                                w2b_sc[ei].rearrange("(kc p) d2 -> p kc d2", p=128)[:, 2 * c:2 * c + 2, :],
                                ci[0] % 2)
                        cast_done[(ei, 2)] = sch.cur("d_sy")
                    cast_fin_dsy[0] = sch.cur("d_sy")

                sq_free = [0]
                last_tr_pe = [0]
                tp_i = {"banks": [p_tp], "free": [0], "idx": [0]}
                silu_done = {}
                hp_done = {}
                yev_done = {}
                xnsc_st = {}
                mtr_done = [0]
                hmm_last = [0]
                for g in range(NGRP):
                    def fload(e, g=g):
                        return e.dma_start(
                            out=x_g.ap().rearrange("p (t d2) -> p t d2", t=NT),
                            in_=src_t.rearrange("(t p) d2 -> p t d2", p=128)[:, g * NT:(g + 1) * NT, :])
                    sch.emit("sync", fload, waits=[("d_sy", sch.cur("d_sy")), ("e_dv", sch.cur("e_dv"))],
                             incs=[("d_sy", 16)])
                    xg_loaded = sch.cur("d_sy")

                    for t in range(NT):
                        gt = g * NT + t
                        xt = x_g[:, t * D:(t + 1) * D]
                        def f_sum(e, xt=xt):
                            return e.reduce_sum(col["sum"][:], xt, axis=AX.X)
                        sch.emit("vector", f_sum, waits=[("d_sy", xg_loaded)], incs=[("e_dv", 1)])
                        def f_sq(e, xt=xt):
                            return e.square(dump[:], xt)
                        sch.emit("scalar", f_sq, waits=[("d_sy", xg_loaded), ("e_dv", sq_free[0])],
                                 incs=[("e_ac", 1)])
                        sq_ac = sch.cur("e_ac")
                        def f_ssq(e):
                            return e.reduce_sum(col["sumsq"][:], dump[:], axis=AX.X)
                        sch.emit("vector", f_ssq, waits=[("e_ac", sq_ac)], incs=[("e_dv", 1)])
                        sq_done = sch.cur("e_dv")
                        sq_free[0] = sq_done
                        steps = [
                            lambda e: e.tensor_scalar_mul(col["m"][:], col["sum"][:], 1.0 / D),
                            lambda e: e.tensor_scalar_mul(col["ex2"][:], col["sumsq"][:], 1.0 / D),
                            lambda e: e.tensor_scalar_mul(col["m2c"][:], col["m"][:], col["m"][:]),
                            lambda e: e.tensor_sub(col["var"][:], col["ex2"][:], col["m2c"][:]),
                        ]
                        for i_, s_ in enumerate(steps):
                            sch.emit("vector", s_, waits=[("e_dv", sq_done if i_ == 0 else sch.cur("e_dv"))],
                                     incs=[("e_dv", 1)])
                        def f_ln(e):
                            return e.activation(col["rstd"][:], col["var"][:], AF.Ln,
                                                bias=cvec[:, CV_EPS:CV_EPS + 1])
                        sch.emit("scalar", f_ln, waits=[("e_dv", sch.cur("e_dv"))], incs=[("e_ac", 1)])
                        def f_exp(e):
                            return e.activation(col["rstd"][:], col["rstd"][:], AF.Exp,
                                                scale=cvec[:, CV_MHALF:CV_MHALF + 1])
                        sch.emit("scalar", f_exp, waits=[("e_ac", sch.cur("e_ac"))], incs=[("e_ac", 1)])
                        rstd_done = sch.cur("e_ac")
                        xb = xn_t[t % 2]
                        def f_xn(e, xt=xt, xb=xb):
                            return e.tensor_scalar(xb[:], xt, col["m"][:], col["rstd"][:],
                                                   ALU.subtract, ALU.mult)
                        sch.emit("vector", f_xn,
                                 waits=[("e_ac", rstd_done), ("e_dv", sch.cur("e_dv")),
                                        ("d_sy", xnsc_st.get(gt - 1, 0)),
                                        ("e_pe", last_tr_pe[0])],
                                 incs=[("e_dv", 1)])
                        xn_done_dv = sch.cur("e_dv")
                        def f_bridge(e):
                            return e.nop()
                        sch.emit("scalar", f_bridge, waits=[("e_dv", xn_done_dv)], incs=[("e_ac", 1)])
                        xn_done = sch.cur("e_ac")
                        def f_xns(e, gt=gt, xb=xb):
                            return e.dma_start(
                                out=xn_sc.rearrange("(a p) d2 -> a p d2", p=128)[gt],
                                in_=xb[:])
                        sch.emit("sync", f_xns, waits=[("e_dv", xn_done_dv)], incs=[("d_sy", 16)])
                        xnsc_st[gt] = sch.cur("d_sy")
                        for kc in range(8):
                            def f_tr(e, xb=xb, kc=kc):
                                return e.transpose(p_tp[:, kc * 128:(kc + 1) * 128],
                                                   xb[:, kc * 128:(kc + 1) * 128], ident[:])
                            w = [("e_ac", max(xn_done, tp_i["free"][0]))] if kc == 0 else []
                            sch.emit("tensor", f_tr, waits=w,
                                     incs=([("e_pe", 1)] if kc == 7 else ()))
                        tr_done = sch.cur("e_pe")
                        last_tr_pe[0] = tr_done
                        def f_ev(e, t=t):
                            return e.copy(bass.AP(xnT, t * 128, [[NT * D, 128], [D, 8], [1, 128]]),
                                          p_tp[:])
                        sch.emit("scalar", f_ev, waits=[("e_pe", max(tr_done, hmm_last[0]))],
                                 incs=[("e_ac", 1)])
                        tp_i["free"][0] = sch.cur("e_ac")

                    xnT_ready = sch.cur("e_ac")

                    for t in range(NT):
                        gt = g * NT + t
                        for kc in range(8):
                            def f_g(e, t=t, kc=kc):
                                return e.matmul(p_gt[:], xnT[:, kc * D + t * 128: kc * D + (t + 1) * 128],
                                                wg_sb[:, kc * 8:(kc + 1) * 8],
                                                start=(kc == 0), stop=(kc == 7))
                            w = []
                            if kc == 0:
                                w = [("e_ac", xnT_ready), ("d_gp", 48),
                                     ("e_dv", sch.cur("e_dv"))]
                            sch.emit("tensor", f_g, waits=w,
                                     incs=([("e_pe", 1)] if kc == 7 else ()))
                        gmm_done = sch.cur("e_pe")
                        def f_sig(e):
                            return e.activation(gates[:], p_gt[:], AF.Sigmoid)
                        sch.emit("scalar", f_sig, waits=[("e_pe", gmm_done)], incs=[("e_ac", 1)])
                        sig_done = sch.cur("e_ac")
                        chain = [
                            lambda e: e.reduce_max(col["gm1"][:], gates[:], axis=AX.X),
                            lambda e, gt=gt: e.tensor_scalar(m1all[:, gt * 8:(gt + 1) * 8], gates[:],
                                                             col["gm1"][:], None, ALU.is_ge),
                            lambda e, gt=gt: e.scalar_tensor_tensor(gtmp2[:], m1all[:, gt * 8:(gt + 1) * 8],
                                                                    -2.0, gates[:], ALU.mult, ALU.add),
                            lambda e: e.reduce_max(col["gm2"][:], gtmp2[:], axis=AX.X),
                            lambda e, gt=gt: e.tensor_scalar(t2all[:, gt * 8:(gt + 1) * 8], gates[:],
                                                             col["gm2"][:], None, ALU.is_ge),
                            lambda e: e.tensor_add(col["den"][:], col["gm1"][:], col["gm2"][:]),
                            lambda e: e.reciprocal(col["rden"][:], col["den"][:]),
                            lambda e, gt=gt: e.tensor_scalar(wab[:, gt * 2: gt * 2 + 1], col["gm1"][:],
                                                             col["rden"][:], 1.0 / 6.0, ALU.mult, ALU.mult),
                            lambda e, gt=gt: e.tensor_scalar(wab[:, gt * 2 + 1: gt * 2 + 2], col["gm2"][:],
                                                             col["rden"][:], 1.0 / 6.0, ALU.mult, ALU.mult),
                        ]
                        for i_, cf in enumerate(chain):
                            w = [("e_ac", sig_done)] if i_ == 0 else [("e_dv", sch.cur("e_dv"))]
                            sch.emit("vector", cf, waits=w, incs=[("e_dv", 1)])
                        t2_done = sch.cur("e_dv")
                        def f_mtr(e, gt=gt, t=t):
                            return e.transpose(p_tp[0:8, (t % 8) * 128:(t % 8 + 1) * 128],
                                               t2all[:, gt * 8:(gt + 1) * 8], ident[:])
                        w = [("e_dv", t2_done)]
                        if t % 8 == 0:
                            w.append(("e_ac", max(mtr_done[0], tp_i["free"][0])))
                        sch.emit("tensor", f_mtr, waits=w, incs=[("e_pe", 1)])
                        if t % 8 == 7:
                            mtrp = sch.cur("e_pe")
                            def f_mev(e, g=g):
                                return e.copy(maskT[:, g * 1024:(g + 1) * 1024], p_tp[0:8, :])
                            sch.emit("scalar", f_mev, waits=[("e_pe", mtrp)], incs=[("e_ac", 1)])
                            mtr_done[0] = sch.cur("e_ac")
                            tp_i["free"][0] = max(tp_i["free"][0], mtr_done[0])

                    u = 0
                    for tt in range(NTT):
                        for fc in range(16):
                            half = fc // 8
                            slot = half
                            fcl = fc % 8
                            b = fc % 2
                            for which, psum, wofs in ((0, p_h1[b], 0), (1, p_h3[b], 8192)):
                                for kc in range(8):
                                    def f_mm(e, slot=slot, wofs=wofs, kc=kc, fcl=fcl, tt=tt, psum=psum):
                                        return e.matmul(
                                            psum[:],
                                            wslot[slot][:, wofs + kc * 1024 + fcl * 128: wofs + kc * 1024 + (fcl + 1) * 128],
                                            xnT[:, kc * D + tt * 512: kc * D + (tt + 1) * 512],
                                            start=(kc == 0), stop=(kc == 7))
                                    w = []
                                    if kc == 0:
                                        w.append(("d_gp", loaded_dgp[(phase_tag, slot)]))
                                        if which == 0:
                                            w.append(("e_ac", silu_done.get((g, tt, fc - 2), silu_done.get((g, tt - 1, 14 + b), silu_done.get((g - 1, NTT - 1, 14 + b), 0)))))
                                        else:
                                            w.append(("e_dv", hp_done.get((g, tt, fc - 2), hp_done.get((g, tt - 1, 14 + b), hp_done.get((g - 1, NTT - 1, 14 + b), 0)))))
                                    sch.emit("tensor", f_mm, waits=w,
                                             incs=([("e_pe", 1)] if kc == 7 else ()))
                                if which == 0:
                                    h1_done = sch.cur("e_pe")
                                else:
                                    h3_done = sch.cur("e_pe")
                                    hmm_last[0] = sch.cur("e_pe")
                            tb = tsil[fc % 2]
                            tmp_free = hp_done.get((g, tt, fc - 2), hp_done.get((g, tt - 1, 14 + (fc % 2)), hp_done.get((g - 1, NTT - 1, 14 + (fc % 2)), 0)))
                            def f_sil(e, tb=tb, psum=p_h1[b]):
                                return e.activation(tb[:], psum[:], AF.Silu)
                            sch.emit("scalar", f_sil,
                                     waits=[("e_pe", h1_done), ("e_dv", tmp_free)],
                                     incs=[("e_ac", 1)])
                            silu_done[(g, tt, fc)] = sch.cur("e_ac")
                            def f_hp(e, tb=tb, psum=p_h3[b], fc=fc):
                                return e.tensor_mul(hT[:, fc * 512:(fc + 1) * 512], tb[:], psum[:])
                            sch.emit("vector", f_hp,
                                     waits=[("e_ac", silu_done[(g, tt, fc)]), ("e_pe", h3_done)],
                                     incs=[("e_dv", 1)])
                            hp_done[(g, tt, fc)] = sch.cur("e_dv")
                        hp_all = sch.cur("e_dv")
                        for m in range(4):
                            t = tt * 4 + m
                            for dc in range(2):
                                b = (m * 2 + dc) % 2
                                for fc in range(16):
                                    def f_ymm(e, fc=fc, m=m, dc=dc, b=b):
                                        return e.matmul(
                                            p_y[b][:],
                                            hT[:, fc * 512 + m * 128: fc * 512 + (m + 1) * 128],
                                            wslot[2][:, fc * 1024 + dc * 512: fc * 1024 + (dc + 1) * 512],
                                            start=(fc == 0), stop=(fc == 15))
                                    w = []
                                    if fc == 0:
                                        w = [("e_dv", hp_all), ("d_gp", loaded_dgp[(phase_tag, 2)]),
                                             ("e_dv", yev_done.get((g, tt, m * 2 + dc - 2),
                                                      yev_done.get((g, tt - 1, 6 + b),
                                                      yev_done.get((g - 1, NTT - 1, 6 + b), 0))))]
                                    sch.emit("tensor", f_ymm, waits=w,
                                             incs=([("e_pe", 1)] if fc == 15 else ()))
                                ymm_done = sch.cur("e_pe")
                                def f_ye(e, t=t, dc=dc, b=b):
                                    return e.scalar_tensor_tensor(
                                        x_g[:, t * D + dc * 512: t * D + (dc + 1) * 512],
                                        p_y[b][:], 1.0 / 6.0,
                                        x_g[:, t * D + dc * 512: t * D + (dc + 1) * 512],
                                        ALU.mult, ALU.add)
                                sch.emit("vector", f_ye,
                                         waits=[("e_pe", ymm_done)],
                                         incs=[("e_dv", 1)])
                                yev_done[(g, tt, m * 2 + dc)] = sch.cur("e_dv")
                    last_ev = sch.cur("e_dv")
                    def f_store(e, g=g):
                        return e.dma_start(
                            out=x_work.rearrange("(t p) d2 -> p t d2", p=128)[:, g * NT:(g + 1) * NT, :],
                            in_=x_g.ap().rearrange("p (t d2) -> p t d2", t=NT))
                    sch.emit("sync", f_store, waits=[("e_dv", last_ev)], incs=[("d_sy", 16)])

                phaseA_pe = sch.cur("e_pe")
                slot_free_pe[0] = slot_free_pe[1] = slot_free_pe[2] = phaseA_pe
                xnsc_all = sch.cur("d_sy")

                if STAGE < 1:
                    return
                def f_scan(e):
                    return e.tensor_tensor_scan(posT[:], maskT[:],
                                                bass.AP(onesc, 0, [[4, 8], [0, TOK]]),
                                                0.0, ALU.add, ALU.mult)
                sch.emit("vector", f_scan,
                         waits=[("e_ac", mtr_done[0]), ("e_dv", sch.cur("e_dv"))],
                         incs=[("e_dv", 1)])
                def f_excl(e):
                    return e.tensor_sub(posT[:], posT[:], maskT[:])
                sch.emit("vector", f_excl, waits=[("e_dv", sch.cur("e_dv"))], incs=[("e_dv", 1)])
                sch.emit("vector", lambda e: e.memset(payA[:], 0), incs=[("e_dv", 1)])
                sch.emit("vector", lambda e: e.memset(payB[:], 0),
                         waits=[("e_dv", sch.cur("e_dv"))], incs=[("e_dv", 1)])
                sch.emit("vector", lambda e: e.memset(payA[:, 1:2], 1),
                         waits=[("e_dv", sch.cur("e_dv"))], incs=[("e_dv", 1)])
                scan_done = sch.cur("e_dv")

                P_B = 20
                NTL = TOK // 128
                B0 = sch.cur("e_dv")
                LG0 = sch.cur("d_gp")
                pre_dsy = prefill_dsy
                def f_bloop(e):
                    e.wait_ge(sems["d_sy"], pre_dsy)
                    with e.register(f"r_w_{id(f_bloop)}") as r_w:
                        e.reg_mov(r_w, B0 + P_B)
                        with e.Fori(0, NTL):
                            e.wait_ge(sems["e_dv"], r_w)
                            e.indirect_dma_start(
                                idx_sc.ap(), bass.IndirectOffsetOnAxis(coli["off1"][:], 0),
                                payA[:], None,
                                bounds_check=8 * CAP - 1, oob_is_err=False).then_inc(sems["d_gp"], 16)
                            e.indirect_dma_start(
                                idx_sc.ap(), bass.IndirectOffsetOnAxis(coli["off2"][:], 0),
                                payB[:], None,
                                bounds_check=8 * CAP - 1, oob_is_err=False).then_inc(sems["d_gp"], 16)
                            e.reg_add(r_w, r_w, P_B)
                    e.nop().then_inc(sems["e_gp"], 1)
                sch.emit("gpsimd", f_bloop, waits=[("e_dv", scan_done)],
                         incs=[("d_gp", NTL * 32), ("e_gp", 1)], self_inc=True)
                scat_all = sch.cur("d_gp")

                for tl in range(NTL):
                    def f_ptr(e, tl=tl):
                        return e.transpose(p_gt[:], posT[0:8, tl * 128:(tl + 1) * 128],
                                           id32_sb[:])
                    sch.emit("tensor", f_ptr,
                             waits=[("e_dv", scan_done), ("e_pe", phaseA_pe),
                                    ("e_dv", sch.cur("e_dv")), ("d_sy", consts_dsy)],
                             incs=[("e_pe", 1)])
                    ptr_done = sch.cur("e_pe")
                    assert sch.cur("e_dv") == B0 + P_B * tl, (sch.cur("e_dv"), B0, tl)
                    def f_pev(e):
                        return e.tensor_copy(ptile[:], p_gt[:])
                    sch.emit("vector", f_pev,
                             waits=[("e_pe", ptr_done), ("d_gp", LG0 + 32 * tl)],
                             incs=[("e_dv", 1)])
                    def f_pe2(e):
                        return e.tensor_add(pecap[:], ptile[:], ecap_sb[:])
                    sch.emit("vector", f_pe2, waits=[("e_dv", sch.cur("e_dv"))], incs=[("e_dv", 1)])
                    ops = [
                        lambda e, tl=tl: e.tensor_mul(gtmp[:], m1all[:, tl * 8:(tl + 1) * 8], pecap[:]),
                        lambda e: e.reduce_sum(col["offf"][:], gtmp[:], axis=AX.X),
                        lambda e, tl=tl: e.tensor_mul(gtmp[:], m1all[:, tl * 8:(tl + 1) * 8], ptile[:]),
                        lambda e: e.reduce_sum(col["pos1"][:], gtmp[:], axis=AX.X),
                        lambda e: e.tensor_single_scalar(col["ovf"][:], col["pos1"][:], float(CAP), ALU.is_ge),
                        lambda e: e.scalar_tensor_tensor(col["offf"][:], col["ovf"][:], float(BIG),
                                                         col["offf"][:], ALU.mult, ALU.add),
                        lambda e: e.tensor_copy(coli["off1"][:], col["offf"][:]),
                        lambda e, tl=tl: e.tensor_sub(gtmp2[:], t2all[:, tl * 8:(tl + 1) * 8],
                                                      m1all[:, tl * 8:(tl + 1) * 8]),
                        lambda e: e.tensor_mul(gtmp[:], gtmp2[:], pecap[:]),
                        lambda e: e.reduce_sum(col["offf"][:], gtmp[:], axis=AX.X),
                        lambda e: e.tensor_mul(gtmp[:], gtmp2[:], ptile[:]),
                        lambda e: e.reduce_sum(col["pos1"][:], gtmp[:], axis=AX.X),
                        lambda e: e.tensor_single_scalar(col["ovf"][:], col["pos1"][:], float(CAP), ALU.is_ge),
                        lambda e: e.scalar_tensor_tensor(col["offf"][:], col["ovf"][:], float(BIG),
                                                         col["offf"][:], ALU.mult, ALU.add),
                        lambda e: e.tensor_copy(coli["off2"][:], col["offf"][:]),
                        lambda e, tl=tl: e.tensor_single_scalar(col["tkf"][:], iotac[:], float(tl * 128), ALU.add),
                        lambda e: e.tensor_copy(payA[:, 0:1], col["tkf"][:]),
                        lambda e: e.tensor_copy(payB[:, 0:1], col["tkf"][:]),
                    ]
                    for op_ in ops:
                        sch.emit("vector", op_, waits=[("e_dv", sch.cur("e_dv"))], incs=[("e_dv", 1)])
                    assert sch.cur("e_dv") == B0 + P_B * (tl + 1), (sch.cur("e_dv"), tl)

                if STAGE < 2:
                    return
                NKT = CAP // 512
                NTILE = 8 * NKT
                loadC_dsc = {}
                slot_freeC = dict(enumerate(slot_free_pe))
                lidx = [0]
                def sched_slot_loadC(ei, s):
                    l = lidx[0]
                    lidx[0] += 1
                    slot = l % 3
                    pe_need = slot_freeC[slot]
                    if s < 2:
                        def f(e, ei=ei, s=s, slot=slot):
                            e.dma_start(
                                out=wslot[slot].ap().rearrange("p (kc f) -> p kc f", kc=16)[:, 0:8, :],
                                in_=w1b_sc[ei].rearrange("(kc p) f -> p kc f", p=128)[:, :, s * 1024:(s + 1) * 1024]).then_inc(sems["d_sc"], 16)
                            e.dma_start(
                                out=wslot[slot].ap().rearrange("p (kc f) -> p kc f", kc=16)[:, 8:16, :],
                                in_=w3b_sc[ei].rearrange("(kc p) f -> p kc f", p=128)[:, :, s * 1024:(s + 1) * 1024]).then_inc(sems["d_sc"], 16)
                        sch.emit("scalar", f, waits=[("e_pe", pe_need), ("d_sy", cast_done.get((ei, s), 0))],
                                 incs=[("d_sc", 32)], self_inc=True)
                    else:
                        def f(e, ei=ei, slot=slot):
                            return e.dma_start(
                                out=wslot[slot].ap().rearrange("p (kc d2) -> p kc d2", kc=16),
                                in_=w2b_sc[ei].rearrange("(kc p) d2 -> p kc d2", p=128))
                        sch.emit("scalar", f, waits=[("e_pe", pe_need), ("d_sy", cast_done.get((ei, s), 0))],
                                 incs=[("d_sc", 32)])
                    loadC_dsc[l] = sch.cur("d_sc")
                    return l

                sch.emit("vector", lambda e: e.memset(oau[:], BIG),
                         waits=[("e_dv", sch.cur("e_dv"))], incs=[("e_dv", 1)])
                sch.emit("vector", lambda e: e.memset(yones[:], 1.0),
                         waits=[("e_dv", sch.cur("e_dv"))], incs=[("e_dv", 1)])

                C_pe0 = sch.cur("e_pe")
                B0C = sch.cur("e_dv")
                G0C = sch.cur("d_gp")
                P_DV = 48

                P_PE = 45 if CSTUB & 512 else 44

                def hp_base(k):
                    return B0C + 4 + (P_DV * k if k >= 1 else 0)
                dsy_ix0 = sch.cur("d_sy")
                xnT_free_c = [phaseA_pe]
                xnsc_all_ = xnsc_all
                scat_all_ = scat_all

                def f_cloop(e):
                    e.wait_ge(sems["d_gp"], scat_all_)
                    e.wait_ge(sems["d_sy"], xnsc_all_)
                    with (
                        e.register(f"r_st_{id(f_cloop)}") as r_st,
                        e.register(f"r_ta_{id(f_cloop)}") as r_ta,
                        e.register(f"r_tb_{id(f_cloop)}") as r_tb,
                    ):
                        e.reg_mov(r_st, B0C + 4)
                        e.reg_mov(r_ta, max(0, C_pe0 + 4 - P_PE))
                        e.reg_mov(r_tb, C_pe0 + 2)
                        with e.Fori(0, NTILE + 1):
                            if not NOLOOP:
                                e.wait_ge(sems["e_dv"], r_st)
                                e.wait_ge(sems["e_pe"], r_ta)
                            if FAKELOOP:
                                for _i in range(8):
                                    e.dma_start(out=dbg_sc.ap()[:, 0:4], in_=idxst[:]).then_inc(sems["d_gp"], 16)
                                if not NOLOOP:
                                    e.wait_ge(sems["e_pe"], r_tb)
                            else:
                                e.indirect_dma_start(
                                    xg_sb[0][:], None, xn_sc.ap(),
                                    bass.IndirectOffsetOnAxis(idxst[:, 0:1], 0),
                                    bounds_check=TOK - 1, oob_is_err=False).then_inc(sems["d_gp"], 16)
                                e.indirect_dma_start(
                                    xg_sb[1][:], None, xn_sc.ap(),
                                    bass.IndirectOffsetOnAxis(idxst[:, 1:2], 0),
                                    bounds_check=TOK - 1, oob_is_err=False).then_inc(sems["d_gp"], 16)
                                e.wait_ge(sems["e_pe"], r_tb)
                                e.indirect_dma_start(
                                    xg_sb[0][:], None, xn_sc.ap(),
                                    bass.IndirectOffsetOnAxis(idxst[:, 2:3], 0),
                                    bounds_check=TOK - 1, oob_is_err=False).then_inc(sems["d_gp"], 16)
                                e.indirect_dma_start(
                                    xg_sb[1][:], None, xn_sc.ap(),
                                    bass.IndirectOffsetOnAxis(idxst[:, 3:4], 0),
                                    bounds_check=TOK - 1, oob_is_err=False).then_inc(sems["d_gp"], 16)
                                for m in range(4):
                                    e.indirect_dma_start(
                                        yab_sc.ap(), bass.IndirectOffsetOnAxis(oau[:, m:m + 1], 0),
                                        y_sb[:, m * D:(m + 1) * D], None,
                                        bounds_check=2 * TOK - 1, oob_is_err=False).then_inc(sems["d_gp"], 16)
                            e.reg_add(r_st, r_st, P_DV)
                            e.reg_add(r_ta, r_ta, P_PE)
                            e.reg_add(r_tb, r_tb, P_PE)
                    e.nop().then_inc(sems["e_gp"], 1)
                sch.emit("gpsimd", f_cloop, incs=[("d_gp", (NTILE + 1) * 128), ("e_gp", 1)],
                         self_inc=True)
                gall = lambda k: G0C + 128 * k + 64
                sall_ = lambda k: G0C + 128 * (k + 1)

                ydrain = {}
                siluC = {(-1, 14): sch.cur("e_ac"), (-1, 15): sch.cur("e_ac")}
                hpC = {(-1, 14): sch.cur("e_dv"), (-1, 15): sch.cur("e_dv")}
                yevC = {(-1, 6): sch.cur("e_dv"), (-1, 7): sch.cur("e_dv")}
                hp_need = {}
                h1_doneC = {}
                h3_doneC = {}
                ymmC = {}
                lmapC = {}

                def emit_stage(k, rg, ix_loaded):
                    for sub in range(4):
                        if k < NTILE and not NOGATH:
                            srcap = idxt[rg][:, sub * 8: sub * 8 + 1]
                        else:
                            srcap = oobt[:, 0:1]
                        def f(e, srcap=srcap, sub=sub):
                            return e.tensor_copy(idxst[:, sub:sub + 1], srcap)
                        if CSTUB & 64:
                            f = lambda e: e.nop()
                        w = [("e_dv", sch.cur("e_dv"))]
                        if sub == 0:
                            w.append(("d_sy", ix_loaded))
                            if k >= 1:
                                w.append(("d_gp", gall(k - 1)))
                        sch.emit("vector", f, waits=w, incs=[("e_dv", 1)])

                def emit_oau(t):
                    rg = t % 2
                    for m in range(4):
                        ops = [
                            lambda e, rg=rg, m=m: e.tensor_copy(col["tkf"][:], idxt[rg][:, m * 8: m * 8 + 1]),
                            lambda e, rg=rg, m=m: e.tensor_copy(col["af"][:], idxt[rg][:, m * 8 + 1: m * 8 + 2]),
                            lambda e: e.tensor_scalar(col["t2f"][:], col["af"][:], -float(TOK), float(TOK),
                                                      ALU.mult, ALU.add),
                            lambda e: e.tensor_add(col["oaf"][:], col["tkf"][:], col["t2f"][:]),
                            (lambda e: e.tensor_copy(col["offf"][:], col["oaf"][:])) if NOSCAT else
                            (lambda e, m=m: e.tensor_copy(oau[:, m:m + 1], col["oaf"][:])),
                        ]
                        for i_, op_ in enumerate(ops):
                            if CSTUB & 64:
                                op_ = lambda e: e.nop()
                            w = [("e_dv", sch.cur("e_dv"))]
                            if m == 0 and i_ == 0:
                                w.append(("d_gp", sall_(t)))
                            sch.emit("vector", op_, waits=w, incs=[("e_dv", 1)])

                def emit_hp(k, b_of_fc):
                    for fc in range(16):
                        b = fc % 2
                        tb = tsil[b]
                        def f_hp(e, tb=tb, psum=p_h3[b], fc=fc):
                            return e.tensor_mul(hT[:, fc * 512:(fc + 1) * 512], tb[:], psum[:])
                        if CSTUB & 8:
                            f_hp = lambda e: e.nop()
                        sch.emit("vector", f_hp,
                                 waits=[("e_ac", siluC[(k, fc)]), ("e_pe", h3_doneC[(k, fc)]),
                                        ("e_dv", sch.cur("e_dv"))],
                                 incs=[("e_dv", 1)])
                        hpC[(k, fc)] = sch.cur("e_dv")
                        assert hpC[(k, fc)] == hp_base(k) + fc + 1, (k, fc, hpC[(k, fc)])
                    hp_need[k] = sch.cur("e_dv")

                def emit_yev(k):
                    for m in range(4):
                        for dc in range(2):
                            b = (m * 2 + dc) % 2
                            def f_ye(e, m=m, dc=dc, b=b):
                                return e.tensor_mul(y_sb[:, m * D + dc * 512: m * D + (dc + 1) * 512],
                                                    bass.AP(yones, 0, [[8, 128], [0, 512]]),
                                                    p_y[b][:])
                            if CSTUB & 32 or k >= CKMAX:
                                f_ye = lambda e: e.nop()
                            w_ye = [("e_pe", ymmC[(k, m, dc)]), ("e_dv", sch.cur("e_dv")),
                                    ("d_gp", sall_(k))]
                            if k == 0:
                                w_ye.append(("d_sy", cast_fin_dsy[0]))
                            sch.emit("vector", f_ye, waits=w_ye, incs=[("e_dv", 1)])
                            yevC[(k, m * 2 + dc)] = sch.cur("e_dv")

                for s0 in range(3):
                    lmapC[(0, s0)] = sched_slot_loadC(0, s0)
                for k in range(NTILE):
                    ei, kt = divmod(k, NKT)
                    rg = k % 2
                    def f_ixl(e, ei=ei, kt=kt, rg=rg):
                        return e.dma_start(
                            out=idxt[rg].ap().rearrange("p (s w) -> p s w", s=4),
                            in_=idx_sc.rearrange("(s p) w -> p s w", p=128)[:, (ei * CAP + kt * 512) // 128: (ei * CAP + kt * 512) // 128 + 4, :])
                    war_dv = 0 if k < 2 else B0C + 4 + P_DV * (k - 1)
                    sch.emit("sync", f_ixl,
                             waits=[("d_gp", scat_all_), ("e_dv", min(war_dv, sch.cur("e_dv")))],
                             incs=[("d_sy", 16)])
                    ix_loaded = sch.cur("d_sy")
                    assert ix_loaded == dsy_ix0 + 16 * (k + 1), (ix_loaded, dsy_ix0, k)

                    assert sch.cur("e_dv") == B0C + (0 if k == 0 else 28 + P_DV * (k - 1)), (sch.cur("e_dv"), k)
                    emit_stage(k, rg, ix_loaded)
                    if k >= 1:
                        emit_oau(k - 1)
                    assert sch.cur("e_dv") == B0C + 4 + (P_DV * k if k >= 1 else 0), (sch.cur("e_dv"), k)

                    assert sch.cur("e_pe") == C_pe0 + P_PE * k, (sch.cur("e_pe"), C_pe0, k)
                    for sub in range(4):
                        sr = sub % 2
                        for kc in range(8):
                            def f_tr(e, sr=sr, kc=kc):
                                return e.transpose(p_tp[:, kc * 128:(kc + 1) * 128],
                                                   xg_sb[sr][:, kc * 128:(kc + 1) * 128], ident[:])
                            if CSTUB & 1:
                                f_tr = lambda e: e.nop()
                            w = []
                            if kc == 0:
                                w = [("d_gp", G0C + 128 * k + 16 * (sub + 1)),
                                     ("e_ac", tp_i["free"][0])]
                            sch.emit("tensor", f_tr, waits=w,
                                     incs=([("e_pe", 1)] if kc == 7 else ()))
                        trp = sch.cur("e_pe")
                        def f_ev(e, rg=rg, sub=sub):
                            return e.copy(
                                bass.AP(xnT, rg * 4096 + sub * 128, [[NT * D, 128], [512, 8], [1, 128]]),
                                p_tp[:])
                        if CSTUB & 1:
                            f_ev = lambda e: e.nop()
                        sch.emit("scalar", f_ev,
                                 waits=[("e_pe", max(trp, xnT_free_c[0]))],
                                 incs=[("e_ac", 1)])
                        tp_i["free"][0] = sch.cur("e_ac")
                    xnTe_done = sch.cur("e_ac")
                    for fc in range(16):
                        half = fc // 8
                        l13 = lmapC[(ei, half)]
                        slot = l13 % 3
                        fcl = fc % 8
                        b = fc % 2
                        for which, psum, wofs in ((0, p_h1[b], 0), (1, p_h3[b], 8192)):
                            for kc in range(8):
                                def f_mm(e, slot=slot, wofs=wofs, kc=kc, fcl=fcl, rg=rg, psum=psum):
                                    return e.matmul(
                                        psum[:],
                                        wslot[slot][:, wofs + kc * 1024 + fcl * 128: wofs + kc * 1024 + (fcl + 1) * 128],
                                        xnT[:, rg * 4096 + kc * 512: rg * 4096 + (kc + 1) * 512],
                                        start=(kc == 0), stop=(kc == 7))
                                if CSTUB & 2:
                                    f_mm = lambda e: e.nop()
                                w = []
                                fw = []
                                if kc == 0:
                                    w.append(("d_sc", loadC_dsc[l13]))
                                    w.append(("e_ac", xnTe_done))
                                    if which == 0:
                                        w.append(("e_ac", siluC.get((k, fc - 2), siluC.get((k - 1, 14 + b), 0))))
                                    else:
                                        if fc >= 2:
                                            fw.append(("e_dv", hp_base(k) + fc - 1))
                                        else:
                                            w.append(("e_dv", hpC.get((k - 1, 14 + b), hpC[(-1, 14 + b)])))
                                sch.emit("tensor", f_mm, waits=w, fwaits=fw,
                                         incs=([("e_pe", 1)] if kc == 7 else ()))
                            if which == 0:
                                h1_doneC[(k, fc)] = sch.cur("e_pe")
                            else:
                                h3_doneC[(k, fc)] = sch.cur("e_pe")
                                xnT_free_c[0] = sch.cur("e_pe")
                        slot_freeC[slot] = sch.cur("e_pe")
                        tb = tsil[fc % 2]
                        def f_sil(e, tb=tb, psum=p_h1[b]):
                            return e.activation(tb[:], psum[:], AF.Silu)
                        if CSTUB & 4:
                            f_sil = lambda e: e.nop()
                        if fc >= 2:
                            sch.emit("scalar", f_sil,
                                     waits=[("e_pe", h1_doneC[(k, fc)])],
                                     fwaits=[("e_dv", hp_base(k) + fc - 1)],
                                     incs=[("e_ac", 1)])
                        else:
                            tmp_free = hpC.get((k - 1, 14 + (fc % 2)), hpC[(-1, 14 + (fc % 2))])
                            sch.emit("scalar", f_sil,
                                     waits=[("e_pe", h1_doneC[(k, fc)]), ("e_dv", tmp_free)],
                                     incs=[("e_ac", 1)])
                        siluC[(k, fc)] = sch.cur("e_ac")
                        if kt == NKT - 1 and fc in (7, 15) and ei + 1 < 8:
                            ns = 0 if fc == 7 else 1
                            lmapC[(ei + 1, ns)] = sched_slot_loadC(ei + 1, ns)
                    emit_hp(k, None)
                    l2 = lmapC[(ei, 2)]
                    slot2 = l2 % 3
                    for m in range(4):
                        for dc in range(2):
                            b = (m * 2 + dc) % 2
                            for fc in range(16):
                                def f_ymm(e, fc=fc, m=m, dc=dc, slot2=slot2, b=b):
                                    return e.matmul(
                                        p_y[b][:],
                                        hT[:, fc * 512 + m * 128: fc * 512 + (m + 1) * 128],
                                        wslot[slot2][:, fc * 1024 + dc * 512: fc * 1024 + (dc + 1) * 512],
                                        start=(fc == 0), stop=(fc == 15))
                                if CSTUB & 16 or k >= CKMAX:
                                    f_ymm = lambda e: e.nop()
                                w = []
                                if fc == 0:
                                    w = [("e_dv", hp_need[k]), ("d_sc", loadC_dsc[l2]),
                                         ("e_dv", yevC.get((k, m * 2 + dc - 2),
                                                  yevC.get((k - 1, 6 + b), yevC[(-1, 6 + b)])))]
                                sch.emit("tensor", f_ymm, waits=w,
                                         incs=([("e_pe", 1)] if fc == 15 else ()))
                            ymmC[(k, m, dc)] = sch.cur("e_pe")
                    if CSTUB & 512:
                        sch.emit("tensor",
                                 lambda e: e.matmul(p_gt[:], ident[:], ident[:, 0:8],
                                                    start=True, stop=True),
                                 incs=[("e_pe", 1)])
                    ydrain[k] = sch.cur("e_pe")
                    slot_freeC[slot2] = sch.cur("e_pe")
                    yev0 = hp_base(k) + 17
                    for fi in range(16):
                        sch.emit("tensor",
                                 lambda e: e.matmul(p_gt[:], ident[:], ident[:, 0:8],
                                                    start=True, stop=True),
                                 fwaits=[("e_dv", yev0 + min(fi // 2, 7))])
                    emit_yev(k)
                    if kt == NKT - 1 and ei + 1 < 8:
                        lmapC[(ei + 1, 2)] = sched_slot_loadC(ei + 1, 2)
                emit_stage(NTILE, 0, dsy_ix0)
                emit_oau(NTILE - 1)
                for _ in range(4):
                    sch.emit("tensor", lambda e: e.nop(), incs=[("e_pe", 1)])
                scat_fin = sch.cur("d_gp")

                if STAGE < 3:
                    return
                for g in range(NGRP):
                    def fload(e, g=g):
                        return e.dma_start(
                            out=x_g.ap().rearrange("p (t d2) -> p t d2", t=NT),
                            in_=x_work.rearrange("(t p) d2 -> p t d2", p=128)[:, g * NT:(g + 1) * NT, :])
                    sch.emit("sync", fload, waits=[("d_gp", scat_fin), ("e_dv", sch.cur("e_dv")),
                                                   ("d_sy", sch.cur("d_sy"))],
                             incs=[("d_sy", 16)])
                    xg_loaded = sch.cur("d_sy")
                    for t in range(NT):
                        gt = g * NT + t
                        def f_la(e, gt=gt):
                            return e.dma_start(out=y_sb[:, 0:D],
                                               in_=yab_sc.rearrange("(a p) d2 -> a p d2", p=128)[gt])
                        sch.emit("sync", f_la, waits=[("e_dv", sch.cur("e_dv"))], incs=[("d_sy", 16)])
                        def f_lb(e, gt=gt):
                            return e.dma_start(out=y_sb[:, D:2 * D],
                                               in_=yab_sc.rearrange("(a p) d2 -> a p d2", p=128)[TOK // 128 + gt])
                        sch.emit("scalar", f_lb, waits=[("e_dv", sch.cur("e_dv")),
                                                        ("d_gp", scat_fin)], incs=[("d_sc", 16)])
                        la = sch.cur("d_sy")
                        lb = sch.cur("d_sc")
                        def f_swa(e, t=t, gt=gt):
                            return e.scalar_tensor_tensor(
                                x_g[:, t * D:(t + 1) * D], y_sb[:, 0:D],
                                wab[:, gt * 2: gt * 2 + 1],
                                x_g[:, t * D:(t + 1) * D], ALU.mult, ALU.add)
                        sch.emit("vector", f_swa, waits=[("d_sy", max(xg_loaded, la)),
                                                         ("e_dv", sch.cur("e_dv"))],
                                 incs=[("e_dv", 1)])
                        def f_swb(e, t=t, gt=gt):
                            return e.scalar_tensor_tensor(
                                x_g[:, t * D:(t + 1) * D], y_sb[:, D:2 * D],
                                wab[:, gt * 2 + 1: gt * 2 + 2],
                                x_g[:, t * D:(t + 1) * D], ALU.mult, ALU.add)
                        sch.emit("vector", f_swb, waits=[("d_sc", lb), ("e_dv", sch.cur("e_dv"))],
                                 incs=[("e_dv", 1)])
                    swept = sch.cur("e_dv")
                    def f_st2(e, g=g):
                        return e.dma_start(
                            out=dst_t.rearrange("(t p) d2 -> p t d2", p=128)[:, g * NT:(g + 1) * NT, :],
                            in_=x_g.ap().rearrange("p (t d2) -> p t d2", t=NT))
                    sch.emit("sync", f_st2, waits=[("e_dv", swept)], incs=[("d_sy", 16)])

        def ln_tile(col, cvx, x_ap, out_ap, xload_sem, xload_cnt, extra_wait=()):
            def f_sum(e):
                return e.reduce_sum(col["sum"][:], x_ap, axis=AX.X)
            sch.emit("vector", f_sum, waits=[(xload_sem, xload_cnt)] + list(extra_wait), incs=[("e_dv", 1)])
            def f_sq(e):
                return e.square(col["dump"][:], x_ap)
            sch.emit("scalar", f_sq, waits=[(xload_sem, xload_cnt), ("e_dv", col["sq_free"][0])],
                     incs=[("e_ac", 1)])
            sq_ac = sch.cur("e_ac")
            def f_ssq(e):
                return e.reduce_sum(col["sumsq"][:], col["dump"][:], axis=AX.X)
            sch.emit("vector", f_ssq, waits=[("e_ac", sq_ac)], incs=[("e_dv", 1)])
            sq = sch.cur("e_dv")
            col["sq_free"][0] = sq
            steps = [
                lambda e: e.tensor_scalar_mul(col["m"][:], col["sum"][:], 1.0 / D),
                lambda e: e.tensor_scalar_mul(col["ex2"][:], col["sumsq"][:], 1.0 / D),
                lambda e: e.tensor_scalar_mul(col["m2c"][:], col["m"][:], col["m"][:]),
                lambda e: e.tensor_sub(col["var"][:], col["ex2"][:], col["m2c"][:]),
            ]
            for i, s in enumerate(steps):
                sch.emit("vector", s, waits=[("e_dv", sq if i == 0 else sch.cur("e_dv"))],
                         incs=[("e_dv", 1)])
            var_done = sch.cur("e_dv")
            def f_ln(e):
                return e.activation(col["rstd"][:], col["var"][:], AF.Ln, bias=cvx[:, 0:1])
            sch.emit("scalar", f_ln, waits=[("e_dv", var_done)], incs=[("e_ac", 1)])
            def f_exp(e):
                return e.activation(col["rstd"][:], col["rstd"][:], AF.Exp, scale=cvx[:, 1:2])
            sch.emit("scalar", f_exp, waits=[("e_ac", sch.cur("e_ac"))], incs=[("e_ac", 1)])
            rstd_done = sch.cur("e_ac")
            def f_xn(e):
                return e.tensor_scalar(out_ap, x_ap, col["m"][:], col["rstd"][:],
                                       ALU.subtract, ALU.mult)
            sch.emit("vector", f_xn, waits=[("e_ac", rstd_done), ("e_dv", sch.cur("e_dv"))],
                     incs=[("e_dv", 1)])
            xn_done_dv = sch.cur("e_dv")
            def f_bridge(e):
                return e.nop()
            sch.emit("scalar", f_bridge, waits=[("e_dv", xn_done_dv)], incs=[("e_ac", 1)])
            return sch.cur("e_ac")

        def transposes_batch(src_sb, dst_sb, tp, ready_waits):
            b = tp["idx"][0] % 2
            tp["idx"][0] += 1
            pb = tp["banks"][b]
            for kc in range(8):
                def f_tr(e, kc=kc, pb=pb):
                    return e.transpose(pb[:, kc * 128:(kc + 1) * 128],
                                       src_sb[:, kc * 128:(kc + 1) * 128], ident[:])
                w = list(ready_waits) + [("e_ac", tp["free"][b])] if kc == 0 else []
                sch.emit("tensor", f_tr, waits=w,
                         incs=([("e_pe", 1)] if kc == 7 else ()))
            tr_done = sch.cur("e_pe")
            def f_ev(e, pb=pb):
                return e.copy(dst_sb[:], pb[:])
            sch.emit("scalar", f_ev, waits=[("e_pe", tr_done)], incs=[("e_ac", 1)])
            tp["free"][b] = sch.cur("e_ac")
            return sch.cur("e_ac")

        def rope4(p_src, gt_sb, t0, wbuf, out_bf, ch, cos_sb, sin_sb, psrc_done,
                  out_war=0, prev_gp=0):
            cosb = bass.AP(cos_sb, ch * 32, [[NCH * 32, 128], [0, 32], [1, 32]])
            sinb = bass.AP(sin_sb, ch * 32, [[NCH * 32, 128], [0, 16], [1, 32]])
            hap = lambda t, o: bass.AP(t, o, [[int(t.shape[1]), 128], [64, 16], [1, 32]])
            def f_t0(e):
                return e.tensor_mul(t0[:], p_src[:], gt_sb[:])
            sch.emit("vector", f_t0, waits=[("e_pe", psrc_done), ("e_gp", prev_gp)],
                     incs=[("e_dv", 1)])
            t0_done = sch.cur("e_dv")
            def f_u(e):
                return e.tensor_tensor(wbuf[:], t0[:], cosb, ALU.mult)
            sch.emit("vector", f_u, waits=[("e_dv", t0_done)], incs=[("e_gp", 1)])
            def f_w0(e):
                return e.tensor_tensor(hap(out_bf, 0), hap(t0, 32), sinb, ALU.mult)
            sch.emit("vector", f_w0, waits=[("e_dv", t0_done), ("e_pe", out_war)],
                     incs=[("e_gp", 1)])
            def f_w1(e):
                return e.tensor_tensor(hap(out_bf, 32), hap(t0, 0), sinb, ALU.mult)
            sch.emit("vector", f_w1, waits=[("e_dv", t0_done)], incs=[("e_gp", 1)])
            w_done = sch.cur("e_gp")
            def f_c0(e):
                return e.tensor_sub(hap(out_bf, 0), hap(wbuf, 0), hap(out_bf, 0))
            sch.emit("vector", f_c0, waits=[("e_gp", w_done)], incs=[("e_dv", 1)])
            def f_c1(e):
                return e.tensor_add(hap(out_bf, 32), hap(out_bf, 32), hap(wbuf, 32))
            sch.emit("vector", f_c1, waits=[("e_dv", sch.cur("e_dv"))], incs=[("e_dv", 1)])
            return sch.cur("e_dv")

        def attn_pass1():
            fs = contextlib.ExitStack()
            with fs:
                FE = fs.enter_context
                RING = 3
                wk_sb = FE(nc.sbuf_tensor("wk_sb", [128, 8 * D], BF16))
                wv_sb = FE(nc.sbuf_tensor("wv_sb", [128, 8 * D], BF16))
                gk_sb = FE(nc.sbuf_tensor("gk_sb", [128, D], F32))
                gc_sb = FE(nc.sbuf_tensor("gc_sb", [64, D], F32))
                cos_sb = FE(nc.sbuf_tensor("cos_sb1", [128, NCH * 32], F32))
                sin_sb = FE(nc.sbuf_tensor("sin_sb1", [128, NCH * 32], F32))
                x_ch = [FE(nc.sbuf_tensor(f"x_ch1_{i}", [128, D], F32)) for i in range(RING)]
                xn_ch = [FE(nc.sbuf_tensor(f"xn_ch1_{i}", [128, D], BF16)) for i in range(RING)]
                xnT_ch = [FE(nc.sbuf_tensor(f"xnT_ch1_{i}", [128, D], BF16)) for i in range(RING)]
                t0 = FE(nc.sbuf_tensor("t0_1", [128, D], F32))
                wbuf = FE(nc.sbuf_tensor("wbuf_1", [128, D], F32))
                khat = [FE(nc.sbuf_tensor(f"khat_{i}", [128, D], BF16)) for i in range(RING)]
                khT = [FE(nc.sbuf_tensor(f"khT_{i}", [128, D], BF16)) for i in range(RING)]
                vbf = [FE(nc.sbuf_tensor(f"vbf_{i}", [128, D], BF16)) for i in range(RING)]
                sgb = FE(nc.sbuf_tensor("sgb", [64, D], F32))
                sall = FE(nc.sbuf_tensor("sall", [64, D], F32))
                srecv = FE(nc.sbuf_tensor("srecv", [64, D], F32))
                tst = FE(nc.sbuf_tensor("tst", [64, D], F32))
                col = {n: FE(nc.sbuf_tensor(f"a1_{n}", [128, 1], F32))
                       for n in ["sum", "sumsq", "m", "ex2", "m2c", "var", "rstd", "mb"]}
                col["dump"] = FE(nc.sbuf_tensor("a1_dump", [128, D], F32))
                col["sq_free"] = [0]
                tp = {"banks": [FE(nc.psum_tensor(f"a1_tp{i}", [128, 1024], BF16))
                                for i in range(2)],
                      "free": [0, 0], "idx": [0]}
                p_k = FE(nc.psum_tensor("a1_pk", [128, D], F32))
                p_v = FE(nc.psum_tensor("a1_pv", [128, D], F32))
                p_s = FE(nc.psum_tensor("a1_ps", [64, D], F32))

                def f_w(e):
                    e.dma_start(out=wk_sb.ap().rearrange("p (kc d2) -> p kc d2", kc=8),
                                in_=wkT.rearrange("(kc p) d2 -> p kc d2", p=128)).then_inc(sems["d_gp"], 16)
                    e.dma_start(out=wv_sb.ap().rearrange("p (kc d2) -> p kc d2", kc=8),
                                in_=wvT.rearrange("(kc p) d2 -> p kc d2", p=128)).then_inc(sems["d_gp"], 16)
                sch.emit("gpsimd", f_w, incs=[("d_gp", 32)], self_inc=True)
                def f_c(e):
                    e.dma_start(out=gk_sb[:], in_=gk_in).then_inc(sems["d_sc"], 16)
                    e.dma_start(out=gc_sb[:], in_=gc_in).then_inc(sems["d_sc"], 16)
                    e.dma_start(out=cos_sb[:], in_=cos_in).then_inc(sems["d_sc"], 16)
                    e.dma_start(out=sin_sb[:], in_=sin_in).then_inc(sems["d_sc"], 16)
                sch.emit("scalar", f_c, incs=[("d_sc", 64)], self_inc=True)
                w_loaded = sch.cur("d_gp")
                c_loaded = sch.cur("d_sc")
                sch.emit("vector", lambda e: e.memset(sgb[:], 0.0), incs=[("e_dv", 1)])

                kv_mm_done = {}
                rope_done = {}
                rope_gp = {}
                store_done = {}
                for ch in range(NCH):
                    r = ch % RING
                    def f_x(e, ch=ch, r=r):
                        return e.dma_start(out=x_ch[r][:], in_=x_work.rearrange(
                            "(c p) d2 -> c p d2", p=128)[ch])
                    sch.emit("sync", f_x, waits=[("e_dv", rope_done.get(ch - RING, 0))],
                             incs=[("d_sy", 16)])
                    x_loaded = sch.cur("d_sy")
                    xn_done = ln_tile(col, cvec, x_ch[r][:], xn_ch[r][:], "d_sy", x_loaded)
                    xnT_done = transposes_batch(xn_ch[r], xnT_ch[r], tp, [("e_ac", xn_done)])
                    for which, psum, wsb in ((0, p_k, wk_sb), (1, p_v, wv_sb)):
                        for dc in range(2):
                            for kc in range(8):
                                def f_mm(e, psum=psum, wsb=wsb, dc=dc, kc=kc, r=r):
                                    return e.matmul(
                                        psum[:, dc * 512:(dc + 1) * 512],
                                        xnT_ch[r][:, kc * 128:(kc + 1) * 128],
                                        wsb[:, kc * D + dc * 512: kc * D + (dc + 1) * 512],
                                        start=(kc == 0), stop=(kc == 7))
                                w = []
                                if kc == 0:
                                    w = [("e_ac", xnT_done)]
                                    if ch == 0 and which == 0 and dc == 0:
                                        w.append(("d_gp", w_loaded))
                                    if which == 0:
                                        w.append(("e_dv", rope_done.get(ch - 1, 0)))
                                    else:
                                        w.append(("e_dv", store_done.get(("vev", ch - 1), 0)))
                                sch.emit("tensor", f_mm, waits=w,
                                         incs=([("e_pe", 1)] if kc == 7 else ()))
                        kv_mm_done[(ch, which)] = sch.cur("e_pe")
                    if ch == 0:
                        sch.emit("vector", lambda e: None, waits=[("d_sc", c_loaded)])
                        sch.emit("gpsimd", lambda e: None, waits=[("d_sc", c_loaded)])
                    kd = rope4(p_k, gk_sb, t0, wbuf, khat[r], ch, cos_sb, sin_sb,
                               kv_mm_done[(ch, 0)], prev_gp=rope_gp.get(ch - 1, 0))
                    rope_done[ch] = kd
                    rope_gp[ch] = sch.cur("e_gp")
                    def f_vev(e, r=r):
                        return e.tensor_copy(vbf[r][:], p_v[:])
                    sch.emit("vector", f_vev,
                             waits=[("e_pe", kv_mm_done[(ch, 1)]),
                                    ("d_sc", store_done.get(("kvst", ch - RING), 0))],
                             incs=[("e_dv", 1)])
                    store_done[("vev", ch)] = sch.cur("e_dv")
                    khT_done = transposes_batch(
                        khat[r], khT[r], tp,
                        [("e_dv", kd), ("d_sc", store_done.get(("kvst", ch - RING), 0))])
                    def f_st(e, ch=ch, r=r):
                        e.dma_start(out=kt_sc.rearrange("(c p) d2 -> c p d2", p=128)[ch],
                                    in_=khT[r][:]).then_inc(sems["d_sc"], 16)
                        e.dma_start(out=v_sc.rearrange("(c p) d2 -> c p d2", p=128)[ch],
                                    in_=vbf[r][:]).then_inc(sems["d_sc"], 16)
                    sch.emit("scalar", f_st, waits=[("e_ac", khT_done), ("e_dv", store_done[("vev", ch)])],
                             incs=[("d_sc", 32)], self_inc=True)
                    store_done[("kvst", ch)] = sch.cur("d_sc")
                    for h in range(H):
                        def f_smm(e, h=h, r=r):
                            return e.matmul(p_s[:, h * 64:(h + 1) * 64],
                                            khat[r][:, h * 64:(h + 1) * 64],
                                            vbf[r][:, h * 64:(h + 1) * 64],
                                            start=True, stop=True)
                        w = []
                        if h == 0:
                            w = [("e_dv", max(kd, store_done[("vev", ch)],
                                              store_done.get(("sup", ch - 1), 0)))]
                        sch.emit("tensor", f_smm, waits=w,
                                 incs=([("e_pe", 1)] if h == H - 1 else ()))
                    smm_done = sch.cur("e_pe")
                    def f_sup(e):
                        return e.tensor_add(sall[:], sgb[:], p_s[:])
                    sch.emit("vector", f_sup, waits=[("e_pe", smm_done), ("e_dv", sch.cur("e_dv"))],
                             incs=[("e_dv", 1)])
                    def f_sg(e):
                        return e.tensor_mul(sgb[:], sall[:], gc_sb[:])
                    sch.emit("vector", f_sg,
                             waits=[("e_dv", sch.cur("e_dv")),
                                    ("d_sc", store_done.get(("sgst", ch - 1), 0))],
                             incs=[("e_dv", 1)])
                    store_done[("sup", ch)] = sch.cur("e_dv")
                    if ch < NCH - 1:
                        def f_sgst(e, ch=ch):
                            return e.dma_start(out=sg_sc.rearrange("(c p) d2 -> c p d2", p=64)[ch],
                                               in_=sgb[:])
                        sch.emit("scalar", f_sgst, waits=[("e_dv", store_done[("sup", ch)])],
                                 incs=[("d_sc", 16)])
                        store_done[("sgst", ch)] = sch.cur("d_sc")
                sall_done = sch.cur("e_dv")

                def f_xch(e):
                    with (
                        e.register("r_h") as r_h,
                        e.register("r_soff") as r_soff,
                        e.register("r_poff") as r_poff,
                        e.register("r_foff") as r_foff,
                        e.register("r_pfoff") as r_pfoff,
                        e.register("r_magic") as r_magic,
                        e.register("r_flag") as r_flag,
                        e.register("r_cond") as r_cond,
                        e.register("r_tgt") as r_tgt,
                    ):
                        pid = e.compute_val(nc.partition_id(), donate=False)
                        e.reg_alu(r_h, pid, 1, ALU.bitwise_and)
                        e.reg_mul(r_soff, r_h, 64 * D)
                        e.reg_alu(r_poff, r_h, 1, ALU.bitwise_xor)
                        e.reg_mul(r_poff, r_poff, 64 * D)
                        e.reg_mul(r_foff, r_h, 16)
                        e.reg_alu(r_pfoff, r_h, 1, ALU.bitwise_xor)
                        e.reg_mul(r_pfoff, r_pfoff, 16)
                        e.reg_load(r_magic, s_magic[:1, :1])
                        e.dma_start(out=bass.AP(xch, r_soff, [[D, 64], [1, D]]),
                                    in_=sall[:]).then_inc(sems["pl"], 16)
                        e.reg_mov(r_tgt, 16)
                        e.wait_ge(sems["pl"], r_tgt)
                        e.dma_start(out=bass.AP(xchf, r_foff, [[16, 1], [1, 16]]),
                                    in_=s_magic[:]).then_inc(sems["pl"], 16)
                        e.reg_mov(r_tgt, 32)
                        e.wait_ge(sems["pl"], r_tgt)
                        e.reg_mov(r_cond, 1)
                        with e.While(r_cond):
                            e.dma_start(out=s_flag[:],
                                        in_=bass.AP(xchf, r_pfoff, [[16, 1], [1, 16]])
                                        ).then_inc(sems["pl"], 16)
                            e.reg_add(r_tgt, r_tgt, 16)
                            e.wait_ge(sems["pl"], r_tgt)
                            e.reg_load(r_flag, s_flag[:1, :1])
                            e.reg_sub(r_cond, r_flag, r_magic)
                        e.dma_start(out=srecv[:],
                                    in_=bass.AP(xch, r_poff, [[D, 64], [1, D]])
                                    ).then_inc(sems["pl"], 16)
                        e.reg_add(r_tgt, r_tgt, 16)
                        e.wait_ge(sems["pl"], r_tgt)
                    return e.nop()
                sch.emit("gpsimd", f_xch,
                         waits=[("e_dv", sall_done), ("d_gp", sch.cur("d_gp"))],
                         incs=[("e_gp", 1)])
                def f_t0x(e):
                    return e.scalar_tensor_tensor(tst[:], srecv[:], rmask[:], gc_sb[:],
                                                  ALU.mult, ALU.mult)
                sch.emit("vector", f_t0x, waits=[("e_gp", sch.cur("e_gp"))], incs=[("e_dv", 1)])
                def f_tst(e):
                    return e.dma_start(out=sg_sc.rearrange("(c p) d2 -> c p d2", p=64)[NCH - 1],
                                       in_=tst[:])
                sch.emit("scalar", f_tst, waits=[("e_dv", sch.cur("e_dv"))], incs=[("d_sc", 16)])

        def attn_pass2():
            fs = contextlib.ExitStack()
            with fs:
                FE = fs.enter_context
                RING = 3
                wq_sb = FE(nc.sbuf_tensor("wq_sb", [128, 8 * D], BF16))
                wgt_sb = FE(nc.sbuf_tensor("wgt_sb", [128, 8 * D], BF16))
                wo_sb = FE(nc.sbuf_tensor("wo_sb", [128, 8 * D], BF16))
                gq_sb = FE(nc.sbuf_tensor("gq_sb", [128, D], F32))
                gc_sb = FE(nc.sbuf_tensor("gc_sb2", [64, D], F32))
                cos_sb = FE(nc.sbuf_tensor("cos_sb2", [128, NCH * 32], F32))
                sin_sb = FE(nc.sbuf_tensor("sin_sb2", [128, NCH * 32], F32))
                x_ch = [FE(nc.sbuf_tensor(f"x_ch2_{i}", [128, D], F32)) for i in range(RING)]
                xn_ch = [FE(nc.sbuf_tensor(f"xn_ch2_{i}", [128, D], BF16)) for i in range(RING)]
                xnT_ch = [FE(nc.sbuf_tensor(f"xnT_ch2_{i}", [128, D], BF16)) for i in range(RING)]
                t0 = FE(nc.sbuf_tensor("t0_2", [128, D], F32))
                wbuf = FE(nc.sbuf_tensor("wbuf_2", [128, D], F32))
                qhat = FE(nc.sbuf_tensor("qhat", [128, D], BF16))
                qhT = FE(nc.sbuf_tensor("qhT", [128, D], BF16))
                g_sb = FE(nc.sbuf_tensor("g_sb", [128, D], BF16))
                og = FE(nc.sbuf_tensor("og", [128, D], BF16))
                ogT = FE(nc.sbuf_tensor("ogT", [128, D], BF16))
                kh_sb = [FE(nc.sbuf_tensor(f"kh2_{i}", [128, D], BF16)) for i in range(RING)]
                v_sb = [FE(nc.sbuf_tensor(f"v2_{i}", [128, D], BF16)) for i in range(RING)]
                adt = [FE(nc.sbuf_tensor(f"adt_{i}", [128, 128], BF16)) for i in range(2)]
                sg_r = [FE(nc.sbuf_tensor(f"sg_r{i}", [64, D], F32)) for i in range(RING)]
                tst = FE(nc.sbuf_tensor("tst2", [64, D], F32))
                rhs_s = FE(nc.sbuf_tensor("rhs_s", [128, D], BF16))
                col = {n: FE(nc.sbuf_tensor(f"a2_{n}", [128, 1], F32))
                       for n in ["sum", "sumsq", "m", "ex2", "m2c", "var", "rstd", "mb"]}
                col["dump"] = FE(nc.sbuf_tensor("a2_dump", [128, D], F32))
                col["sq_free"] = [0]
                tp = {"banks": [FE(nc.psum_tensor(f"a2_tp{i}", [128, 1024], BF16))
                                for i in range(2)],
                      "free": [0, 0], "idx": [0]}
                p_qy = FE(nc.psum_tensor("a2_qy", [128, D], F32))
                p_g = FE(nc.psum_tensor("a2_g", [128, 512], F32))
                p_at = [FE(nc.psum_tensor(f"a2_at{i}", [128, 128], F32)) for i in range(2)]
                p_o = FE(nc.psum_tensor("a2_o", [128, 512], F32))
                maskt_b = bass.AP(maskt, 0, [[128, 128], [0, 4], [1, 128]])

                def f_w(e):
                    e.dma_start(out=wq_sb.ap().rearrange("p (kc d2) -> p kc d2", kc=8),
                                in_=wqT.rearrange("(kc p) d2 -> p kc d2", p=128)).then_inc(sems["d_gp"], 16)
                    e.dma_start(out=wgt_sb.ap().rearrange("p (kc d2) -> p kc d2", kc=8),
                                in_=wgT.rearrange("(kc p) d2 -> p kc d2", p=128)).then_inc(sems["d_gp"], 16)
                    e.dma_start(out=wo_sb.ap().rearrange("p (kc d2) -> p kc d2", kc=8),
                                in_=woT.rearrange("(kc p) d2 -> p kc d2", p=128)).then_inc(sems["d_gp"], 16)
                sch.emit("gpsimd", f_w, incs=[("d_gp", 48)], self_inc=True)
                def f_c(e):
                    e.dma_start(out=gq_sb[:], in_=gq_in).then_inc(sems["d_sc"], 16)
                    e.dma_start(out=gc_sb[:], in_=gc_in).then_inc(sems["d_sc"], 16)
                    e.dma_start(out=cos_sb[:], in_=cos_in).then_inc(sems["d_sc"], 16)
                    e.dma_start(out=sin_sb[:], in_=sin_in).then_inc(sems["d_sc"], 16)
                    e.dma_start(out=tst[:],
                                in_=sg_sc.rearrange("(c p) d2 -> c p d2", p=64)[NCH - 1]).then_inc(sems["d_sc"], 16)
                sch.emit("scalar", f_c, incs=[("d_sc", 80)], self_inc=True)
                w_loaded = sch.cur("d_gp")
                c_loaded = sch.cur("d_sc")

                done = {}
                rope_gp = {}
                for ch in range(NCH):
                    r = ch % RING
                    def f_x(e, ch=ch, r=r):
                        return e.dma_start(out=x_ch[r][:], in_=x_work.rearrange(
                            "(c p) d2 -> c p d2", p=128)[ch])
                    sch.emit("sync", f_x,
                             waits=[("d_sy", sch.cur("d_sy") if ch < RING else done[("xst", ch - RING)])],
                             incs=[("d_sy", 16)])
                    x_loaded = sch.cur("d_sy")
                    def f_kv(e, ch=ch, r=r):
                        e.dma_start(out=kh_sb[r][:],
                                    in_=kt_sc.rearrange("(c p) d2 -> c p d2", p=128)[ch]).then_inc(sems["d_sc"], 16)
                        e.dma_start(out=v_sb[r][:],
                                    in_=v_sc.rearrange("(c p) d2 -> c p d2", p=128)[ch]).then_inc(sems["d_sc"], 16)
                    sch.emit("scalar", f_kv, waits=[("e_pe", done.get(("kvuse", ch - RING), 0))],
                             incs=[("d_sc", 32)], self_inc=True)
                    kv_loaded = sch.cur("d_sc")
                    if ch >= 1:
                        def f_sg(e, ch=ch, r=r):
                            return e.dma_start(out=sg_r[r][:],
                                               in_=sg_sc.rearrange("(c p) d2 -> c p d2", p=64)[ch - 1])
                        sch.emit("scalar", f_sg, waits=[("e_dv", done.get(("rhs", ch - RING), 0))],
                                 incs=[("d_sc", 16)])
                    sg_loaded = sch.cur("d_sc")

                    xn_done = ln_tile(col, cvec, x_ch[r][:], xn_ch[r][:], "d_sy", x_loaded)
                    xnT_done = transposes_batch(xn_ch[r], xnT_ch[r], tp, [("e_ac", xn_done)])
                    for dc in range(2):
                        for kc in range(8):
                            def f_mm(e, dc=dc, kc=kc, r=r):
                                return e.matmul(
                                    p_qy[:, dc * 512:(dc + 1) * 512],
                                    xnT_ch[r][:, kc * 128:(kc + 1) * 128],
                                    wq_sb[:, kc * D + dc * 512: kc * D + (dc + 1) * 512],
                                    start=(kc == 0), stop=(kc == 7))
                            w = []
                            if kc == 0:
                                w = [("e_ac", xnT_done)]
                                if ch == 0 and dc == 0:
                                    w.append(("d_gp", w_loaded))
                                    w.append(("d_sc", c_loaded))
                                w.append(("e_dv", done.get(("xadd", ch - 1), 0)))
                            sch.emit("tensor", f_mm, waits=w,
                                     incs=([("e_pe", 1)] if kc == 7 else ()))
                    done[("qmm", ch)] = sch.cur("e_pe")
                    for gh in range(2):
                        for kc in range(8):
                            def f_mm(e, gh=gh, kc=kc, r=r):
                                return e.matmul(
                                    p_g[:],
                                    xnT_ch[r][:, kc * 128:(kc + 1) * 128],
                                    wgt_sb[:, kc * D + gh * 512: kc * D + (gh + 1) * 512],
                                    start=(kc == 0), stop=(kc == 7))
                            w = []
                            if kc == 0:
                                w = [("e_ac", xnT_done),
                                     ("e_ac", done.get(("gsil", ch, gh - 1),
                                              done.get(("gsil", ch - 1, 1), 0)))]
                            sch.emit("tensor", f_mm, waits=w,
                                     incs=([("e_pe", 1)] if kc == 7 else ()))
                        gmm = sch.cur("e_pe")
                        def f_gs(e, gh=gh):
                            return e.activation(g_sb[:, gh * 512:(gh + 1) * 512], p_g[:], AF.Silu)
                        sch.emit("scalar", f_gs,
                                 waits=[("e_pe", gmm), ("e_dv", done.get(("ogev", ch - 1), 0))],
                                 incs=[("e_ac", 1)])
                        done[("gsil", ch, gh)] = sch.cur("e_ac")
                    done[("gsil", ch)] = sch.cur("e_ac")
                    qd = rope4(p_qy, gq_sb, t0, wbuf, qhat, ch, cos_sb, sin_sb,
                               done[("qmm", ch)], prev_gp=rope_gp.get(ch - 1, 0))
                    rope_gp[ch] = sch.cur("e_gp")
                    qhT_done = transposes_batch(qhat, qhT, tp, [("e_dv", qd)])
                    if ch == 0:
                        def f_rhs(e, r=r):
                            return e.tensor_scalar_mul(rhs_s[0:64, :], tst[:], 1.0)
                        sch.emit("vector", f_rhs, waits=[("d_sc", c_loaded)], incs=[("e_dv", 1)])
                    else:
                        def f_rhs(e, r=r):
                            return e.tensor_add(rhs_s[0:64, :], sg_r[r][:], tst[:])
                        sch.emit("vector", f_rhs,
                                 waits=[("d_sc", sg_loaded), ("e_dv", sch.cur("e_dv")),
                                        ("e_pe", done.get(("cross", ch - 1), 0))],
                                 incs=[("e_dv", 1)])
                    def f_rhs2(e):
                        return e.tensor_copy(rhs_s[64:128, :], rhs_s[0:64, :])
                    sch.emit("vector", f_rhs2, waits=[("e_dv", sch.cur("e_dv"))], incs=[("e_dv", 1)])
                    rhs_done = sch.cur("e_dv")
                    def f_tup(e):
                        return e.tensor_mul(tst[:], tst[:], gc_sb[:])
                    sch.emit("vector", f_tup, waits=[("e_dv", rhs_done)], incs=[("e_dv", 1)])
                    done[("rhs", ch)] = sch.cur("e_dv")

                    for hg in range(2):
                        for hh in range(8):
                            h = hg * 8 + hh
                            pb = h % 2
                            rp = (h % 2) * 64
                            kcb = h // 2
                            def f_at(e, h=h, pb=pb, rp=rp, kcb=kcb, r=r):
                                return e.matmul(p_at[pb][:],
                                                kh_sb[r][rp:rp + 64, kcb * 128:(kcb + 1) * 128],
                                                qhT[rp:rp + 64, kcb * 128:(kcb + 1) * 128],
                                                start=True, stop=True)
                            w = [("e_ac", qhT_done)] if h == 0 else []
                            if h == 0:
                                w.append(("d_sc", kv_loaded))
                            w.append(("e_dv", done.get(("adt", ch, h - 2),
                                              done.get(("adt", ch - 1, h + 14), 0))))
                            sch.emit("tensor", f_at, waits=w, incs=[("e_pe", 1)])
                            at_done = sch.cur("e_pe")
                            def f_adt(e, pb=pb, h=h):
                                return e.tensor_mul(adt[h % 2][:], p_at[pb][:], maskt[:])
                            adt_war = done.get(("omm", ch, h - 2),
                                               done.get(("omm", ch - 1, h + 14), 0))
                            sch.emit("vector", f_adt,
                                     waits=[("e_pe", max(at_done, adt_war))],
                                     incs=[("e_dv", 1)])
                            done[("adt", ch, h)] = sch.cur("e_dv")
                            def f_oi(e, h=h, hh=hh, r=r):
                                return e.matmul(p_o[:, hh * 64:(hh + 1) * 64],
                                                adt[h % 2][:],
                                                v_sb[r][:, h * 64:(h + 1) * 64],
                                                start=True, stop=False)
                            w = [("e_dv", done[("adt", ch, h)])]
                            if hh == 0:
                                w.append(("e_dv", done.get(("ogev_hg", ch, hg - 1),
                                                           done.get(("ogev_hg", ch - 1, 1), 0))))
                            if h == 0:
                                w.append(("e_dv", done[("rhs", ch)]))
                            sch.emit("tensor", f_oi, waits=w)
                            def f_oc(e, hh=hh, h=h, rp=rp, kcb=kcb):
                                return e.matmul(p_o[:, hh * 64:(hh + 1) * 64],
                                                qhT[rp:rp + 64, kcb * 128:(kcb + 1) * 128],
                                                rhs_s[rp:rp + 64, h * 64:(h + 1) * 64],
                                                start=False, stop=True)
                            sch.emit("tensor", f_oc, incs=[("e_pe", 1)])
                            done[("omm", ch, h)] = sch.cur("e_pe")
                        def f_og(e, hg=hg):
                            return e.tensor_mul(og[:, hg * 512:(hg + 1) * 512],
                                                p_o[:], g_sb[:, hg * 512:(hg + 1) * 512])
                        sch.emit("vector", f_og, waits=[("e_pe", done[("omm", ch, hg * 8 + 7)]),
                                                        ("e_ac", done[("gsil", ch)])],
                                 incs=[("e_dv", 1)])
                        done[("ogev_hg", ch, hg)] = sch.cur("e_dv")
                    done[("ogev", ch)] = sch.cur("e_dv")
                    done[("kvuse", ch)] = sch.cur("e_pe")
                    done[("cross", ch)] = sch.cur("e_pe")
                    ogT_done = transposes_batch(og, ogT, tp, [("e_dv", done[("ogev", ch)])])
                    for dc in range(2):
                        for kc in range(8):
                            def f_y(e, dc=dc, kc=kc):
                                return e.matmul(p_qy[:, dc * 512:(dc + 1) * 512],
                                                ogT[:, kc * 128:(kc + 1) * 128],
                                                wo_sb[:, kc * D + dc * 512: kc * D + (dc + 1) * 512],
                                                start=(kc == 0), stop=(kc == 7))
                            w = [("e_ac", ogT_done)] if kc == 0 else []
                            sch.emit("tensor", f_y, waits=w,
                                     incs=([("e_pe", 1)] if kc == 7 else ()))
                        y_done = sch.cur("e_pe")
                        def f_xa(e, dc=dc, r=r):
                            return e.tensor_add(x_ch[r][:, dc * 512:(dc + 1) * 512],
                                                x_ch[r][:, dc * 512:(dc + 1) * 512],
                                                p_qy[:, dc * 512:(dc + 1) * 512])
                        sch.emit("vector", f_xa, waits=[("e_pe", y_done)], incs=[("e_dv", 1)])
                    done[("xadd", ch)] = sch.cur("e_dv")
                    def f_xst(e, ch=ch, r=r):
                        return e.dma_start(out=x_work.rearrange("(c p) d2 -> c p d2", p=128)[ch],
                                           in_=x_ch[r][:])
                    sch.emit("sync", f_xst, waits=[("e_dv", done[("xadd", ch)])], incs=[("d_sy", 16)])
                    done[("xst", ch)] = sch.cur("d_sy")

        init_loads()
        if SIM_FFN_ONLY:
            ffn_sparse(x_in, out_e, "f1")
            fin0 = sch.cur("d_sy")
            sch.emit("sync", lambda e: None, waits=[("d_sy", fin0)])
        else:
            ffn_sparse(x_in, x_work, "f1")
            phase_barrier()
        if not SIM_FFN_ONLY:
            attn_pass1()
            phase_barrier()
            attn_pass2()
            phase_barrier()
            ffn_sparse(x_work, out_e, "f2")
        if DBG_DUMP:
            def f_dbg(e):
                e.dma_start(out=dbg_idx, in_=idx_sc.ap()).then_inc(sems["d_gp"], 16)
                e.dma_start(out=dbg_yab, in_=yab_sc.ap()).then_inc(sems["d_gp"], 16)
            sch.emit("gpsimd", f_dbg,
                     waits=[("d_sy", sch.cur("d_sy")), ("d_gp", sch.cur("d_gp")),
                            ("e_dv", sch.cur("e_dv"))],
                     incs=[("d_gp", 32)], self_inc=True)
        fin = sch.cur("d_sy")
        fing = sch.cur("d_gp")
        sch.emit("sync", lambda e: None, waits=[("d_sy", fin), ("d_gp", fing)])
        sch.finalize()

        block = E(nc.Block())

        @block.tensor
        def _(eng):
            sch.run("tensor", eng, sems)

        @block.vector
        def _(eng):
            sch.run("vector", eng, sems)

        @block.scalar
        def _(eng):
            sch.run("scalar", eng, sems)

        @block.gpsimd
        def _(eng):
            sch.run("gpsimd", eng, sems)

        @block.sync
        def _(eng):
            sch.run("sync", eng, sems)

    return nc


def _host_constants(half):
    gamma = (1.0 - np.exp2(-5.0 - np.arange(H, dtype=np.float64)))
    i = np.arange(C, dtype=np.float64)
    gq = np.empty((128, D), np.float32)
    gk = np.empty((128, D), np.float32)
    gc = np.empty((64, D), np.float32)
    for h in range(H):
        gq[:, h * 64:(h + 1) * 64] = (gamma[h] ** (i + 1 - C) / 8.0)[:, None].astype(np.float32)
        gk[:, h * 64:(h + 1) * 64] = (gamma[h] ** (C - 1 - i))[:, None].astype(np.float32)
        gc[:, h * 64:(h + 1) * 64] = np.float32(gamma[h] ** C)
    maskt = np.triu(np.ones((128, 128), np.float32))
    ident = np.eye(128, dtype=np.float32)
    inv = (1.0 / (10000.0 ** (np.arange(0, DH, 2, dtype=np.float32) / DH))).astype(np.float32)
    pos = (np.arange(TOK, dtype=np.float32) + np.float32(half * TOK))
    freqs = np.einsum('i,j->ij', pos, inv).astype(np.float32)
    cosf = np.cos(freqs).astype(np.float32)
    sinf = np.sin(freqs).astype(np.float32)
    cos_t = np.ascontiguousarray(
        cosf.reshape(NCH, 128, 32).transpose(1, 0, 2).reshape(128, NCH * 32))
    sin_t = np.ascontiguousarray(
        sinf.reshape(NCH, 128, 32).transpose(1, 0, 2).reshape(128, NCH * 32))
    cvec = np.zeros((128, 8), np.float32)
    cvec[:, 0] = EPS
    cvec[:, 1] = -0.5
    rmask = np.full((64, 1), float(half), np.float32)
    return dict(ident_in=ident, maskt_in=maskt, gq_in=gq, gk_in=gk, gc_in=gc,
                cos_in=cos_t, sin_in=sin_t, cvec_in=cvec, rmask_in=rmask)


_NC_CACHE = []


def kernel(L0_attn_x, L0_attn_wq, L0_attn_wk, L0_attn_wv, L0_attn_wg, L0_attn_wo,
           L0_ffn_Wg, L0_ffn_Ws1, L0_ffn_Ws3, L0_ffn_Ws2,
           L0_ffn_We1, L0_ffn_We3, L0_ffn_We2):
    x = np.asarray(L0_attn_x, dtype=np.float32)
    w1cat = np.ascontiguousarray(
        np.concatenate([np.asarray(L0_ffn_Ws1)[None], np.asarray(L0_ffn_We1)], axis=0), np.float32)
    w3cat = np.ascontiguousarray(
        np.concatenate([np.asarray(L0_ffn_Ws3)[None], np.asarray(L0_ffn_We3)], axis=0), np.float32)
    w2cat = np.ascontiguousarray(
        np.concatenate([np.asarray(L0_ffn_Ws2)[None], np.asarray(L0_ffn_We2)], axis=0), np.float32)
    wg_e = np.ascontiguousarray(np.asarray(L0_ffn_Wg), np.float32)
    wqT = np.ascontiguousarray(np.asarray(L0_attn_wq).T, np.float32)
    wkT = np.ascontiguousarray(np.asarray(L0_attn_wk).T, np.float32)
    wvT = np.ascontiguousarray(np.asarray(L0_attn_wv).T, np.float32)
    wgT = np.ascontiguousarray(np.asarray(L0_attn_wg).T, np.float32)
    woT = np.ascontiguousarray(np.asarray(L0_attn_wo).T, np.float32)

    magic = np.full((1, 16), np.random.randint(1, 1 << 30), np.int32)
    ecap = np.tile((np.arange(8, dtype=np.float32) * CAP)[None, :], (128, 1))
    ident32 = np.eye(8, dtype=np.float32)
    iota = np.arange(128, dtype=np.float32).reshape(128, 1)

    if not _NC_CACHE:
        _NC_CACHE.append(build_nc())
    nc = _NC_CACHE[0]

    consts = [_host_constants(h) for h in range(2)]
    in_maps = []
    for c in range(NCORE):
        b, hf = c >> 1, c & 1
        m = dict(x_in=np.ascontiguousarray(x[b, hf * TOK:(hf + 1) * TOK, :]),
                 w1cat=w1cat, w3cat=w3cat, w2cat=w2cat, wg_e=wg_e,
                 wqT=wqT, wkT=wkT, wvT=wvT, wgT=wgT, woT=woT,
                 magic_in=magic, iota_in=iota,
                 ecap_in=ecap, ident32_in=ident32, **consts[hf])
        in_maps.append(m)

    res = run_bass_kernel_spmd(nc, in_maps, core_ids=list(range(NCORE)))
    if DBG_DUMP:
        np.save("/tmp/dbg_idx.npy", np.stack([res.results[c]["dbg_idx"] for c in range(NCORE)]))
        np.save("/tmp/dbg_yab.npy", np.stack([res.results[c]["dbg_yab"].astype(np.float32) for c in range(NCORE)]))

    out = np.empty((4, 8192, D), np.float32)
    for c in range(NCORE):
        b, hf = c >> 1, c & 1
        out[b, hf * TOK:(hf + 1) * TOK, :] = res.results[c]["out_e"]
    return out
